# version 01
# baseline (speedup 1.0000x reference)
import numpy as np
import ml_dtypes
from contextlib import ExitStack
import concourse.bass as bass
import concourse.mybir as mybir
from concourse.bass_utils import run_bass_kernel_spmd

F32 = mybir.dt.float32
BF16 = mybir.dt.bfloat16
AF = mybir.ActivationFunctionType
ALU = mybir.AluOpType
AX = mybir.AxisListType

T = 2048
D = 4096
DC = 32
NH = 64
DSSM = 8192
NCH = 16
NEGBIG = -1e30
BF = ml_dtypes.bfloat16
ENGS = ("sync", "gpsimd", "vector", "scalar", "tensor")


def ap_(t, off, dims):
    if isinstance(t, bass.AP):
        return bass.AP(t.tensor, t.offset + off, [list(x) for x in dims])
    return bass.AP(t, off, [list(x) for x in dims])


class Prog:
    def __init__(self, nc, ctx, block):
        self.nc = nc
        self.ctx = ctx
        self.block = block
        self.sems = {}
        self.cnt = {}
        self.R = {}
        self._scopes = []
        self.right = ExitStack()
        self.mam_res = ExitStack()
        self.lastop = {}
        self.marks = {}
        self.nbar = 0

    def sem(self, name):
        if name in self.sems:
            return self.sems[name]
        h = self.nc.alloc_semaphore(name=name)
        self.sems[name] = h
        self.cnt[name] = 0
        return h

    def phase_end(self, prefixes=()):
        self.barrier()
        names = [n for n in list(self.sems)
                 if any(n.startswith(pre) for pre in prefixes)]
        if names:
            handles = [self.sems[n] for n in names]
            blk = self.block
            nc = self.nc

            @blk.gpsimd
            def _(g):
                nc.clear_and_free_semaphores(handles)
            for n in names:
                del self.sems[n]
                del self.cnt[n]
            self.barrier()

    def inc(self, instr, name, n=1):
        if name not in self.sems:
            self.sem(name)
        instr.then_inc(self.sems[name], n)
        self.cnt[name] += n
        self.marks[id(instr)] = (instr, name, self.cnt[name])
        return self.cnt[name]

    def wait(self, eng, name, val=None):
        eng.wait_ge(self.sems[name], self.cnt[name] if val is None else val)

    OPNAMES = ("tensor_copy", "tensor_tensor", "tensor_add", "tensor_sub",
               "tensor_mul", "tensor_max", "tensor_scalar", "tensor_single_scalar",
               "tensor_scalar_add", "tensor_scalar_sub", "tensor_scalar_mul",
               "tensor_scalar_max", "tensor_scalar_min", "tensor_relu",
               "tensor_reduce", "reduce_max", "reduce_sum", "scalar_tensor_tensor",
               "tensor_tensor_scan", "memset", "memzero", "reciprocal", "select",
               "copy_predicated", "transpose", "bn_stats", "bn_aggr", "max",
               "max_index", "max_with_indices", "pool", "pool_avg", "pool_max",
               "stream_shuffle", "match_replace", "activation", "sign", "square",
               "sqrt", "mul", "add", "iota", "affine_select",
               "partition_broadcast", "engine_nop")

    def wrap(self, eng, tag):
        return _EngWrap(self, eng, tag)

    def dma(self, q, sem, **kw):
        if sem not in self.sems:
            self.sem(sem)
        if self.cnt[sem]:
            q.wait_ge(self.sems[sem], self.cnt[sem])
        return self.inc(q.dma_start(**kw), sem, 16)

    def sbuf(self, name, shape, dtype, scope=None, side=None):
        cm = self.nc.sbuf_tensor(name, shape, dtype, side=side)
        t = (scope or self.ctx).enter_context(cm)
        self.R[name] = t
        return t

    def psum(self, name, shape, dtype=F32, scope=None):
        cm = self.nc.psum_tensor(name, shape, dtype)
        t = (scope or self.ctx).enter_context(cm)
        self.R[name] = t
        return t

    def scope(self):
        s = ExitStack()
        self._scopes.append(s)
        return s

    def barrier(self):
        self.lastop = {}
        self.nbar += 1
        name = "barrier"
        if name not in self.sems:
            self.sem(name)
        tgt = self.nbar * 5
        blk = self.block

        @blk.sync
        def _(s):
            self.inc(s.nop(), name, 1)
            s.wait_ge(self.sems[name], tgt)

        @blk.gpsimd
        def _(g):
            self.inc(g.engine_nop(), name, 1)
            g.wait_ge(self.sems[name], tgt)

        @blk.vector
        def _(v):
            self.inc(v.engine_nop(), name, 1)
            v.wait_ge(self.sems[name], tgt)

        @blk.scalar
        def _(s):
            self.inc(s.nop(), name, 1)
            s.wait_ge(self.sems[name], tgt)

        @blk.tensor
        def _(t):
            self.inc(t.nop(), name, 1)
            t.wait_ge(self.sems[name], tgt)


class _EngWrap:

    def __init__(self, p, eng, tag):
        self._p = p
        self._e = eng
        self._s = "se_" + tag
        if self._s not in p.sems:
            p.sem(self._s)

    def _chain_wait(self):
        p, e, s = self._p, self._e, self._s
        li = p.lastop.get(self._s)
        if li is None:
            return
        mark = p.marks.get(id(li))
        if mark is None:
            li.then_inc(p.sems[s], 1)
            p.cnt[s] += 1
            mark = (li, s, p.cnt[s])
            p.marks[id(li)] = mark
        e.wait_ge(p.sems[mark[1]], mark[2])

    def __getattr__(self, name):
        e = self._e
        a = getattr(e, name)
        if name in Prog.OPNAMES:
            p = self._p

            def f(*args, **kw):
                self._chain_wait()
                instr = a(*args, **kw)
                p.lastop[self._s] = instr
                return instr
            return f
        if name == "dma_start":
            def f(*args, **kw):
                self._chain_wait()
                return a(*args, **kw)
            return f
        return a



def _pack_lhsT(w):
    K, O = w.shape
    nK = (K + 127) // 128
    nO = (O + 127) // 128
    wp = np.zeros((nK * 128, nO * 128), dtype=np.float32)
    wp[:K, :O] = w
    return np.ascontiguousarray(wp.reshape(nK, 128, nO, 128).transpose(2, 0, 1, 3))


def _rope_perm():
    return np.concatenate([np.arange(0, 128, 2), np.arange(1, 128, 2)])


def prep_consts():
    c = {}
    i = np.arange(128)
    c["mk_mask"] = np.where(i[:, None] <= i[None, :], 0.0, NEGBIG).astype(np.float32)
    c["ident"] = np.eye(128, dtype=np.float32)
    tg = np.arange(T).reshape(16, 128)
    j = np.arange(129)
    allow = (j[None, None, :] == 0) | (tg[:, :, None] >= 16 * j[None, None, :])
    c["cmask"] = np.where(allow, 0.0, NEGBIG).astype(np.float32)
    tl = np.arange(128)
    jl = np.arange(128)
    own = (jl[None, :] // 16 == tl[:, None] // 16) & (jl[None, :] % 16 <= tl[:, None] % 16)
    c["fown"] = own.astype(np.float32)
    c["iotajb"] = np.broadcast_to(np.arange(128, dtype=np.float32), (128, 128)).copy()
    tl = np.arange(128)
    j2 = np.arange(256)
    bf = (j2[None, :] >= tl[:, None] + 64) & (j2[None, :] <= tl[:, None] + 128)
    c["sbandF"] = bf.astype(np.float32)
    j1 = np.arange(128)
    bf0 = (j1[None, :] >= tl[:, None] - 64) & (j1[None, :] <= tl[:, None])
    c["sbandF0"] = bf0.astype(np.float32)
    inv = 1.0 / 10000.0 ** (np.arange(0, 128, 2, dtype=np.float64) / 128.0)
    ang = np.arange(T, dtype=np.float64)[:, None] * inv[None, :]
    c["rcos"] = np.ascontiguousarray(np.cos(ang).T.astype(np.float32))
    c["rsin"] = np.ascontiguousarray(np.sin(ang).T.astype(np.float32))
    return c


def prep_mamba(inp):
    w = {}
    w["w_in"] = _pack_lhsT(np.asarray(inp["L0_attn_a_W_in"], np.float32))
    cw = np.asarray(inp["L0_attn_a_conv_w"], np.float32)
    w["conv_w"] = np.ascontiguousarray(cw.reshape(66, 128, 4).transpose(1, 0, 2).reshape(128, 264))
    w["dt_bias"] = np.asarray(inp["L0_attn_a_dt_bias"], np.float32).reshape(64, 1)
    w["a_log"] = np.asarray(inp["L0_attn_a_A_log"], np.float32).reshape(64, 1)
    w["d_param"] = np.asarray(inp["L0_attn_a_D_param"], np.float32).reshape(1, 64)
    w["norm_w"] = np.ascontiguousarray(
        np.asarray(inp["L0_attn_a_norm_weight"], np.float32).reshape(64, 128).T)
    w["w_out"] = _pack_lhsT(np.asarray(inp["L0_attn_a_W_out"], np.float32))
    return w


def prep_ffn(inp):
    w = {}
    w["ffn_w"] = _pack_lhsT(np.asarray(inp["L0_ffn_W"], np.float32).T)
    w["ffn_b"] = np.ascontiguousarray(np.asarray(inp["L0_ffn_b"], np.float32).reshape(32, 128).T)
    A = np.asarray(inp["L0_ffn_A"], np.float32).reshape(128, 4096)
    w["ffn_a"] = _pack_lhsT(A.T)
    w["ffn_bmat"] = np.ascontiguousarray(
        np.asarray(inp["L0_ffn_B"], np.float32).transpose(0, 2, 1).reshape(128, 4096))
    w["ffn_rw"] = np.ascontiguousarray(
        np.asarray(inp["L0_ffn_router_W"], np.float32).T.reshape(32, 128, 8))
    w["ffn_rb"] = np.asarray(inp["L0_ffn_router_b"], np.float32).reshape(8, 1)
    return w


def prep_nsa(inp):
    w = {}
    perm = _rope_perm()
    wqkv = np.asarray(inp["L0_attn_b_w_qkv"], np.float32).copy()
    for h in range(32):
        wqkv[h * 128:(h + 1) * 128] = wqkv[h * 128 + perm]
    for h in range(8):
        base = 4096 + h * 128
        wqkv[base:base + 128] = wqkv[base + perm]
    wqkv[:4096] *= 128.0 ** -0.5
    w["w_qkv"] = _pack_lhsT(wqkv.T)
    w["w_o"] = _pack_lhsT(np.asarray(inp["L0_attn_b_w_o"], np.float32).T)
    kp = np.asarray(inp["L0_attn_b_k_pos"], np.float32)[:, :, perm]
    w["k_pos"] = np.ascontiguousarray(kp.transpose(2, 0, 1).reshape(128, 128))
    vp = np.asarray(inp["L0_attn_b_v_pos"], np.float32)
    w["v_pos"] = np.ascontiguousarray(vp.transpose(2, 0, 1).reshape(128, 128))
    iperm = (np.arange(2048).reshape(16, 128)[:, perm]).reshape(2048)
    w["k_w1"] = _pack_lhsT(np.asarray(inp["L0_attn_b_k_w1"], np.float32)[:, iperm].T)
    w["k_b1"] = np.ascontiguousarray(np.asarray(inp["L0_attn_b_k_b1"], np.float32).reshape(16, 128).T)
    w["k_w2"] = _pack_lhsT(np.asarray(inp["L0_attn_b_k_w2"], np.float32)[perm].T)
    w["k_b2"] = np.asarray(inp["L0_attn_b_k_b2"], np.float32)[perm].reshape(128, 1)
    w["v_w1"] = _pack_lhsT(np.asarray(inp["L0_attn_b_v_w1"], np.float32).T)
    w["v_b1"] = np.ascontiguousarray(np.asarray(inp["L0_attn_b_v_b1"], np.float32).reshape(16, 128).T)
    w["v_w2"] = _pack_lhsT(np.asarray(inp["L0_attn_b_v_w2"], np.float32).T)
    w["v_b2"] = np.asarray(inp["L0_attn_b_v_b2"], np.float32).reshape(128, 1)
    mem = np.asarray(inp["L0_attn_b_mem_kv"], np.float32)
    w["mem_k"] = np.ascontiguousarray(mem[0, :, 0, :][:, perm].T)
    w["mem_v"] = np.ascontiguousarray(mem[1, :, 0, :].T)
    w["w_gate"] = _pack_lhsT(np.asarray(inp["L0_attn_b_w_gate"], np.float32).T)
    w["b_gate"] = np.concatenate(
        [np.asarray(inp["L0_attn_b_b_gate"], np.float32), np.zeros(32, np.float32)]).reshape(128, 1)
    return w



def declare_dram(nc, parts):
    d = {}

    def inp(name, shape):
        d[name] = nc.dram_tensor(name, list(shape), F32, kind="ExternalInput").ap()

    def scr(name, shape, dtype=BF16):
        d[name] = nc.dram_tensor(name, list(shape), dtype, kind="Internal").ap()

    inp("x0", (D, T))
    inp("ident", (128, 128))
    if "mamba" in parts:
        inp("w_in", (131, 32, 128, 128)); inp("conv_w", (128, 264))
        inp("dt_bias", (64, 1)); inp("a_log", (64, 1)); inp("d_param", (1, 64))
        inp("norm_w", (128, 64)); inp("w_out", (32, 64, 128, 128))
        inp("mk_mask", (128, 128))
        scr("z_sp", (4, 64, 128, 512))
        scr("xact_sp", (16, DSSM, 128))
        scr("y_sp", (16, 64, 128, 128))
        scr("cum_sp", (2, 7, 64, 128))
        scr("dbg_sp", (64, T), F32)
    if "ffn" in parts:
        inp("ffn_w", (32, 32, 128, 128)); inp("ffn_b", (128, 32)); inp("ffn_a", (1, 32, 128, 128))
        inp("ffn_bmat", (128, 4096)); inp("ffn_rw", (32, 128, 8)); inp("ffn_rb", (8, 1))
        scr("r8", (1, 8), F32)
    if "nsa" in parts:
        inp("w_qkv", (48, 32, 128, 128)); inp("w_o", (32, 32, 128, 128))
        inp("k_pos", (128, 128)); inp("v_pos", (128, 128))
        inp("k_w1", (16, 16, 128, 128)); inp("k_b1", (128, 16))
        inp("k_w2", (1, 16, 128, 128)); inp("k_b2", (128, 1))
        inp("v_w1", (16, 16, 128, 128)); inp("v_b1", (128, 16))
        inp("v_w2", (1, 16, 128, 128)); inp("v_b2", (128, 1))
        inp("mem_k", (128, 8)); inp("mem_v", (128, 8))
        inp("w_gate", (1, 32, 128, 128)); inp("b_gate", (128, 1))
        inp("cmask", (16, 128, 129)); inp("fown", (128, 128)); inp("iotajb", (128, 128))
        inp("sbandF", (128, 256)); inp("sbandF0", (128, 128))
        inp("rcos", (64, T)); inp("rsin", (64, T))
        scr("q_sp", (32, 128, T))
        scr("k_sp", (8, 128, T))
        scr("v_sp", (8, 128, T))
        scr("gat_sp", (32, 128, T))
    scr("res_sp", (2, DC, 128, T))
    d["out"] = nc.dram_tensor("out", [D, T], F32, kind="ExternalOutput").ap()
    return d



def emit_init(p, d):
    block = p.block
    identf = p.sbuf("identf", [128, 128], F32, scope=p.right, side="right")
    identb = p.sbuf("identb", [128, 128], BF16, scope=p.right, side="right")
    p.sbuf("onesrowf", [1, 128], F32, scope=p.right, side="right")
    p.sbuf("onesrowb", [1, 128], BF16, scope=p.right, side="right")
    p.sbuf("onescolb", [128, 1], BF16, scope=p.right, side="right")
    p.sem("ini_s"); p.sem("ini_g"); p.sem("ini")
    if "norm_w" in d:
        nw = p.sbuf("m3_nw", [128, 64], F32, scope=p.right, side="right")

        @block.sync
        def _(s):
            p.dma(s, "ini_s", out=nw[:], in_=d["norm_w"])

    inis = p.cnt["ini_s"]

    @block.sync
    def _(s):
        p.dma(s, "ini_s", out=identf[:], in_=d["ident"])

    @block.gpsimd
    def _(g):
        g = p.wrap(g, "g")
        g.wait_ge(p.sems["ini_s"], inis + 16)
        p.dma(g, "ini_g", out=identb[:], in_=d["ident"])

    @block.vector
    def _(v):
        v = p.wrap(v, "v")
        v.memset(p.R["onesrowf"][:], 1.0)
        v.memset(p.R["onesrowb"][:], 1.0)
        v.memset(p.R["onescolb"][:], 1.0)
        p.inc(v.engine_nop(), "ini", 1)



def emit_boundary(p, d, idx, src, scope):
    import os
    BSTEP = int(os.environ.get("BSTEP", "9"))
    block = p.block
    nc = p.nc
    xln = p.sbuf(f"xln{idx}", [128, DC * T], BF16, scope=scope)
    bs = p.scope()
    x0a = p.sbuf(f"bn{idx}_x0a", [128, T], F32, scope=bs)
    x0b_ = p.sbuf(f"bn{idx}_x0b", [128, T], F32, scope=bs)
    sq = p.sbuf(f"bn{idx}_sq", [128, T], BF16, scope=bs)
    rows = p.sbuf(f"bn{idx}_rows", [1, 3 * T], F32, scope=bs)
    mub = p.sbuf(f"bn{idx}_mub", [128, T], BF16, scope=bs)
    rstdb = p.sbuf(f"bn{idx}_rstdb", [128, T], BF16, scope=bs)
    pstat = p.psum(f"bn{idx}_pstat", [128, 4096], scope=bs)
    ld = f"bn{idx}_ld"; cast = f"bn{idx}_cast"; mm = f"bn{idx}_mm"
    sqs = f"bn{idx}_sq"; fin = f"bn{idx}_fin"; fins = f"bn{idx}_fins"
    bct = f"bn{idx}_bct"; bcs = f"bn{idx}_bcs"; nrm = f"bn{idx}_nrm"
    for s_ in (ld, cast, mm, sqs, fin, fins, bct, bcs, nrm):
        p.sem(s_)

    if src[0] == "x0":
        @block.sync
        def _(s):
            for c in range(DC):
                if c >= 2:
                    p.wait(s, cast, 16 * (c - 1))
                p.dma(s, ld, out=(x0a if c % 2 == 0 else x0b_)[:],
                                  in_=d["x0"][c * 128:(c + 1) * 128, :])

        @block.gpsimd
        def _(g):
            g = p.wrap(g, "g")
            for c in range(DC):
                p.wait(g, ld, 16 * (c + 1))
                p.dma(g, cast, out=ap_(xln, c * T, [[DC * T, 128], [1, T]]),
                                  in_=(x0a if c % 2 == 0 else x0b_)[:])
    else:
        ri = src[1]

        @block.sync
        def _(s):
            for c in range(DC):
                p.dma(s, cast, out=ap_(xln, c * T, [[DC * T, 128], [1, T]]),
                                  in_=d["res_sp"][ri, c])

    if BSTEP < 2:
        return xln

    @block.scalar
    def _(s):
        s = p.wrap(s, "s")
        for c in range(DC):
            p.wait(s, cast, 16 * (c + 1))
            if c >= 1:
                p.wait(s, mm, 2 * c)
            p.inc(s.activation(sq[:], ap_(xln, c * T, [[DC * T, 128], [1, T]]), AF.Square),
                  sqs, 1)

    @block.tensor
    def _(t):
        t.wait_ge(p.sems["ini"], 1)
        for c in range(DC):
            p.wait(t, cast, 16 * (c + 1))
            for j in range(4):
                mi = t.matmul(ap_(pstat, j * 512, [[4096, 1], [1, 512]]),
                              p.R["onescolb"][:],
                              ap_(xln, c * T + j * 512, [[DC * T, 128], [1, 512]]),
                              start=(c == 0), stop=(c == DC - 1))
            p.inc(mi, mm, 1)
            p.wait(t, sqs, c + 1)
            for j in range(4):
                mi = t.matmul(ap_(pstat, 2048 + j * 512, [[4096, 1], [1, 512]]),
                              p.R["onescolb"][:], sq[:, j * 512:(j + 1) * 512],
                              start=(c == 0), stop=(c == DC - 1))
            p.inc(mi, mm, 1)

    if BSTEP < 3:
        return xln
    r_mu = rows[:, 0:T]
    r_v = rows[:, T:2 * T]
    r_t = rows[:, 2 * T:3 * T]

    @block.vector
    def _(v):
        v = p.wrap(v, "v")
        p.wait(v, mm, 2 * DC)
        v.tensor_copy(r_mu, ap_(pstat, 0, [[4096, 1], [1, T]]))
        v.tensor_copy(r_v, ap_(pstat, 2048, [[4096, 1], [1, T]]))
        v.tensor_scalar_mul(r_mu, r_mu, 1.0 / D)
        v.tensor_scalar_mul(r_v, r_v, 1.0 / D)
        v.tensor_mul(r_t, r_mu, r_mu)
        v.tensor_sub(r_v, r_v, r_t)
        v.tensor_scalar_add(r_v, r_v, 1e-5)
        p.inc(v.engine_nop(), fin, 1)

    @block.scalar
    def _(s):
        s = p.wrap(s, "s")
        p.wait(s, fin, 1)
        p.inc(s.activation(r_v, r_v, AF.Sqrt), fins, 1)

    @block.vector
    def _(v):
        v = p.wrap(v, "v")
        p.wait(v, fins, 1)
        v.reciprocal(r_v, r_v)
        v.tensor_mul(r_t, r_mu, r_v)
        p.inc(v.engine_nop(), fin, 1)

    if BSTEP < 4:
        return xln
    @block.tensor
    def _(t):
        p.wait(t, fin, 2)
        for j in range(4):
            t.matmul(ap_(pstat, j * 512, [[4096, 128], [1, 512]]),
                     p.R["onesrowf"][:], r_v[:, j * 512:(j + 1) * 512], start=True, stop=True)
            mi = t.matmul(ap_(pstat, 2048 + j * 512, [[4096, 128], [1, 512]]),
                          p.R["onesrowf"][:], r_t[:, j * 512:(j + 1) * 512],
                          start=True, stop=True)
        p.inc(mi, bct, 1)

    @block.scalar
    def _(s):
        s = p.wrap(s, "s")
        p.wait(s, bct, 1)
        s.copy(rstdb[:], ap_(pstat, 0, [[4096, 128], [1, T]]))
        p.inc(s.copy(mub[:], ap_(pstat, 2048, [[4096, 128], [1, T]])), bcs, 1)

    if BSTEP < 5:
        return xln
    @block.vector
    def _(v):
        v = p.wrap(v, "v")
        p.wait(v, bcs, 1)
        for c in range(DC):
            xv = ap_(xln, c * T, [[DC * T, 128], [1, T]])
            v.tensor_mul(xv, xv, rstdb[:])
            v.tensor_sub(xv, xv, mub[:])
        p.inc(v.engine_nop(), nrm, 1)

    bs.close()
    return xln



def emit_mamba_m1(p, d, xln):
    block = p.block
    ms = p.scope()
    wbuf = [p.sbuf(f"m1_w{i}", [128, 32 * 128], BF16, scope=ms) for i in range(2)]
    zrow = [p.sbuf(f"m1_z{i}", [128, T], BF16, scope=ms) for i in range(2)]
    xrow = [p.sbuf(f"m1_x{i}", [128, T], BF16, scope=ms) for i in range(2)]
    crow = [p.sbuf(f"m1_c{i}", [128, T], F32, scope=ms) for i in range(2)]
    convw = p.sbuf("m1_convw", [128, 264], F32, scope=ms)
    ps = [p.psum(f"m1_ps{i}", [128, 2048], scope=ms) for i in range(2)]
    Bfm = p.sbuf("Bfm", [128, T], BF16, scope=p.mam_res, side="right")
    Cfm = p.sbuf("Cfm", [128, T], BF16, scope=p.mam_res, side="right")
    dtraw = p.sbuf("dtraw", [64, T], F32, scope=p.mam_res, side="right")
    for s_ in ("m1_wld", "m1_mm", "m1_ev", "m1_evd", "m1_cv", "m1_sl", "m1_st",
               "m1_cwld"):
        p.sem(s_)

    NO = 131
    st_after = []
    tot = 0
    for u in range(128):
        tot += 64 if u < 64 else 16
        st_after.append(tot)

    @block.sync
    def _(s):
        p.dma(s, "m1_cwld", out=convw[:], in_=d["conv_w"])

    @block.gpsimd
    def _(g):
        g = p.wrap(g, "g")
        for o in range(NO):
            if o >= 2:
                g.wait_ge(p.sems["m1_mm"], o - 1)
            p.dma(g, "m1_wld", 
                out=ap_(wbuf[o % 2], 0, [[4096, 128], [128, 32], [1, 128]]),
                in_=d["w_in"][o].rearrange("a p q -> p a q"))

    @block.tensor
    def _(t):
        for o in range(NO):
            t.wait_ge(p.sems["m1_wld"], 16 * (o + 1))
            if o >= 2:
                t.wait_ge(p.sems["m1_ev"], o - 1)
            w = wbuf[o % 2]
            pso = ps[o % 2]
            for dd in range(32):
                lt = w[:, dd * 128:(dd + 1) * 128]
                for j in range(4):
                    mi = t.matmul(ap_(pso, j * 512, [[2048, 128], [1, 512]]),
                                  lt, ap_(xln, dd * T + j * 512, [[DC * T, 128], [1, 512]]),
                                  start=(dd == 0), stop=(dd == 31))
            p.inc(mi, "m1_mm", 1)

    @block.scalar
    def _(s):
        s = p.wrap(s, "s")
        for o in range(NO):
            if o < 130:
                s.wait_ge(p.sems["m1_mm"], o + 1)
                pso = ps[o % 2]
                if o < 64:
                    u = o
                    if u >= 2:
                        s.wait_ge(p.sems["m1_st"], st_after[u - 2])
                    p.inc(s.copy(zrow[u % 2][:], pso[:]), "m1_ev", 1)
                else:
                    nx = o - 64
                    if nx >= 2:
                        s.wait_ge(p.sems["m1_cv"], nx - 1)
                    p.inc(s.copy(xrow[nx % 2][:], pso[:]), "m1_ev", 1)
            cc = o - 65
            if 0 <= cc < 66:
                s.wait_ge(p.sems["m1_cv"], cc + 1)
                if cc < 64:
                    u = 64 + cc
                    if u >= 2:
                        s.wait_ge(p.sems["m1_st"], st_after[u - 2])
                    dst = zrow[u % 2]
                elif cc == 64:
                    dst = Bfm
                else:
                    dst = Cfm
                p.inc(s.activation(dst[:], crow[cc % 2][:], AF.Silu), "m1_sl", 1)

    @block.vector
    def _(v):
        v = p.wrap(v, "v")
        v.wait_ge(p.sems["m1_cwld"], 16)
        for o in range(64, NO):
            if o < 130:
                nx = o - 64
                cc = nx
                v.wait_ge(p.sems["m1_ev"], o + 1)
                if nx >= 2:
                    v.wait_ge(p.sems["m1_sl"], nx - 1)
                xr = xrow[nx % 2]
                cr = crow[nx % 2]
                v.tensor_scalar(cr[:], xr[:], convw[:, cc * 4 + 3:cc * 4 + 4], None, ALU.mult)
                for k_ in (2, 1, 0):
                    sh = 3 - k_
                    v.scalar_tensor_tensor(cr[:, sh:T], xr[:, 0:T - sh],
                                           convw[:, cc * 4 + k_:cc * 4 + k_ + 1],
                                           cr[:, sh:T], ALU.mult, ALU.add)
                p.inc(v.engine_nop(), "m1_cv", 1)
            else:
                v.wait_ge(p.sems["m1_mm"], NO)
                v.tensor_copy(dtraw[:], ap_(ps[o % 2], 0, [[2048, 64], [1, T]]))
                p.inc(v.engine_nop(), "m1_evd", 1)

    @block.sync
    def _(s):
        for o in range(NO):
            if o < 64:
                u = o
                s.wait_ge(p.sems["m1_ev"], o + 1)
                for j in range(4):
                    p.dma(s, "m1_st", out=d["z_sp"][j, o],
                                      in_=zrow[u % 2][:, j * 512:(j + 1) * 512])
            elif o < 128:
                cc = o - 64
                u = 64 + cc
                s.wait_ge(p.sems["m1_sl"], cc + 1)
                p.dma(s, "m1_st", 
                    out=ap_(d["xact_sp"], cc * 128 * 128,
                            [[128, 128], [DSSM * 128, 16], [1, 128]]),
                    in_=ap_(zrow[u % 2], 0, [[T, 128], [128, 16], [1, 128]]))
    ms.close()
    return Bfm, Cfm, dtraw



def emit_mamba_m3pre(p, d):
    block = p.block
    ps_ = p.scope()
    dtsp = p.sbuf("m3_dtsp", [64, T], F32, scope=ps_)
    tmpa = p.sbuf("m3_tmpa", [64, T], F32, scope=ps_)
    tmpb = p.sbuf("m3_tmpb", [64, T], F32, scope=ps_)
    pD1 = p.psum("m3pre_p", [128, 64], scope=ps_)
    dtec = p.sbuf("m3_dtec", [128, T], BF16, scope=p.mam_res, side="right")
    dtspb = dtec[0:64, :]
    ec_hm = dtec[64:128, :]
    p.R["m3_dtspb"] = dtspb
    p.R["m3_echm"] = ec_hm
    cumpair = p.sbuf("m3_cumpair", [128, T], F32, scope=p.mam_res, side="right")
    cum = cumpair[0:64, :]
    negcum = cumpair[64:128, :]
    p.R["m3_cum"] = cum
    p.R["m3_negcum"] = negcum
    cumL = p.sbuf("m3_cumL", [64, 16], F32, scope=p.mam_res, side="right")
    hpk = [p.sbuf(f"m3_hml{i}", [128, T], BF16, scope=p.mam_res, side="right")
           for i in range(3)]
    for i, n in enumerate(("phi", "pmid", "plo", "nhi", "nmid", "nlo")):
        p.R["m3_" + n] = hpk[i % 3][0 if i < 3 else 64:64 if i < 3 else 128, :]
    dbias = p.sbuf("m3_dbias", [64, 1], F32, scope=p.mam_res, side="right")
    alog = p.sbuf("m3_alog", [64, 1], F32, scope=p.mam_res, side="right")
    dprow = p.sbuf("m3_dprow", [1, 64], F32, scope=p.mam_res, side="right")
    D_bc = p.sbuf("m3_Dbc", [128, 64], F32, scope=p.mam_res, side="right")
    mkmask = p.sbuf("mkmask", [128, 128], F32, scope=p.mam_res, side="right")
    for s_ in ("m3p_ld", "m3p_as", "m3p_av", "m3p_bs", "m3p_bv", "m3p_cv",
               "m3p_cs", "m3p_ct"):
        p.sem(s_)
    hml = {n: p.R[f"m3_{n}"] for n in ("phi", "pmid", "plo", "nhi", "nmid", "nlo")}
    del dtec, cumpair

    @block.sync
    def _(s):
        p.dma(s, "m3p_ld", out=dbias[:], in_=d["dt_bias"])
        p.dma(s, "m3p_ld", out=alog[:], in_=d["a_log"])
        p.dma(s, "m3p_ld", out=dprow[:], in_=d["d_param"])
        p.dma(s, "m3p_ld", out=mkmask[:], in_=d["mk_mask"])

    @block.scalar
    def _(s):
        s = p.wrap(s, "s")
        s.wait_ge(p.sems["m3p_ld"], 64)
        s.wait_ge(p.sems["m1_ev"], 130)
        s.wait_ge(p.sems["m1_evd"], 1)
        s.add(dtsp[:], p.R["dtraw"][:], dbias[:])
        p.inc(s.activation(tmpa[:], dtsp[:], AF.Abs), "m3p_as", 1)

    @block.vector
    def _(v):
        v = p.wrap(v, "v")
        v.wait_ge(p.sems["m3p_as"], 1)
        v.tensor_scalar_mul(tmpa[:], tmpa[:], -1.0)
        p.inc(v.engine_nop(), "m3p_av", 1)

    @block.scalar
    def _(s):
        s = p.wrap(s, "s")
        s.wait_ge(p.sems["m3p_av"], 1)
        s.activation(tmpa[:], tmpa[:], AF.Exp)
        s.activation(tmpa[:], tmpa[:], AF.Ln, bias=1.0)
        p.inc(s.activation(alog[:], alog[:], AF.Exp), "m3p_as", 1)

    @block.vector
    def _(v):
        v = p.wrap(v, "v")
        v.wait_ge(p.sems["m3p_as"], 2)
        v.tensor_scalar_max(tmpb[:], dtsp[:], 0.0)
        v.tensor_add(dtsp[:], tmpa[:], tmpb[:])
        v.tensor_copy(dtspb[:], dtsp[:])
        v.tensor_scalar_mul(alog[:], alog[:], -1.0)
        v.tensor_scalar(tmpa[:], dtsp[:], alog[:], None, ALU.mult)
        for c in range(NCH):
            v.tensor_tensor_scan(cum[:, c * 128:(c + 1) * 128],
                                 tmpa[:, c * 128:(c + 1) * 128],
                                 tmpa[:, c * 128:(c + 1) * 128],
                                 0.0, ALU.add, ALU.bypass)
        v.tensor_copy(cum[:], cum[:])
        v.tensor_copy(cumL[:], ap_(cum, 127, [[T, 64], [128, 16]]))
        v.tensor_copy(hml["phi"][:], cum[:])
        v.tensor_copy(tmpa[:], hml["phi"][:])
        v.tensor_sub(tmpb[:], cum[:], tmpa[:])
        v.tensor_copy(hml["pmid"][:], tmpb[:])
        v.tensor_copy(tmpa[:], hml["pmid"][:])
        v.tensor_sub(tmpb[:], tmpb[:], tmpa[:])
        v.tensor_copy(hml["plo"][:], tmpb[:])
        for n in ("hi", "mid", "lo"):
            v.tensor_scalar_mul(hml["n" + n][:], hml["p" + n][:], -1.0)
        v.tensor_scalar_mul(negcum[:], cum[:], -1.0)
        p.inc(v.engine_nop(), "m3p_cv", 1)

    @block.scalar
    def _(s):
        s = p.wrap(s, "s")
        s.wait_ge(p.sems["m3p_cv"], 1)
        p.inc(s.activation(ec_hm[:], cum[:], AF.Exp), "m3p_cs", 1)

    @block.tensor
    def _(t):
        t.wait_ge(p.sems["m3p_cv"], 1)
        t.wait_ge(p.sems["ini"], 1)
        t.wait_ge(p.sems["ini_g"], 16)
        t.wait_ge(p.sems["ini_s"], p.cnt["ini_s"])
        t.wait_ge(p.sems["m3p_ld"], 64)
        p.inc(t.matmul(pD1[:], p.R["onesrowf"][:], dprow[:], start=True, stop=True),
              "m3p_ct", 1)

    @block.vector
    def _(v):
        v = p.wrap(v, "v")
        v.wait_ge(p.sems["m3p_ct"], 1)
        v.tensor_copy(D_bc[:], pD1[:])
        p.inc(v.engine_nop(), "m3p_cv", 1)
    ps_.close()



def emit_mamba_m3(p, d):
    block = p.block
    ms = p.scope()
    GW = 1024
    xtp = [p.sbuf(f"m3_xtp{i}", [128, GW], BF16, scope=ms) for i in range(2)]
    xfm = [p.sbuf(f"m3_xfm{i}", [128, GW], BF16, scope=ms) for i in range(2)]
    dtx = [p.sbuf(f"m3_dtx{i}", [128, GW], BF16, scope=ms) for i in range(2)]
    dtxs = [p.sbuf(f"m3_dtxs{i}", [128, GW], BF16, scope=ms) for i in range(2)]
    Cs = [p.sbuf(f"m3_Cs{i}", [128, GW], BF16, scope=ms) for i in range(2)]
    Sg = [p.sbuf(f"m3_S{i}", [128, GW], BF16, scope=ms) for i in range(2)]
    sfp = [p.sbuf(f"m3_sfp{i}", [128, GW], F32, scope=ms) for i in range(2)]
    xd = [p.sbuf(f"m3_xd{i}", [128, GW], BF16, scope=ms) for i in range(2)]
    ybuf = [p.sbuf(f"m3_yb{i}", [128, GW], BF16, scope=ms) for i in range(2)]
    stgA = p.sbuf("m3_stgA", [6, 8192], BF16, scope=ms)
    stgB = p.sbuf("m3_stgB", [6, 8192], BF16, scope=ms)
    ecrowT = p.sbuf("m3_ecrow", [1, 8192], BF16, scope=ms)
    dtT = [p.sbuf(f"m3_dtT{i}", [128, 64], BF16, scope=ms) for i in range(2)]
    edT = [p.sbuf(f"m3_edT{i}", [128, 64], BF16, scope=ms) for i in range(2)]
    Btp = [p.sbuf(f"m3_Btp{i}", [128, 128], BF16, scope=ms) for i in range(2)]
    Gf = [p.sbuf(f"m3_Gf{i}", [128, 128], F32, scope=ms) for i in range(2)]
    ed_hm = [p.sbuf(f"m3_edhm{i}", [64, 128], BF16, scope=ms) for i in range(2)]
    decayrow = [p.sbuf(f"m3_dr{i}", [1, 64], F32, scope=ms) for i in range(2)]
    decay_bc = [p.sbuf(f"m3_dbc{i}", [128, 64], F32, scope=ms) for i in range(2)]
    expL = [p.sbuf(f"m3_eL{i}", [64, 1], F32, scope=ms) for i in range(2)]
    state = p.sbuf("m3_state", [128, DSSM], F32, scope=ms)
    statebf = p.sbuf("m3_statebf", [128, DSSM], BF16, scope=ms)
    pBig = p.psum("m3_pBig", [128, 2048], scope=ms)
    pD = p.psum("m3_pD", [128, 1024], scope=ms)
    pG = p.psum("m3_pG", [128, 512], scope=ms)
    ptr = p.psum("m3_ptr", [128, 1024], BF16, scope=ms)
    for s_ in ("s3_sb", "s3_pre", "s3_ex", "s3_ld", "s3_yst", "s3_pg", "s3_pv",
               "s3_tg", "s3_tv", "s3_cs", "s3_csv", "s3_gam", "s3_sp", "s3_sS",
               "s3_ymm", "s3_dmm", "s3_ya", "s3_dx", "s3_xd", "s3_dts",
               "s3_smg", "s3_su", "s3_iv", "s3_ig"):
        p.sem(s_)

    cum = p.R["m3_cum"]; cumL = p.R["m3_cumL"]
    hml = {n: p.R[f"m3_{n}"] for n in ("phi", "pmid", "plo", "nhi", "nmid", "nlo")}
    ec_hm = p.R["m3_echm"]
    Bfm = p.R["Bfm"]; Cfm = p.R["Cfm"]
    mkm = p.R["mkmask"]
    D_bc = p.R["m3_Dbc"]
    dtspb = p.R["m3_dtspb"]

    @block.vector
    def _(v):
        v = p.wrap(v, "v")
        v.memset(state[:], 0.0)
        v.memset(statebf[:], 0.0)
        p.inc(v.engine_nop(), "s3_iv", 1)

    @block.gpsimd
    def _(g):
        g = p.wrap(g, "g")
        g.wait_ge(p.sems["m3p_cv"], 2)
        g.wait_ge(p.sems["m3p_cs"], 1)
        g.memset(stgA[0:6, :], 1.0)
        g.memset(stgB[0:6, :], 1.0)
        p.inc(g.engine_nop(), "s3_ig", 1)

    @block.scalar
    def _(s):
        s = p.wrap(s, "s")
        s.wait_ge(p.sems["s3_iv"], 1)
        s.wait_ge(p.sems["s3_ig"], 1)
        names = ["nhi", "nmid", "nlo", "phi", "pmid", "plo"]
        for c in range(NCH):
            sl = c % 2
            if c >= 1:
                s.wait_ge(p.sems["s3_gam"], 8 * c)
                s.wait_ge(p.sems["s3_cs"], 8 * c)
            if c >= 2:
                s.wait_ge(p.sems["s3_sb"], 224 * (c - 1))
            for i, n in enumerate(names):
                p.dma(s, "s3_sb", out=d["cum_sp"][sl, i],
                                  in_=hml[n][:, c * 128:(c + 1) * 128])
            p.dma(s, "s3_sb", out=d["cum_sp"][sl, 6],
                              in_=ap_(ec_hm, c * 128, [[T, 64], [1, 128]]))
            s.wait_ge(p.sems["s3_sb"], 224 * c + 112)
            for i in range(3):
                p.dma(s, "s3_sb", out=stgA[i:i + 1, :],
                                  in_=ap_(d["cum_sp"], (sl * 7 + i) * 8192,
                                          [[8192, 1], [1, 8192]]))
                p.dma(s, "s3_sb", out=stgB[3 + i:4 + i, :],
                                  in_=ap_(d["cum_sp"], (sl * 7 + 3 + i) * 8192,
                                          [[8192, 1], [1, 8192]]))
            p.dma(s, "s3_sb", out=ecrowT[:],
                              in_=ap_(d["cum_sp"], (sl * 7 + 6) * 8192,
                                      [[8192, 1], [1, 8192]]))
            if c >= 2:
                s.wait_ge(p.sems["s3_tg"], c - 1)
            s.activation(ed_hm[sl][:], p.R["m3_negcum"][:, c * 128:(c + 1) * 128], AF.Exp,
                         bias=cumL[:, c:c + 1])
            p.inc(s.activation(expL[sl][:], cumL[:, c:c + 1], AF.Exp), "s3_pre", 1)
            for g_ in range(8):
                n = 8 * c + g_
                s.wait_ge(p.sems["s3_sp"], n + 1)
                p.inc(s.activation(sfp[n % 2][:], sfp[n % 2][:], AF.Exp), "s3_ex", 1)

    @block.sync
    def _(s):
        s.wait_ge(p.sems["s3_iv"], 1)
        s.wait_ge(p.sems["s3_ig"], 1)
        s.wait_ge(p.sems["m1_st"], 64 * 64 + 16 * 64)
        for c in range(NCH):
            for g_ in range(8):
                n = 8 * c + g_
                if n >= 2:
                    s.wait_ge(p.sems["s3_dx"], n - 1)
                    s.wait_ge(p.sems["s3_xd"], n - 1)
                p.dma(s, "s3_ld", out=xtp[n % 2][:],
                                  in_=ap_(d["xact_sp"], c * DSSM * 128 + g_ * GW * 128,
                                          [[128, GW], [1, 128]]),
                                  transpose=True)
                p.dma(s, "s3_ld", out=xfm[n % 2][:],
                                  in_=ap_(d["xact_sp"], c * DSSM * 128 + g_ * GW * 128,
                                          [[128, 128], [16384, 8], [1, 128]]))
                if n >= 1:
                    m = n - 1
                    s.wait_ge(p.sems["s3_ya"], m + 1)
                    p.dma(s, "s3_yst", 
                        out=ap_(d["y_sp"], (m // 8) * 64 * 16384 + (m % 8) * 8 * 16384,
                                [[128, 128], [16384, 8], [1, 128]]),
                        in_=ap_(ybuf[m % 2], 0, [[GW, 128], [128, 8], [1, 128]]))
        m = NCH * 8 - 1
        s.wait_ge(p.sems["s3_ya"], m + 1)
        p.dma(s, "s3_yst", 
            out=ap_(d["y_sp"], (m // 8) * 64 * 16384 + (m % 8) * 8 * 16384,
                    [[128, 128], [16384, 8], [1, 128]]),
            in_=ap_(ybuf[m % 2], 0, [[GW, 128], [128, 8], [1, 128]]))
        s.wait_ge(p.sems["s3_yst"], 16 * 128)

    @block.tensor
    def _(t):
        t.wait_ge(p.sems["s3_iv"], 1)
        t.wait_ge(p.sems["s3_ig"], 1)
        t.wait_ge(p.sems["m3p_cv"], 2)
        t.wait_ge(p.sems["m3p_cs"], 1)
        t.wait_ge(p.sems["m1_sl"], 66)
        for c in range(NCH):
            sl = c % 2
            if c >= 1:
                t.wait_ge(p.sems["s3_pv"], 3 * c)
            mi = t.matmul(ap_(pG, 0, [[512, 128], [1, 128]]),
                          Bfm[:, c * 128:(c + 1) * 128], Cfm[:, c * 128:(c + 1) * 128],
                          start=True, stop=True)
            p.inc(mi, "s3_pg", 1)
            if c >= 1:
                t.wait_ge(p.sems["s3_tv"], c)
            t.wait_ge(p.sems["s3_pre"], c + 1)
            t.transpose(ap_(ptr, 0, [[1024, 128], [1, 64]]),
                        ap_(dtspb, c * 128, [[T, 64], [1, 128]]), p.R["identb"][0:64, 0:64])
            t.transpose(ap_(ptr, 64, [[1024, 128], [1, 64]]), ed_hm[sl][:],
                        p.R["identb"][0:64, 0:64])
            mi = t.transpose(ap_(ptr, 128, [[1024, 128], [1, 128]]),
                             ap_(Bfm, c * 128, [[T, 128], [1, 128]]), p.R["identb"][:])
            p.inc(mi, "s3_tg", 1)
            t.wait_ge(p.sems["s3_pv"], 3 * c + 1)
            mi = t.transpose(ap_(pG, 128, [[512, 1], [1, 64]]), expL[sl][:],
                             p.R["identf"][0:64, 0:64])
            p.inc(mi, "s3_pg", 1)
            t.wait_ge(p.sems["s3_pv"], 3 * c + 2)
            mi = t.matmul(ap_(pG, 192, [[512, 128], [1, 64]]), p.R["onesrowf"][:],
                          decayrow[sl][:], start=True, stop=True)
            p.inc(mi, "s3_pg", 1)
            for g_ in range(8):
                n = 8 * c + g_
                gsl = n % 2
                t.wait_ge(p.sems["s3_sb"], 224 * (c + 1))
                if n >= 1:
                    t.wait_ge(p.sems["s3_sp"], n)
                for q in range(2):
                    mi = t.matmul(ap_(pBig, q * 512, [[2048, 128], [1, 512]]),
                                  p.R["onesrowb"][:],
                                  ecrowT[:, g_ * GW + q * 512: g_ * GW + (q + 1) * 512],
                                  start=True, stop=True)
                p.inc(mi, "s3_cs", 1)
                t.wait_ge(p.sems["s3_csv"], n + 1)
                for hh in range(8):
                    h = g_ * 8 + hh
                    mi = t.matmul(ap_(pBig, hh * 128, [[2048, 128], [1, 128]]),
                                  stgA[0:6, h * 128:(h + 1) * 128],
                                  stgB[0:6, h * 128:(h + 1) * 128],
                                  start=True, stop=True)
                p.inc(mi, "s3_gam", 1)
                t.wait_ge(p.sems["s3_sS"], n + 1)
                t.wait_ge(p.sems["s3_dx"], n + 1)
                if n >= 1:
                    t.wait_ge(p.sems["s3_ya"], n)
                if c >= 1:
                    t.wait_ge(p.sems["s3_su"], n - 7)
                for hh in range(8):
                    t.matmul(ap_(pBig, 1024 + hh * 128, [[2048, 128], [1, 128]]),
                             dtx[gsl][:, hh * 128:(hh + 1) * 128],
                             Sg[gsl][:, hh * 128:(hh + 1) * 128],
                             start=True, stop=False)
                    mi = t.matmul(ap_(pBig, 1024 + hh * 128, [[2048, 128], [1, 128]]),
                                  statebf[:, (g_ * 8 + hh) * 128:(g_ * 8 + hh + 1) * 128],
                                  Cs[gsl][:, hh * 128:(hh + 1) * 128],
                                  start=False, stop=True)
                p.inc(mi, "s3_ymm", 1)
                t.wait_ge(p.sems["s3_tv"], c + 1)
                t.wait_ge(p.sems["s3_dts"], n + 1)
                if n >= 1:
                    t.wait_ge(p.sems["s3_su"], n)
                for hh in range(8):
                    mi = t.matmul(ap_(pD, hh * 128, [[1024, 128], [1, 128]]),
                                  Btp[sl][:], dtxs[gsl][:, hh * 128:(hh + 1) * 128],
                                  start=True, stop=True)
                p.inc(mi, "s3_dmm", 1)

    @block.vector
    def _(v):
        v = p.wrap(v, "v")
        for c in range(NCH):
            sl = c % 2
            v.wait_ge(p.sems["s3_pg"], 3 * c + 1)
            if c >= 2:
                v.wait_ge(p.sems["s3_sS"], 8 * (c - 1))
            v.tensor_copy(Gf[sl][:], ap_(pG, 0, [[512, 128], [1, 128]]))
            p.inc(v.engine_nop(), "s3_pv", 1)
            v.wait_ge(p.sems["s3_pg"], 3 * c + 2)
            v.tensor_copy(decayrow[sl][:], ap_(pG, 128, [[512, 1], [1, 64]]))
            p.inc(v.engine_nop(), "s3_pv", 1)
            v.wait_ge(p.sems["s3_pg"], 3 * c + 3)
            if c >= 2:
                v.wait_ge(p.sems["s3_smg"], 8 * (c - 1))
            v.tensor_copy(decay_bc[sl][:], ap_(pG, 192, [[512, 128], [1, 64]]))
            p.inc(v.engine_nop(), "s3_pv", 1)
            v.wait_ge(p.sems["s3_tg"], c + 1)
            v.tensor_copy(dtT[sl][:], ap_(ptr, 0, [[1024, 128], [1, 64]]))
            v.tensor_copy(edT[sl][:], ap_(ptr, 64, [[1024, 128], [1, 64]]))
            v.tensor_copy(Btp[sl][:], ap_(ptr, 128, [[1024, 128], [1, 128]]))
            p.inc(v.engine_nop(), "s3_tv", 1)
            for g_ in range(8):
                n = 8 * c + g_
                gsl = n % 2
                v.wait_ge(p.sems["s3_ld"], 32 * (n + 1))
                if n >= 2:
                    v.wait_ge(p.sems["s3_ymm"], n - 1)
                v.tensor_mul(dtx[gsl][:], xtp[gsl][:],
                             ap_(dtT[sl], g_ * 8, [[64, 128], [1, 8], [0, 128]]))
                p.inc(v.engine_nop(), "s3_dx", 1)
                v.wait_ge(p.sems["s3_cs"], n + 1)
                if n >= 2:
                    v.wait_ge(p.sems["s3_ymm"], n - 1)
                v.tensor_mul(Cs[gsl][:],
                             ap_(Cfm, c * 128, [[T, 128], [0, 8], [1, 128]]),
                             pBig[:, 0:1024])
                p.inc(v.engine_nop(), "s3_csv", 1)
                v.wait_ge(p.sems["s3_gam"], n + 1)
                if n >= 2:
                    v.wait_ge(p.sems["s3_sS"], n - 1)
                v.tensor_add(sfp[gsl][:], pBig[:, 0:1024],
                             ap_(mkm, 0, [[128, 128], [0, 8], [1, 128]]))
                p.inc(v.engine_nop(), "s3_sp", 1)
                if n >= 1:
                    m = n - 1
                    msl = m % 2
                    mc, mg = divmod(m, 8)
                    v.wait_ge(p.sems["s3_ymm"], m + 1)
                    v.wait_ge(p.sems["s3_xd"], m + 1)
                    if m >= 2:
                        v.wait_ge(p.sems["s3_yst"], 16 * (m - 1))
                    v.tensor_add(ybuf[msl][:], pBig[:, 1024:2048], xd[msl][:])
                    p.inc(v.engine_nop(), "s3_ya", 1)
                    v.wait_ge(p.sems["s3_dmm"], m + 1)
                    v.wait_ge(p.sems["s3_smg"], m + 1)
                    v.tensor_add(state[:, mg * GW:(mg + 1) * GW],
                                 state[:, mg * GW:(mg + 1) * GW], pD[:])
                    v.tensor_copy(statebf[:, mg * GW:(mg + 1) * GW],
                                  state[:, mg * GW:(mg + 1) * GW])
                    p.inc(v.engine_nop(), "s3_su", 1)
        m = NCH * 8 - 1
        msl = m % 2
        mg = m % 8
        v.wait_ge(p.sems["s3_ymm"], m + 1)
        v.wait_ge(p.sems["s3_xd"], m + 1)
        v.wait_ge(p.sems["s3_yst"], 16 * (m - 1))
        v.tensor_add(ybuf[msl][:], pBig[:, 1024:2048], xd[msl][:])
        p.inc(v.engine_nop(), "s3_ya", 1)
        v.wait_ge(p.sems["s3_dmm"], m + 1)
        v.wait_ge(p.sems["s3_smg"], m + 1)
        v.tensor_add(state[:, mg * GW:(mg + 1) * GW], state[:, mg * GW:(mg + 1) * GW], pD[:])
        v.tensor_copy(statebf[:, mg * GW:(mg + 1) * GW], state[:, mg * GW:(mg + 1) * GW])
        p.inc(v.engine_nop(), "s3_su", 1)

    @block.gpsimd
    def _(g):
        g = p.wrap(g, "g")
        for c in range(NCH):
            sl = c % 2
            for g_ in range(8):
                n = 8 * c + g_
                gsl = n % 2
                g.wait_ge(p.sems["s3_ld"], 32 * (n + 1))
                if n >= 2:
                    g.wait_ge(p.sems["s3_ya"], n - 1)
                g.tensor_mul(xd[gsl][:], xfm[gsl][:],
                             ap_(D_bc, g_ * 8, [[64, 128], [1, 8], [0, 128]]))
                p.inc(g.engine_nop(), "s3_xd", 1)
                g.wait_ge(p.sems["s3_dx"], n + 1)
                g.wait_ge(p.sems["s3_tv"], c + 1)
                if n >= 2:
                    g.wait_ge(p.sems["s3_dmm"], n - 1)
                g.tensor_mul(dtxs[gsl][:], dtx[gsl][:],
                             ap_(edT[sl], g_ * 8, [[64, 128], [1, 8], [0, 128]]))
                p.inc(g.engine_nop(), "s3_dts", 1)
                g.wait_ge(p.sems["s3_ex"], n + 1)
                g.wait_ge(p.sems["s3_pv"], 3 * c + 1)
                if n >= 2:
                    g.wait_ge(p.sems["s3_ymm"], n - 1)
                g.tensor_mul(Sg[gsl][:], sfp[gsl][:],
                             ap_(Gf[sl], 0, [[128, 128], [0, 8], [1, 128]]))
                p.inc(g.engine_nop(), "s3_sS", 1)
                g.wait_ge(p.sems["s3_pv"], 3 * c + 3)
                if c >= 1:
                    g.wait_ge(p.sems["s3_su"], n - 7)
                g.tensor_mul(state[:, g_ * GW:(g_ + 1) * GW],
                             state[:, g_ * GW:(g_ + 1) * GW],
                             ap_(decay_bc[sl], g_ * 8, [[64, 128], [1, 8], [0, 128]]))
                p.inc(g.engine_nop(), "s3_smg", 1)
    ms.close()



def emit_mamba_m4(p, d, res_old, res_new_idx):
    block = p.block
    ms = p.scope()
    yz = p.sbuf("m4_yz", [128, 64 * 1024], BF16, scope=ms)
    wob = [p.sbuf(f"m4_w{i}", [128, 64 * 128], BF16, scope=ms) for i in range(2)]
    zst = [p.sbuf(f"m4_z{i}", [128, 1024], BF16, scope=ms) for i in range(2)]
    zsil = [p.sbuf(f"m4_zs{i}", [128, 1024], BF16, scope=ms) for i in range(2)]
    sqb = [p.sbuf(f"m4_sq{i}", [128, 1024], BF16, scope=ms) for i in range(2)]
    rrow = p.sbuf("m4_rrow", [1, 1024], F32, scope=ms)
    rstdb = p.sbuf("m4_rstdb", [128, 1024], BF16, scope=ms)
    rold = [p.sbuf(f"m4_ro{i}", [128, 1024], F32, scope=ms) for i in range(2)]
    rnew = [p.sbuf(f"m4_rn{i}", [128, 1024], BF16, scope=ms) for i in range(2)]
    psE = [p.psum(f"m4_ps{i}", [128, 1024], scope=ms) for i in range(2)]
    pst = p.psum("m4_pst", [128, 1024], scope=ms)
    for s_ in ("m4_yld", "m4_zld", "m4_sil", "m4_yzs", "m4_sq", "m4_smm",
               "m4_rowv", "m4_rowss", "m4_bct", "m4_bcs", "m4_yn", "m4_wld",
               "m4_mm", "m4_rold", "m4_ev", "m4_st"):
        p.sem(s_)
    nw = p.R["m3_nw"]

    for th in range(2):
        B = {k: p.cnt[k] for k in ("m4_yld", "m4_zld", "m4_sil", "m4_yzs", "m4_sq",
                                   "m4_smm", "m4_rowv", "m4_rowss", "m4_bct", "m4_bcs",
                                   "m4_yn", "m4_wld", "m4_mm", "m4_rold", "m4_ev",
                                   "m4_st")}

        @block.sync
        def _(s):
            if th == 1:
                s.wait_ge(p.sems["m4_yn"], B["m4_yn"])
                s.wait_ge(p.sems["m4_mm"], B["m4_mm"])
            else:
                s.wait_ge(p.sems["s3_yst"], 16 * 128)
            for cc in range(8):
                c = th * 8 + cc
                p.dma(s, "m4_yld", 
                    out=ap_(yz, cc * 128, [[65536, 128], [1024, 64], [1, 128]]),
                    in_=ap_(d["y_sp"], c * 64 * 16384, [[128, 128], [16384, 64], [1, 128]]))
            for h in range(64):
                if h >= 2:
                    s.wait_ge(p.sems["m4_sil"], B["m4_sil"] + h - 1)
                for jj in range(2):
                    p.dma(s, "m4_zld", out=zst[h % 2][:, jj * 512:(jj + 1) * 512],
                                      in_=d["z_sp"][th * 2 + jj, h])

        @block.scalar
        def _(s):
            s = p.wrap(s, "s")
            if B["m4_yzs"]:
                s.wait_ge(p.sems["m4_yzs"], B["m4_yzs"])
            for h in range(64):
                s.wait_ge(p.sems["m4_zld"], B["m4_zld"] + 32 * (h + 1))
                if h >= 2:
                    s.wait_ge(p.sems["m4_yzs"], B["m4_yzs"] + h - 1)
                p.inc(s.activation(zsil[h % 2][:], zst[h % 2][:], AF.Silu), "m4_sil", 1)

        @block.vector
        def _(v):
            v = p.wrap(v, "v")
            v.wait_ge(p.sems["m4_yld"], B["m4_yld"] + 16 * 8)
            for h in range(64):
                v.wait_ge(p.sems["m4_sil"], B["m4_sil"] + h + 1)
                yv = yz[:, h * 1024:(h + 1) * 1024]
                v.tensor_mul(yv, yv, zsil[h % 2][:])
                p.inc(v.engine_nop(), "m4_yzs", 1)
                if h >= 2:
                    v.wait_ge(p.sems["m4_smm"], B["m4_smm"] + h - 1)
                v.tensor_mul(sqb[h % 2][:], yv, yv)
                p.inc(v.engine_nop(), "m4_sq", 1)

        @block.tensor
        def _(t):
            for h in range(64):
                t.wait_ge(p.sems["m4_sq"], B["m4_sq"] + h + 1)
                for q in range(2):
                    mi = t.matmul(ap_(pst, q * 512, [[1024, 1], [1, 512]]),
                                  p.R["onescolb"][:], sqb[h % 2][:, q * 512:(q + 1) * 512],
                                  start=(h == 0), stop=(h == 63))
                p.inc(mi, "m4_smm", 1)

        @block.vector
        def _(v):
            v = p.wrap(v, "v")
            v.wait_ge(p.sems["m4_smm"], B["m4_smm"] + 64)
            v.tensor_copy(rrow[:], ap_(pst, 0, [[1024, 1], [1, 1024]]))
            v.tensor_scalar(rrow[:], rrow[:], 1.0 / DSSM, None, ALU.mult)
            v.tensor_scalar_add(rrow[:], rrow[:], 1e-5)
            p.inc(v.engine_nop(), "m4_rowv", 1)

        @block.scalar
        def _(s):
            s = p.wrap(s, "s")
            s.wait_ge(p.sems["m4_rowv"], B["m4_rowv"] + 1)
            p.inc(s.activation(rrow[:], rrow[:], AF.Sqrt), "m4_rowss", 1)

        @block.vector
        def _(v):
            v = p.wrap(v, "v")
            v.wait_ge(p.sems["m4_rowss"], B["m4_rowss"] + 1)
            v.reciprocal(rrow[:], rrow[:])
            p.inc(v.engine_nop(), "m4_rowv", 1)

        @block.tensor
        def _(t):
            t.wait_ge(p.sems["m4_rowv"], B["m4_rowv"] + 2)
            for q in range(2):
                mi = t.matmul(ap_(pst, q * 512, [[1024, 128], [1, 512]]),
                              p.R["onesrowf"][:], rrow[:, q * 512:(q + 1) * 512],
                              start=True, stop=True)
            p.inc(mi, "m4_bct", 1)

        @block.scalar
        def _(s):
            s = p.wrap(s, "s")
            s.wait_ge(p.sems["m4_bct"], B["m4_bct"] + 1)
            p.inc(s.copy(rstdb[:], pst[:]), "m4_bcs", 1)

        @block.vector
        def _(v):
            v = p.wrap(v, "v")
            v.wait_ge(p.sems["m4_bcs"], B["m4_bcs"] + 1)
            for h in range(64):
                yv = yz[:, h * 1024:(h + 1) * 1024]
                v.scalar_tensor_tensor(yv, yv, nw[:, h:h + 1], rstdb[:],
                                       ALU.mult, ALU.mult)
            p.inc(v.engine_nop(), "m4_yn", 1)

        def _rold(g, o):
            if o >= 2:
                g.wait_ge(p.sems["m4_ev"], B["m4_ev"] + o - 1)
            elif B["m4_ev"]:
                g.wait_ge(p.sems["m4_ev"], B["m4_ev"])
            if res_old[0] == "x0":
                src_ = ap_(d["x0"], o * 128 * T + th * 1024, [[T, 128], [1, 1024]])
            else:
                src_ = ap_(d["res_sp"],
                           res_old[1] * DC * 128 * T + o * 128 * T + th * 1024,
                           [[T, 128], [1, 1024]])
            p.dma(g, "m4_rold", out=rold[o % 2][:], in_=src_)

        @block.gpsimd
        def _(g):
            g = p.wrap(g, "g")
            if B["m4_mm"]:
                g.wait_ge(p.sems["m4_mm"], B["m4_mm"])
            for o in range(32):
                if o >= 2:
                    g.wait_ge(p.sems["m4_mm"], B["m4_mm"] + o - 1)
                p.dma(g, "m4_wld", 
                    out=ap_(wob[o % 2], 0, [[64 * 128, 128], [128, 64], [1, 128]]),
                    in_=d["w_out"][o].rearrange("a p q -> p a q"))
                if o >= 2:
                    _rold(g, o - 2)
            _rold(g, 30)
            _rold(g, 31)

        @block.tensor
        def _(t):
            t.wait_ge(p.sems["m4_yn"], B["m4_yn"] + 1)
            for o in range(32):
                t.wait_ge(p.sems["m4_wld"], B["m4_wld"] + 16 * (o + 1))
                if o >= 2:
                    t.wait_ge(p.sems["m4_ev"], B["m4_ev"] + o - 1)
                for k in range(64):
                    lt = wob[o % 2][:, k * 128:(k + 1) * 128]
                    for j in range(2):
                        mi = t.matmul(ap_(psE[o % 2], j * 512, [[1024, 128], [1, 512]]),
                                      lt, yz[:, k * 1024 + j * 512: k * 1024 + (j + 1) * 512],
                                      start=(k == 0), stop=(k == 63))
                p.inc(mi, "m4_mm", 1)

        @block.vector
        def _(v):
            v = p.wrap(v, "v")
            if B["m4_st"]:
                v.wait_ge(p.sems["m4_st"], B["m4_st"])
            for o in range(32):
                v.wait_ge(p.sems["m4_mm"], B["m4_mm"] + o + 1)
                v.wait_ge(p.sems["m4_rold"], B["m4_rold"] + 16 * (o + 1))
                if o >= 2:
                    v.wait_ge(p.sems["m4_st"], B["m4_st"] + 16 * (o - 1))
                v.tensor_add(rnew[o % 2][:], psE[o % 2][:], rold[o % 2][:])
                p.inc(v.engine_nop(), "m4_ev", 1)

        @block.sync
        def _(s):
            for o in range(32):
                s.wait_ge(p.sems["m4_ev"], B["m4_ev"] + o + 1)
                p.dma(s, "m4_st", 
                    out=ap_(d["res_sp"], res_new_idx * DC * 128 * T + o * 128 * T + th * 1024,
                            [[T, 128], [1, 1024]]),
                    in_=rnew[o % 2][:])
            s.wait_ge(p.sems["m4_st"], B["m4_st"] + 16 * 32)
    ms.close()



def emit_dump_res(p, d, idx):
    block = p.block
    ms = p.scope()
    stage = [p.sbuf(f"du_{i}", [128, T], BF16, scope=ms) for i in range(2)]
    p.sem("du_ld"); p.sem("du_st")

    @block.sync
    def _(s):
        for c in range(DC):
            if c >= 2:
                s.wait_ge(p.sems["du_st"], 16 * (c - 1))
            p.dma(s, "du_ld", out=stage[c % 2][:], in_=d["res_sp"][idx, c])

    @block.gpsimd
    def _(g):
        g = p.wrap(g, "g")
        for c in range(DC):
            g.wait_ge(p.sems["du_ld"], 16 * (c + 1))
            p.dma(g, "du_st", out=d["out"][c * 128:(c + 1) * 128, :], in_=stage[c % 2][:])
        g.wait_ge(p.sems["du_st"], 16 * DC)
    ms.close()



def emit_dump_sbuf(p, d, src_tensor, rows, cols):
    block = p.block
    p.sem("dux")

    @block.gpsimd
    def _(g):
        g = p.wrap(g, "g")
        for s_ in list(p.sems):
            pass
        p.dma(g, "dux", out=d["out"][0:rows, 0:cols], in_=src_tensor[0:rows, 0:cols])
        g.wait_ge(p.sems["dux"], 16)


def emit_dump_dram(p, d, name, nrow, ncol, row_stride=None):
    block = p.block
    ms = p.scope()
    stage = p.sbuf("dud", [128, ncol], BF16, scope=ms)
    p.sem("dud_a"); p.sem("dud_b")
    nt = nrow // 128

    @block.sync
    def _(s):
        for i in range(nt):
            if i >= 1:
                s.wait_ge(p.sems["dud_b"], 16 * i)
            p.dma(s, "dud_a", out=stage[:],
                              in_=ap_(d[name], i * 128 * ncol, [[ncol, 128], [1, ncol]]))

    @block.gpsimd
    def _(g):
        g = p.wrap(g, "g")
        for i in range(nt):
            g.wait_ge(p.sems["dud_a"], 16 * (i + 1))
            p.dma(g, "dud_b", out=d["out"][i * 128:(i + 1) * 128, 0:ncol], in_=stage[:])
        g.wait_ge(p.sems["dud_b"], 16 * nt)
    ms.close()


def emit_final_drain(p):
    block = p.block
    cnts = dict(p.cnt)

    @block.sync
    def _(s):
        for name, v in cnts.items():
            if v > 0:
                s.wait_ge(p.sems[name], v)


def build_full(parts=("mamba", "ffn", "nsa"), upto="all", test_dump=None, n_cores=8):
    nc = bass.Bass("TRN2", target_bir_lowering=False)
    with ExitStack() as ctx:
        block = ctx.enter_context(nc.Block())
        p = Prog(nc, ctx, block)
        d = declare_dram(nc, parts)
        emit_init(p, d)
        xsc = p.scope()
        xln = emit_boundary(p, d, 0, ("x0",), xsc)
        p.phase_end(("bn0_",))
        if upto == "boundary":
            emit_dump_sbuf2(p, d, xln)
            emit_final_drain(p)
            p.right.close()
            return nc
        if "mamba" in parts:
            emit_mamba_m1(p, d, xln)
            p.barrier()
            if upto == "m1":
                emit_dump_dram(p, d, "xact_sp", 4096, 128)
                emit_final_drain(p)
                xsc.close(); p.mam_res.close(); p.right.close()
                return nc
            xsc.close()
            emit_mamba_m3pre(p, d)
            if upto == "m3pre":
                emit_dump_sbuf3(p, d)
                emit_final_drain(p)
                p.mam_res.close(); p.right.close()
                return nc
            emit_mamba_m3(p, d)
            p.barrier()
            if upto == "m3":
                emit_dump_dram(p, d, "y_sp", 4096, 128)
                emit_final_drain(p)
                p.mam_res.close(); p.right.close()
                return nc
            p.mam_res.close()
            emit_mamba_m4(p, d, ("x0",), 0)
            p.barrier()
        if "ffn" in parts and upto in ("x2", "x3", "full", "all", "nsaq", "nsag"):
            xs1 = p.scope()
            xln1 = emit_boundary(p, d, 1, ("res", 0), xs1)
            p.phase_end(("bn1_",))
            emit_ffn(p, d, xln1, 1, ("res", 0), 1, final=False)
            p.phase_end(("f1_",))
            xs1.close()
            if upto == "x2":
                emit_dump_res(p, d, 1)
                emit_final_drain(p)
                p.right.close()
                return nc
        if "nsa" in parts and upto in ("x3", "full", "all", "nsaq", "nsag"):
            xs2 = p.scope()
            xln2 = emit_boundary(p, d, 2, ("res", 1), xs2)
            p.phase_end(("bn2_",))
            nres = p.scope()
            emit_nsa_qkv(p, d, xln2, nres)
            p.phase_end(("nq_",))
            xs2.close()
            emit_nsa_prep(p, d, nres)
            p.phase_end(("n0_",))
            if upto == "nsaq":
                emit_dump_dram(p, d, "q_sp", 4096, 2048)
                emit_final_drain(p)
                nres.close(); p.right.close()
                return nc
            emit_nsa_attn(p, d, nres)
            p.phase_end(("n1_",))
            nres.close()
            if upto == "nsag":
                emit_dump_dram(p, d, "gat_sp", 4096, 2048)
                emit_final_drain(p)
                p.right.close()
                return nc
            emit_nsa_out(p, d, 1, 0)
            p.phase_end(("no_",))
            if upto == "x3":
                emit_dump_res(p, d, 0)
                emit_final_drain(p)
                p.right.close()
                return nc
            xs3 = p.scope()
            xln3 = emit_boundary(p, d, 3, ("res", 0), xs3)
            p.phase_end(("bn3_",))
            emit_ffn(p, d, xln3, 3, ("res", 0), None, final=True)
            xs3.close()
        if test_dump is not None:
            emit_dump_res(p, d, test_dump)
        emit_final_drain(p)
        p.right.close()
    return nc


def emit_dump_sbuf2(p, d, xln):
    block = p.block
    p.sem("dux")

    @block.gpsimd
    def _(g):
        g = p.wrap(g, "g")
        for c in range(DC):
            p.dma(g, "dux", out=d["out"][c * 128:(c + 1) * 128, :],
                              in_=ap_(xln, c * T, [[DC * T, 128], [1, T]]))
        g.wait_ge(p.sems["dux"], 16 * DC)


def emit_dump_sbuf3(p, d):
    block = p.block
    p.sem("dux"); p.sem("duy")

    @block.sync
    def _(s):
        p.dma(s, "duy", out=d["dbg_sp"], in_=p.R["m3_cum"][:])

    @block.gpsimd
    def _(g):
        g = p.wrap(g, "g")
        g.wait_ge(p.sems["duy"], 16)
        p.dma(g, "dux", out=d["out"][0:64, :], in_=d["dbg_sp"])
        g.wait_ge(p.sems["dux"], 16)


def host_inputs(inputs, parts=("mamba", "ffn", "nsa")):
    consts = prep_consts()
    w = {}
    if "mamba" in parts:
        w.update(prep_mamba(inputs))
    if "ffn" in parts:
        w.update(prep_ffn(inputs))
    if "nsa" in parts:
        w.update(prep_nsa(inputs))
    needed_consts = {"ident"}
    if "mamba" in parts:
        needed_consts |= {"mk_mask"}
    if "nsa" in parts:
        needed_consts |= {"cmask", "fown", "iotajb", "sbandF", "sbandF0", "rcos", "rsin"}
    base = {k: consts[k] for k in needed_consts}
    base.update(w)
    x = np.asarray(inputs["L0_attn_a_hidden_states"], np.float32)
    maps = []
    for b in range(x.shape[0]):
        m = dict(base)
        m["x0"] = np.ascontiguousarray(x[b].T)
        maps.append(m)
    return maps


def kernel(**inputs):
    parts = ("mamba", "ffn", "nsa")
    nc = build_full(parts=parts, test_dump=None)
    maps = host_inputs(inputs, parts)
    res = run_bass_kernel_spmd(nc, maps, core_ids=list(range(8)))
    return np.stack([np.asarray(r["out"]).T for r in res.results], axis=0)



def emit_ffn(p, d, xln, idx, res_old, res_new, final=False):
    block = p.block
    fs = p.scope()
    wfb = [p.sbuf(f"f{idx}_w{i}", [128, 32 * 128], BF16, scope=fs) for i in range(2)]
    brf = p.sbuf(f"f{idx}_brf", [128, 4096], F32, scope=fs)
    bsc = p.sbuf(f"f{idx}_bsc", [128, 4096], BF16, scope=fs)
    afb = p.sbuf(f"f{idx}_afb", [128, 32 * 128], BF16, scope=fs)
    ter = p.sbuf(f"f{idx}_ter", [128, T], BF16, scope=fs)
    qcol = p.sbuf(f"f{idx}_q", [128, 32], F32, scope=fs)
    rwf = p.sbuf(f"f{idx}_rw", [128, 256], F32, scope=fs)
    rbc = p.sbuf(f"f{idx}_rb", [8, 1], F32, scope=fs)
    lcol = p.sbuf(f"f{idx}_lc", [8, 1], F32, scope=fs)
    lrow = p.sbuf(f"f{idx}_lr", [1, 8], F32, scope=fs)
    lmax = p.sbuf(f"f{idx}_lm", [1, 1], F32, scope=fs)
    rcol = p.sbuf(f"f{idx}_rc", [128, 1], F32, scope=fs)
    fbc = p.sbuf(f"f{idx}_fb", [128, 32], F32, scope=fs)
    rold = [p.sbuf(f"f{idx}_ro{i}", [128, T],
                   F32 if res_old[0] == "x0" else BF16, scope=fs) for i in range(2)]
    rnew = [p.sbuf(f"f{idx}_rn{i}", [128, T], F32 if final else BF16, scope=fs)
            for i in range(2)]
    psF = [p.psum(f"f{idx}_ps{i}", [128, 2048], scope=fs) for i in range(2)]
    plog = psF[1]
    S = lambda n: f"f{idx}_" + n
    for n in ("ld", "qv", "lmm", "lev", "rt1", "rt2", "rt2v", "rt2s", "rt3", "rts",
              "bsc", "aldg", "ter", "tev", "wld", "mm", "rold", "ev", "st"):
        p.sem(S(n))

    @block.sync
    def _(s):
        p.dma(s, S("ld"), out=ap_(rwf, 0, [[256, 128], [8, 32], [1, 8]]),
              in_=d["ffn_rw"].rearrange("a p q -> p a q"))
        p.dma(s, S("ld"), out=rbc[:], in_=d["ffn_rb"])
        p.dma(s, S("ld"), out=fbc[:], in_=d["ffn_b"])
        p.dma(s, S("ld"), out=brf[:], in_=d["ffn_bmat"])
        s.wait_ge(p.sems[S("lev")], 1)
        p.dma(s, S("rt1"), out=d["r8"], in_=ap_(lcol, 0, [[1, 8], [1, 1]]))
        s.wait_ge(p.sems[S("rt1")], 16)
        p.dma(s, S("rt2"), out=lrow[:], in_=d["r8"])
        s.wait_ge(p.sems[S("rts")], 1)
        p.dma(s, S("rt1"), out=d["r8"], in_=lrow[:])
        s.wait_ge(p.sems[S("rt1")], 32)
        p.dma(s, S("rt3"), out=rcol[:], in_=ap_(d["r8"], 0, [[1, 8], [0, 16]]))

    @block.gpsimd
    def _(g):
        g = p.wrap(g, "g")
        p.dma(g, S("aldg"), out=afb[:],
              in_=d["ffn_a"][0].rearrange("a p q -> p a q"))
        for o in range(32):
            if o >= 2:
                g.wait_ge(p.sems[S("mm")], o - 1)
            p.dma(g, S("wld"), out=ap_(wfb[o % 2], 0, [[4096, 128], [128, 32], [1, 128]]),
                  in_=d["ffn_w"][o].rearrange("a p q -> p a q"))
            if o >= 2:
                oo = o - 2
                if oo >= 2:
                    g.wait_ge(p.sems[S("ev")], oo - 1)
                if res_old[0] == "x0":
                    p.dma(g, S("rold"), out=rold[oo % 2][:],
                          in_=ap_(d["x0"], oo * 128 * T, [[T, 128], [1, T]]))
                else:
                    p.dma(g, S("rold"), out=rold[oo % 2][:],
                          in_=ap_(d["res_sp"], res_old[1] * DC * 128 * T + oo * 128 * T,
                                  [[T, 128], [1, T]]))
        for oo in (30, 31):
            g.wait_ge(p.sems[S("ev")], oo - 1)
            if res_old[0] == "x0":
                p.dma(g, S("rold"), out=rold[oo % 2][:],
                      in_=ap_(d["x0"], oo * 128 * T, [[T, 128], [1, T]]))
            else:
                p.dma(g, S("rold"), out=rold[oo % 2][:],
                      in_=ap_(d["res_sp"], res_old[1] * DC * 128 * T + oo * 128 * T,
                              [[T, 128], [1, T]]))

    @block.vector
    def _(v):
        v = p.wrap(v, "v")
        for c in range(DC):
            v.reduce_sum(qcol[:, c:c + 1],
                         ap_(xln, c * T + 611, [[DC * T, 128], [1, 1436]]), axis=AX.X)
        v.tensor_scalar(qcol[:], qcol[:], 1.0 / 1436.0, None, ALU.mult)
        p.inc(v.engine_nop(), S("qv"), 1)
        v.wait_ge(p.sems[S("lmm")], 1)
        v.wait_ge(p.sems[S("ld")], 32)
        v.tensor_add(lcol[:], ap_(plog, 0, [[2048, 8], [1, 1]]), rbc[:])
        p.inc(v.engine_nop(), S("lev"), 1)
        v.wait_ge(p.sems[S("rt2")], 16)
        v.reduce_max(lmax[:], lrow[:], axis=AX.X)
        v.tensor_scalar(lrow[:], lrow[:], lmax[:], None, ALU.subtract)
        p.inc(v.engine_nop(), S("rt2v"), 1)

    @block.scalar
    def _(s):
        s = p.wrap(s, "s")
        s.wait_ge(p.sems[S("rt2v")], 1)
        p.inc(s.activation(lrow[:], lrow[:], AF.Exp), S("rt2s"), 1)

    @block.vector
    def _(v):
        v = p.wrap(v, "v")
        v.wait_ge(p.sems[S("rt2s")], 1)
        v.reduce_sum(lmax[:], lrow[:], axis=AX.X)
        v.reciprocal(lmax[:], lmax[:])
        v.tensor_scalar(lrow[:], lrow[:], lmax[:], None, ALU.mult)
        p.inc(v.engine_nop(), S("rts"), 1)
        v.wait_ge(p.sems[S("rt3")], 16)
        v.wait_ge(p.sems[S("ld")], 64)
        v.tensor_scalar(bsc[:], brf[:], rcol[:], 2.0, ALU.mult, ALU.mult)
        p.inc(v.engine_nop(), S("bsc"), 1)
        v.wait_ge(p.sems[S("ter")], 1)
        v.tensor_copy(ter[:], psF[0][:])
        p.inc(v.engine_nop(), S("tev"), 1)

    @block.tensor
    def _(t):
        t.wait_ge(p.sems[S("qv")], 1)
        t.wait_ge(p.sems[S("ld")], 16)
        for c in range(DC):
            mi = t.matmul(ap_(plog, 0, [[2048, 8], [1, 1]]),
                          rwf[:, c * 8:(c + 1) * 8], qcol[:, c:c + 1],
                          start=(c == 0), stop=(c == DC - 1))
        p.inc(mi, S("lmm"), 1)
        t.wait_ge(p.sems[S("aldg")], 16)
        for dd in range(DC):
            lt = afb[:, dd * 128:(dd + 1) * 128]
            for j in range(4):
                mi = t.matmul(ap_(psF[0], j * 512, [[2048, 128], [1, 512]]),
                              lt, ap_(xln, dd * T + j * 512, [[DC * T, 128], [1, 512]]),
                              start=(dd == 0), stop=(dd == DC - 1))
        p.inc(mi, S("ter"), 1)
        t.wait_ge(p.sems[S("tev")], 1)
        t.wait_ge(p.sems[S("bsc")], 1)
        for o in range(32):
            t.wait_ge(p.sems[S("wld")], 16 * (o + 1))
            if o >= 2:
                t.wait_ge(p.sems[S("ev")], o - 1)
            pso = psF[o % 2]
            for dd in range(DC):
                lt = wfb[o % 2][:, dd * 128:(dd + 1) * 128]
                for j in range(4):
                    t.matmul(ap_(pso, j * 512, [[2048, 128], [1, 512]]),
                             lt, ap_(xln, dd * T + j * 512, [[DC * T, 128], [1, 512]]),
                             start=(dd == 0), stop=False)
            for j in range(4):
                mi = t.matmul(ap_(pso, j * 512, [[2048, 128], [1, 512]]),
                              bsc[:, o * 128:(o + 1) * 128],
                              ter[:, j * 512:(j + 1) * 512],
                              start=False, stop=True)
            p.inc(mi, S("mm"), 1)

    @block.vector
    def _(v):
        v = p.wrap(v, "v")
        v.wait_ge(p.sems[S("ld")], 48)
        for o in range(32):
            v.wait_ge(p.sems[S("mm")], o + 1)
            v.wait_ge(p.sems[S("rold")], 16 * (o + 1))
            if o >= 2:
                v.wait_ge(p.sems[S("st")], 16 * (o - 1))
            v.scalar_tensor_tensor(rnew[o % 2][:], psF[o % 2][:], fbc[:, o:o + 1],
                                   rold[o % 2][:], ALU.add, ALU.add)
            p.inc(v.engine_nop(), S("ev"), 1)

    @block.sync
    def _(s):
        for o in range(32):
            s.wait_ge(p.sems[S("ev")], o + 1)
            if final:
                p.dma(s, S("st"), out=d["out"][o * 128:(o + 1) * 128, :], in_=rnew[o % 2][:])
            else:
                p.dma(s, S("st"),
                      out=ap_(d["res_sp"], res_new * DC * 128 * T + o * 128 * T,
                              [[T, 128], [1, T]]),
                      in_=rnew[o % 2][:])
        s.wait_ge(p.sems[S("st")], 16 * 32)
    fs.close()



def emit_nsa_qkv(p, d, xln, nres):
    block = p.block
    ns = p.scope()
    wqb = [p.sbuf(f"nq_w{i}", [128, 32 * 128], BF16, scope=ns) for i in range(2)]
    qst = [p.sbuf(f"nq_s{i}", [128, T], BF16, scope=ns) for i in range(2)]
    wgb = p.sbuf("nq_wg", [128, 32 * 128], BF16, scope=ns)
    bgc = p.sbuf("nq_bg", [128, 1], F32, scope=ns)
    gfm = p.sbuf("nq_gfm", [128, T], F32, scope=ns)
    psQ = [p.psum(f"nq_ps{i}", [128, 2048], scope=ns) for i in range(2)]
    gtok = p.sbuf("gtok", [128, 2048], F32, scope=nres, side="right")
    for n in ("nq_wld", "nq_gld", "nq_mm", "nq_ev", "nq_st", "nq_gmm", "nq_gev",
              "nq_gtr", "nq_gtv"):
        p.sem(n)

    @block.sync
    def _(s):
        p.dma(s, "nq_gld", out=bgc[:], in_=d["b_gate"])

    @block.gpsimd
    def _(g):
        g = p.wrap(g, "g")
        p.dma(g, "nq_wld", out=ap_(wgb, 0, [[4096, 128], [128, 32], [1, 128]]),
              in_=d["w_gate"][0].rearrange("a p q -> p a q"))
        for o in range(48):
            if o >= 2:
                g.wait_ge(p.sems["nq_mm"], o - 1)
            p.dma(g, "nq_wld", out=ap_(wqb[o % 2], 0, [[4096, 128], [128, 32], [1, 128]]),
                  in_=d["w_qkv"][o].rearrange("a p q -> p a q"))

    @block.tensor
    def _(t):
        t.wait_ge(p.sems["nq_wld"], 16)
        for dd in range(DC):
            lt = wgb[:, dd * 128:(dd + 1) * 128]
            for j in range(4):
                mi = t.matmul(ap_(psQ[1], j * 512, [[2048, 128], [1, 512]]),
                              lt, ap_(xln, dd * T + j * 512, [[DC * T, 128], [1, 512]]),
                              start=(dd == 0), stop=(dd == DC - 1))
        p.inc(mi, "nq_gmm", 1)
        t.wait_ge(p.sems["nq_gev"], 1)
        for tc in range(NCH):
            if tc >= 2:
                t.wait_ge(p.sems["nq_gtv"], tc - 1)
            mi = t.transpose(ap_(psQ[1], (tc % 2) * 512, [[2048, 128], [1, 128]]),
                             gfm[:, tc * 128:(tc + 1) * 128], p.R["identf"][:])
            p.inc(mi, "nq_gtr", 1)
        for o in range(48):
            t.wait_ge(p.sems["nq_wld"], 16 * (o + 2))
            if o <= 1:
                t.wait_ge(p.sems["nq_gtv"], NCH)
            if o >= 2:
                t.wait_ge(p.sems["nq_ev"], o - 1)
            pso = psQ[o % 2]
            for dd in range(DC):
                lt = wqb[o % 2][:, dd * 128:(dd + 1) * 128]
                for j in range(4):
                    mi = t.matmul(ap_(pso, j * 512, [[2048, 128], [1, 512]]),
                                  lt, ap_(xln, dd * T + j * 512, [[DC * T, 128], [1, 512]]),
                                  start=(dd == 0), stop=(dd == DC - 1))
            p.inc(mi, "nq_mm", 1)

    @block.scalar
    def _(s):
        s = p.wrap(s, "s")
        s.wait_ge(p.sems["nq_gmm"], 1)
        s.wait_ge(p.sems["nq_gld"], 16)
        p.inc(s.activation(gfm[:], psQ[1][:], AF.Sigmoid, bias=bgc[:]), "nq_gev", 1)
        for o in range(48):
            s.wait_ge(p.sems["nq_mm"], o + 1)
            if o >= 2:
                s.wait_ge(p.sems["nq_st"], 16 * (o - 1))
            p.inc(s.copy(qst[o % 2][:], psQ[o % 2][:]), "nq_ev", 1)

    @block.vector
    def _(v):
        v = p.wrap(v, "v")
        for tc in range(NCH):
            v.wait_ge(p.sems["nq_gtr"], tc + 1)
            v.tensor_copy(gtok[:, tc * 128:(tc + 1) * 128],
                          ap_(psQ[1], (tc % 2) * 512, [[2048, 128], [1, 128]]))
            p.inc(v.engine_nop(), "nq_gtv", 1)

    @block.sync
    def _(s):
        for o in range(48):
            s.wait_ge(p.sems["nq_ev"], o + 1)
            if o < 32:
                dst = ap_(d["q_sp"], o * 128 * T, [[T, 128], [1, T]])
            elif o < 40:
                dst = ap_(d["k_sp"], (o - 32) * 128 * T, [[T, 128], [1, T]])
            else:
                dst = ap_(d["v_sp"], (o - 40) * 128 * T, [[T, 128], [1, T]])
            p.dma(s, "nq_st", out=dst, in_=qst[o % 2][:])
        s.wait_ge(p.sems["nq_st"], 16 * 48)
    ns.close()



def emit_nsa_prep(p, d, nres):
    block = p.block
    n0 = p.scope()
    rk = p.sbuf("n_rk", [128, 16384], BF16, scope=nres, side="right")
    vT = p.sbuf("n_vT", [128, 16384], BF16, scope=nres, side="right")
    ckb = p.sbuf("n_ckb", [128, 1032], BF16, scope=nres, side="right")
    cvb = p.sbuf("n_cvb", [128, 1032], BF16, scope=nres, side="right")
    cvT = p.sbuf("n_cvT", [128, 1024], BF16, scope=nres, side="right")
    cvlast = p.sbuf("n_cvlast", [65, 1024], BF16, scope=nres, side="right")
    cmaskS = p.sbuf("n_cmask", [128, 2064], F32, scope=nres, side="right")
    fownS = p.sbuf("n_fown", [128, 128], F32, scope=nres, side="right")
    sbF = p.sbuf("n_sbF", [128, 256], F32, scope=nres, side="right")
    sbF0 = p.sbuf("n_sbF0", [128, 128], F32, scope=nres, side="right")
    rcosf = p.sbuf("n0_cos", [128, T], BF16, scope=nres, side="right")
    rsinf = p.sbuf("n0_sin", [128, T], BF16, scope=nres, side="right")
    kfm = p.sbuf("n0_kfm", [128, 16384], BF16, scope=n0)
    ckin = p.sbuf("n0_ckin", [128, 8192], BF16, scope=n0)
    h1 = p.sbuf("n0_h1", [128, 8192], BF16, scope=n0)
    kposf = p.sbuf("n0_kp", [128, 128], F32, scope=n0)
    rtmp = p.sbuf("n0_rt", [128, T], BF16, scope=n0)
    w1b = [p.sbuf(f"n0_w1{i}", [128, 2048], BF16, scope=n0) for i in range(2)]
    w2b = p.sbuf("n0_w2", [128, 2048], BF16, scope=n0)
    b1f = p.sbuf("n0_b1", [128, 16], F32, scope=n0)
    b2f = p.sbuf("n0_b2", [128, 1], F32, scope=n0)
    memf = p.sbuf("n0_mem", [128, 8], F32, scope=n0)
    psN = [p.psum(f"n0_ps{i}", [128, 512], scope=n0) for i in range(2)]
    ptrN = p.psum("n0_ptr", [128, 1024], BF16, scope=n0)
    for n in ("n0_cld", "n0_cldg", "n0_kld", "n0_rp", "n0_ci", "n0_w1", "n0_w2",
              "n0_mm1", "n0_h1", "n0_mm2", "n0_cke", "n0_meme", "n0_tr", "n0_tv",
              "n0_bld"):
        p.sem(n)

    def side(wname1, wname2, bname1, bname2, posname, memname, spname, is_k):
        SD = 0 if is_k else 1
        ckdst = ckb if is_k else cvb

        @block.sync
        def _(s):
            if is_k:
                p.dma(s, "n0_cld", out=cmaskS[:],
                      in_=ap_(d["cmask"], 0, [[129, 128], [128 * 129, 16], [1, 129]]))
                p.dma(s, "n0_cld", out=fownS[:], in_=d["fown"])
                p.dma(s, "n0_cld", out=sbF[:], in_=d["sbandF"])
                p.dma(s, "n0_cld", out=sbF0[:], in_=d["sbandF0"])
            else:
                s.wait_ge(p.sems["n0_ci"], 2)
                s.wait_ge(p.sems["n0_h1"], 32)
                s.wait_ge(p.sems["n0_cke"], 2)
                s.wait_ge(p.sems["n0_meme"], 1)
                s.wait_ge(p.sems["n0_rp"], 8)
            p.dma(s, "n0_bld", out=kposf[:], in_=d[posname])
            p.dma(s, "n0_bld", out=b1f[:], in_=d[bname1])
            p.dma(s, "n0_bld", out=b2f[:], in_=d[bname2])
            p.dma(s, "n0_bld", out=memf[:], in_=d[memname])
            for h in range(8):
                p.dma(s, "n0_kld", out=ap_(kfm, h * 2048, [[16384, 128], [1, T]]),
                      in_=ap_(d[spname], h * 128 * T, [[T, 128], [1, T]]))

        @block.gpsimd
        def _(g):
            g = p.wrap(g, "g")
            if is_k:
                p.dma(g, "n0_cldg", out=rcosf[0:64, :], in_=d["rcos"])
                p.dma(g, "n0_cldg", out=ap_(rcosf, 64 * T, [[T, 64], [1, T]]),
                      in_=d["rcos"])
                p.dma(g, "n0_cldg", out=rsinf[0:64, :], in_=d["rsin"])
                p.dma(g, "n0_cldg", out=ap_(rsinf, 64 * T, [[T, 64], [1, T]]),
                      in_=d["rsin"])
                g.wait_ge(p.sems["n0_cldg"], 64)
                cos0 = ap_(rcosf, 0, [[T, 64], [1, T]])
                cos1 = ap_(rcosf, 64 * T, [[T, 64], [1, T]])
                sin0 = ap_(rsinf, 0, [[T, 64], [1, T]])
                sin1 = ap_(rsinf, 64 * T, [[T, 64], [1, T]])
                tmp0 = ap_(rtmp, 0, [[T, 64], [1, T]])
                tmp1 = ap_(rtmp, 64 * T, [[T, 64], [1, T]])
                for h in range(8):
                    g.wait_ge(p.sems["n0_kld"], 16 * (h + 1))
                    khi = ap_(kfm, h * 2048, [[16384, 64], [1, T]])
                    klo = ap_(kfm, 64 * 16384 + h * 2048, [[16384, 64], [1, T]])
                    rhi = ap_(rk, h * 2048, [[16384, 64], [1, T]])
                    rlo = ap_(rk, 64 * 16384 + h * 2048, [[16384, 64], [1, T]])
                    g.tensor_mul(rhi, khi, cos0)
                    g.tensor_mul(tmp0, klo, sin1)
                    g.tensor_sub(rhi, rhi, tmp0)
                    g.tensor_mul(rlo, klo, cos1)
                    g.tensor_mul(tmp1, khi, sin0)
                    g.tensor_add(rlo, rlo, tmp1)
                    p.inc(g.engine_nop(), "n0_rp", 1)
            for half in range(2):
                for o in range(16):
                    oo = SD * 32 + half * 16 + o
                    if o >= 2:
                        g.wait_ge(p.sems["n0_mm1"], oo - 1)
                    elif oo >= 16:
                        g.wait_ge(p.sems["n0_mm1"], SD * 32 + half * 16)
                    p.dma(g, "n0_w1",
                          out=ap_(w1b[o % 2], 0, [[2048, 128], [128, 16], [1, 128]]),
                          in_=d[wname1][o].rearrange("a p q -> p a q"))
                if SD * 2 + half >= 1:
                    g.wait_ge(p.sems["n0_mm2"], SD * 2 + half)
                p.dma(g, "n0_w2",
                      out=ap_(w2b, 0, [[2048, 128], [128, 16], [1, 128]]),
                      in_=d[wname2][0].rearrange("a p q -> p a q"))

        @block.vector
        def _(v):
            v = p.wrap(v, "v")
            v.wait_ge(p.sems["n0_bld"], 64 * (SD + 1))
            if not is_k:
                v.wait_ge(p.sems["n0_mm1"], 32)
            for half in range(2):
                for h in range(8):
                    if half == 0:
                        v.wait_ge(p.sems["n0_kld"], 16 * (SD * 8 + h + 1))
                    v.tensor_add(
                        ap_(ckin, h * 64, [[8192, 128], [512, 16], [1, 64]]),
                        ap_(kfm, h * 2048 + half * 1024,
                            [[16384, 128], [1, 16], [16, 64]]),
                        ap_(kposf, h * 16, [[128, 128], [1, 16], [0, 64]]))
                p.inc(v.engine_nop(), "n0_ci", 1)
                if not is_k and half == 0:
                    v.wait_ge(p.sems["n0_mm2"], 2)
                for o in range(16):
                    oo = SD * 32 + half * 16 + o
                    v.wait_ge(p.sems["n0_mm1"], oo + 1)
                    v.tensor_scalar(h1[:, o * 512:(o + 1) * 512],
                                    psN[o % 2][:], b1f[:, o:o + 1], 0.0,
                                    ALU.add, ALU.max)
                    p.inc(v.engine_nop(), "n0_h1", 1)
                v.wait_ge(p.sems["n0_mm2"], SD * 2 + half + 1)
                v.tensor_scalar(ap_(ckdst, 1 + half * 64, [[1032, 128], [129, 8], [1, 64]]),
                                psN[0][:], b2f[:], None, ALU.add)
                p.inc(v.engine_nop(), "n0_cke", 1)
                if half == 0:
                    v.tensor_copy(ap_(ckdst, 0, [[1032, 128], [129, 8]]), memf[:])
                    p.inc(v.engine_nop(), "n0_meme", 1)
            if not is_k:
                for b in range(16 + 2):
                    v.wait_ge(p.sems["n0_tr"], b + 1)
                    if b < 16:
                        v.tensor_copy(vT[:, b * 1024:(b + 1) * 1024], ptrN[:])
                    elif b == 16:
                        v.tensor_copy(cvT[:], ap_(ptrN, 0, [[1024, 128], [1, 1024]]))
                    else:
                        v.tensor_copy(ap_(cvlast, 0, [[1024, 1], [1, 1024]]),
                                      ap_(ptrN, 0, [[1024, 1], [1, 1024]]))
                        v.tensor_copy(ap_(cvlast, 64 * 1024, [[1024, 1], [1, 1024]]),
                                      ap_(ptrN, 0, [[1024, 1], [1, 1024]]))
                    p.inc(v.engine_nop(), "n0_tv", 1)

        @block.tensor
        def _(t):
            for half in range(2):
                t.wait_ge(p.sems["n0_ci"], SD * 2 + half + 1)
                if SD * 2 + half >= 1:
                    t.wait_ge(p.sems["n0_cke"], SD * 2 + half)
                    t.wait_ge(p.sems["n0_h1"], (SD * 2 + half) * 16)
                for o in range(16):
                    oo = SD * 32 + half * 16 + o
                    t.wait_ge(p.sems["n0_w1"], 16 * (oo + 1))
                    if o >= 2:
                        t.wait_ge(p.sems["n0_h1"], (SD * 2 + half) * 16 + o - 1)
                    for i in range(16):
                        lt = w1b[o % 2][:, i * 128:(i + 1) * 128]
                        mi = t.matmul(psN[o % 2][:],
                                      lt, ckin[:, i * 512:(i + 1) * 512],
                                      start=(i == 0), stop=(i == 15))
                    p.inc(mi, "n0_mm1", 1)
                t.wait_ge(p.sems["n0_w2"], 16 * (SD * 2 + half + 1))
                t.wait_ge(p.sems["n0_h1"], (SD * 2 + half) * 16 + 16)
                for i2 in range(16):
                    lt = w2b[:, i2 * 128:(i2 + 1) * 128]
                    mi = t.matmul(psN[0][:],
                                  lt, h1[:, i2 * 512:(i2 + 1) * 512],
                                  start=(i2 == 0), stop=(i2 == 15))
                p.inc(mi, "n0_mm2", 1)
            if not is_k:
                t.wait_ge(p.sems["n0_kld"], 256)
                t.wait_ge(p.sems["n0_cke"], 4)
                t.wait_ge(p.sems["n0_meme"], 2)
                for b in range(16):
                    if b >= 1:
                        t.wait_ge(p.sems["n0_tv"], b)
                    for s8 in range(8):
                        ci = b * 8 + s8
                        mi = t.transpose(ap_(ptrN, s8 * 128, [[1024, 128], [1, 128]]),
                                         kfm[:, ci * 128:(ci + 1) * 128], p.R["identb"][:])
                    p.inc(mi, "n0_tr", 1)
                t.wait_ge(p.sems["n0_tv"], 16)
                for h in range(8):
                    mi = t.transpose(ap_(ptrN, h * 128, [[1024, 128], [1, 128]]),
                                     cvb[:, h * 129 + 1:h * 129 + 129], p.R["identb"][:])
                p.inc(mi, "n0_tr", 1)
                t.wait_ge(p.sems["n0_tv"], 17)
                for h in range(8):
                    mi = t.transpose(ap_(ptrN, h * 128, [[1024, 1], [1, 128]]),
                                     cvb[:, h * 129:h * 129 + 1], p.R["identb"][:])
                p.inc(mi, "n0_tr", 1)

    side("k_w1", "k_w2", "k_b1", "k_b2", "k_pos", "mem_k", "k_sp", True)
    side("v_w1", "v_w2", "v_b1", "v_b2", "v_pos", "mem_v", "v_sp", False)

    @block.sync
    def _(s):
        s.wait_ge(p.sems["n0_cld"], 64)
        s.wait_ge(p.sems["n0_cldg"], 64)
        s.wait_ge(p.sems["n0_tv"], 18)
    n0.close()



def emit_nsa_attn(p, d, nres):
    block = p.block
    na = p.scope()
    q1 = [p.sbuf(f"n1_q{i}", [128, T], BF16, scope=na) for i in range(2)]
    rq1 = [p.sbuf(f"n1_rq{i}", [128, T], BF16, scope=na) for i in range(2)]
    rtmp = p.sbuf("n1_rt", [128, T], BF16, scope=na)
    cst = p.sbuf("n1_cst", [128, 2064], F32, scope=na)
    Sc = p.sbuf("n1_Sc", [128, 2064], BF16, scope=na)
    imp = p.sbuf("n1_imp", [128, 2064], F32, scope=na)
    denc = p.sbuf("n1_denc", [128, 16], F32, scope=na)
    rcg = p.sbuf("n1_rcg", [128, 16], F32, scope=na)
    selm = p.sbuf("n1_selm", [128, 2048], BF16, scope=na)
    e2 = p.sbuf("n1_e2", [128, 128], F32, scope=na)
    v8 = p.sbuf("n1_v8", [128, 8], F32, scope=na)
    wcol = p.sbuf("n1_wc", [128, 8], F32, scope=na)
    m1t = p.sbuf("n1_m1", [128, 128], BF16, scope=na)
    m2t = p.sbuf("n1_m2", [128, 128], BF16, scope=na)
    S_f = p.sbuf("n1_Sf", [128, 2048], BF16, scope=na)
    Pf = [p.sbuf(f"n1_Pf{i}", [128, 2048], BF16, scope=na) for i in range(2)]
    ssb = p.sbuf("n1_ss", [128, 256], BF16, scope=na)
    fcol = p.sbuf("n1_fc", [128, 4], F32, scope=na)
    gcol = p.sbuf("n1_gc", [128, 4], F32, scope=na)
    PcT = p.sbuf("n1_PcT", [128, 8192], BF16, scope=na)
    pcrow = [p.sbuf(f"n1_pcr{i}", [65, 2048], BF16, scope=na) for i in range(2)]
    PT = p.sbuf("n1_PT", [128, 8192], BF16, scope=na)
    ogat = [p.sbuf(f"n1_og{i}", [128, 512], BF16, scope=na) for i in range(2)]
    pO = [p.psum(f"n1_pO{i}", [128, 512], scope=na) for i in range(2)]
    pS = p.psum("n1_pS", [128, 2048], scope=na)
    ptr = p.psum("n1_ptr", [128, 1024], BF16, scope=na)
    for n in ("n1_qld", "n1_gst", "n1_csmm", "n1_csev", "n1_imp", "n1_scx",
              "n1_pcb", "n1_ctr", "n1_ctv", "n1_rp", "n1_selv", "n1_sels",
              "n1_fmm", "n1_fex", "n1_pv", "n1_ftr", "n1_ftv", "n1_omm",
              "n1_oev"):
        p.sem(n)

    rk = p.R["n_rk"]; vT = p.R["n_vT"]; ckb = p.R["n_ckb"]
    cvT = p.R["n_cvT"]; cvlast = p.R["n_cvlast"]
    cmaskS = p.R["n_cmask"]; fownS = p.R["n_fown"]
    sbF = p.R["n_sbF"]; sbF0 = p.R["n_sbF0"]; gtok = p.R["gtok"]
    rcosf = p.R["n0_cos"]; rsinf = p.R["n0_sin"]

    def cs_i(K, g, tc):
        return (4 * K + g) * 16 + tc + 1

    f_i = cs_i

    FT_NB = {}
    FT_CUM = {}
    tot = 0
    for K in range(8):
        for g in range(4):
            for tc in range(16):
                nb = (tc + 8) // 8
                FT_NB[(K, g, tc)] = nb
                tot += nb
                FT_CUM[(K, g, tc)] = tot

    def prev_tile(K, g, tc):
        tc -= 1
        if tc < 0:
            tc = 15
            g -= 1
            if g < 0:
                g = 3
                K -= 1
        return (K, g, tc)

    def pCap(tc):
        return ap_(pS, (tc % 2) * 512, [[2048, 128], [1, 129]])

    def pcrow_ap(g, off, n):
        return ap_(pcrow[g // 2], (g % 2) * 64 * 2048 + off, [[2048, 1], [1, n]])

    @block.sync
    def _(s):
        for K in range(8):
            for g in range(4):
                hg = 4 * K + g
                if hg >= 2:
                    s.wait_ge(p.sems["n1_csmm"], (hg - 2) * 16 + 16)
                    s.wait_ge(p.sems["n1_rp"], hg - 1)
                p.dma(s, "n1_qld", out=q1[hg % 2][:],
                      in_=ap_(d["q_sp"], hg * 128 * T, [[T, 128], [1, T]]))
            for g in range(4):
                hg = 4 * K + g
                for qr in range(4):
                    qi = hg * 4 + qr
                    s.wait_ge(p.sems["n1_oev"], qi + 1)
                    p.dma(s, "n1_gst",
                          out=ap_(d["gat_sp"], hg * 128 * T + qr * 512,
                                  [[T, 128], [1, 512]]),
                          in_=ogat[qi % 2][:])
        s.wait_ge(p.sems["n1_gst"], 16 * 128)

    @block.tensor
    def _(t):
        for K in range(8):
            def emit_ctr(g):
                hg = 4 * K + g
                t.wait_ge(p.sems["n1_pcb"], hg + 1)
                if K >= 1 and g == 0:
                    t.wait_ge(p.sems["n1_ftv"], FT_CUM[(K - 1, 3, 15)])
                for b in range(4):
                    bi = hg * 4 + b + 1
                    if bi >= 2:
                        t.wait_ge(p.sems["n1_ctv"], bi - 1)
                    for q4 in range(4):
                        tc = b * 4 + q4
                        t.transpose(ap_(ptr, q4 * 128, [[1024, 128], [1, 128]]),
                                    Sc[:, tc * 129 + 1:tc * 129 + 129], p.R["identb"][:])
                        mi = t.transpose(ap_(ptr, 512 + q4 * 128, [[1024, 1], [1, 128]]),
                                         Sc[:, tc * 129:tc * 129 + 1],
                                         p.R["identb"][:])
                    p.inc(mi, "n1_ctr", 1)
            for g in range(4):
                hg = 4 * K + g
                t.wait_ge(p.sems["n1_qld"], 16 * (hg + 1))
                if K >= 1 and g == 0:
                    t.wait_ge(p.sems["n1_fex"], 64 * K)
                for tc in range(NCH):
                    ci = cs_i(K, g, tc)
                    if ci >= 3:
                        t.wait_ge(p.sems["n1_csev"], ci - 2)
                    mi = t.matmul(pCap(tc),
                                  q1[hg % 2][:, tc * 128:(tc + 1) * 128],
                                  ckb[:, K * 129:(K + 1) * 129],
                                  start=True, stop=True)
                    p.inc(mi, "n1_csmm", 1)
                if g >= 1:
                    emit_ctr(g - 1)
            emit_ctr(3)
            for g in range(4):
                hg = 4 * K + g
                def emit_ftr(tc):
                    fi = f_i(K, g, tc)
                    t.wait_ge(p.sems["n1_pv"], fi)
                    t.wait_ge(p.sems["n1_ctv"], 16 * (K + 1))
                    nb = FT_NB[(K, g, tc)]
                    for b in range(nb):
                        bti = FT_CUM[(K, g, tc)] - nb + b + 1
                        if bti >= 2:
                            t.wait_ge(p.sems["n1_ftv"], bti - 1)
                        ns_ = min(8, tc + 1 - b * 8)
                        for s8 in range(ns_):
                            mi = t.transpose(
                                ap_(ptr, s8 * 128, [[1024, 128], [1, 128]]),
                                Pf[tc % 2][:, (b * 8 + s8) * 128:(b * 8 + s8 + 1) * 128],
                                p.R["identb"][:])
                        p.inc(mi, "n1_ftr", 1)
                for qr in range(4):
                    qi = hg * 4 + qr
                    for t4 in range(4):
                        tc = qr * 4 + t4
                        fi = f_i(K, g, tc)
                        if fi >= 2:
                            t.wait_ge(p.sems["n1_fex"], fi - 1)
                        t.wait_ge(p.sems["n1_rp"], hg + 1)
                        lt = rq1[hg % 2][:, tc * 128:(tc + 1) * 128]
                        wtot = (tc + 1) * 128
                        for q_ in range((wtot + 511) // 512):
                            wv = min(512, wtot - q_ * 512)
                            mi = t.matmul(ap_(pS, q_ * 512, [[2048, 128], [1, wv]]),
                                          lt,
                                          rk[:, K * 2048 + q_ * 512:K * 2048 + q_ * 512 + wv],
                                          start=True, stop=True)
                        p.inc(mi, "n1_fmm", 1)
                        if t4 >= 1:
                            emit_ftr(tc - 1)
                    emit_ftr(qr * 4 + 3)
                    t.wait_ge(p.sems["n1_ctv"], 4 * (hg + 1))
                    t.wait_ge(p.sems["n1_ftv"], FT_CUM[(K, g, qr * 4 + 3)])
                    if qi >= 2:
                        t.wait_ge(p.sems["n1_oev"], qi - 1)
                    t.matmul(pO[qi % 2][:],
                             cvT[:, K * 128:(K + 1) * 128],
                             PcT[:, g * 2048 + qr * 512:g * 2048 + (qr + 1) * 512],
                             start=True, stop=False, skip_group_check=True)
                    t.matmul(pO[qi % 2][:],
                             ap_(cvlast, (g % 2) * 64 * 1024 + K * 128,
                                 [[1024, 1], [1, 128]]),
                             pcrow_ap(g, qr * 512, 512),
                             start=False, stop=False, skip_group_check=True)
                    njc = qr * 4 + 4
                    for jc in range(njc):
                        a0 = max((jc - qr * 4) * 128, 0)
                        mi = t.matmul(ap_(pO[qi % 2], a0, [[512, 128], [1, 512 - a0]]),
                                      vT[:, (K * 16 + jc) * 128:(K * 16 + jc + 1) * 128],
                                      PT[:, jc * 512 + a0:(jc + 1) * 512],
                                      start=False, stop=(jc == njc - 1),
                                      skip_group_check=True)
                    p.inc(mi, "n1_omm", 1)

    @block.scalar
    def _(s):
        s = p.wrap(s, "s")
        for K in range(8):
            for g in range(4):
                hg = 4 * K + g
                s.wait_ge(p.sems["n1_csev"], cs_i(K, g, 15))
                if hg >= 1:
                    s.wait_ge(p.sems["n1_ctr"], 4 * hg)
                p.inc(s.activation(Sc[:], cst[:], AF.Exp), "n1_scx", 1)
            for tc in range(NCH):
                si = 16 * K + tc
                s.wait_ge(p.sems["n1_selv"], 3 * si + 1)
                s.activation(e2[:], imp[:, tc * 129 + 1:tc * 129 + 129],
                             AF.Exp, bias=wcol[:, 1:2])
                p.inc(s.activation(wcol[:, 3:4], wcol[:, 2:3], AF.Exp), "n1_sels", 1)
                s.wait_ge(p.sems["n1_selv"], 3 * si + 2)
                p.inc(s.activation(wcol[:, 5:6], wcol[:, 4:5], AF.Ln), "n1_sels", 1)
            for g in range(4):
                for tc in range(NCH):
                    fi = f_i(K, g, tc)
                    s.wait_ge(p.sems["n1_fmm"], fi)
                    if fi >= 2:
                        s.wait_ge(p.sems["n1_pv"], fi - 1)
                    p.inc(s.activation(S_f[:, 0:(tc + 1) * 128],
                                       pS[:, 0:(tc + 1) * 128], AF.Exp),
                          "n1_fex", 1)

    @block.vector
    def _(v):
        v = p.wrap(v, "v")
        for K in range(8):
            def emit_ctv(g):
                hg = 4 * K + g
                for b in range(4):
                    bi = hg * 4 + b + 1
                    v.wait_ge(p.sems["n1_ctr"], bi)
                    if K >= 1:
                        v.wait_ge(p.sems["n1_omm"], (4 * (K - 1) + g) * 4 + 4)
                    v.tensor_copy(PcT[:, g * 2048 + b * 512:g * 2048 + (b + 1) * 512],
                                  ptr[:, 0:512])
                    v.tensor_copy(pcrow_ap(g, b * 512, 512), ptr[0:1, 512:1024])
                    p.inc(v.engine_nop(), "n1_ctv", 1)
            for g in range(4):
                hg = 4 * K + g
                for tc in range(NCH):
                    ci = cs_i(K, g, tc)
                    v.wait_ge(p.sems["n1_csmm"], ci)
                    if tc == 0:
                        v.wait_ge(p.sems["n1_scx"], hg)
                        v.wait_ge(p.sems["n1_imp"], hg)
                    v.tensor_add(cst[:, tc * 129:(tc + 1) * 129],
                                 pCap(tc), cmaskS[:, tc * 129:(tc + 1) * 129])
                    p.inc(v.engine_nop(), "n1_csev", 1)
                if g == 0:
                    v.wait_ge(p.sems["n1_sels"], 32 * K)
                    v.tensor_copy(imp[:], cst[:])
                else:
                    v.tensor_add(imp[:], imp[:], cst[:])
                    if g == 3:
                        v.tensor_scalar_mul(imp[:], imp[:], 0.25)
                p.inc(v.engine_nop(), "n1_imp", 1)
                if g >= 1:
                    emit_ctv(g - 1)
                v.wait_ge(p.sems["n1_scx"], hg + 1)
                for tc in range(NCH):
                    v.reduce_sum(denc[:, tc:tc + 1],
                                 Sc[:, tc * 129:(tc + 1) * 129], axis=AX.X)
                v.reciprocal(rcg[:], denc[:])
                v.tensor_mul(rcg[:], rcg[:],
                             ap_(gtok, 3 * hg, [[2048, 128], [128, 16]]))
                for tc in range(NCH):
                    v.tensor_scalar(Sc[:, tc * 129:(tc + 1) * 129],
                                    Sc[:, tc * 129:(tc + 1) * 129],
                                    rcg[:, tc:tc + 1], None, ALU.mult)
                p.inc(v.engine_nop(), "n1_pcb", 1)
            emit_ctv(3)
            for tc in range(NCH):
                si = 16 * K + tc
                v.wait_ge(p.sems["n1_imp"], 4 * K + 4)
                v.max(v8[:], imp[:, tc * 129 + 1:tc * 129 + 129])
                v.tensor_scalar_max(wcol[:, 0:1], v8[:, 0:1], -1000.0)
                v.tensor_scalar_mul(wcol[:, 1:2], wcol[:, 0:1], -1.0)
                v.tensor_scalar_sub(wcol[:, 2:3], wcol[:, 1:2], 1000.0)
                p.inc(v.engine_nop(), "n1_selv", 1)
                v.wait_ge(p.sems["n1_sels"], 2 * si + 1)
                v.reduce_sum(wcol[:, 4:5], e2[:], axis=AX.X)
                v.tensor_add(wcol[:, 4:5], wcol[:, 4:5], wcol[:, 3:4])
                p.inc(v.engine_nop(), "n1_selv", 1)
                v.wait_ge(p.sems["n1_sels"], 2 * si + 2)
                v.tensor_add(wcol[:, 6:7], wcol[:, 0:1], wcol[:, 5:6])
                v.tensor_scalar_sub(wcol[:, 6:7], wcol[:, 6:7], 23.02585093)
                v.tensor_scalar(m1t[:], imp[:, tc * 129 + 1:tc * 129 + 129],
                                v8[:, 3:4], None, ALU.is_ge)
                v.tensor_scalar(m2t[:], imp[:, tc * 129 + 1:tc * 129 + 129],
                                wcol[:, 6:7], None, ALU.is_gt)
                v.tensor_mul(selm[:, tc * 128:(tc + 1) * 128], m1t[:], m2t[:])
                p.inc(v.engine_nop(), "n1_selv", 1)
            for g in range(4):
                hg = 4 * K + g
                def emit_ptc(qr, tc):
                    qi = (4 * K + g) * 4 + qr
                    nb = FT_NB[(K, g, tc)]
                    for b in range(nb):
                        bti = FT_CUM[(K, g, tc)] - nb + b + 1
                        v.wait_ge(p.sems["n1_ftr"], bti)
                        if qi >= 1:
                            v.wait_ge(p.sems["n1_omm"], qi)
                        ns_ = min(8, tc + 1 - b * 8)
                        v.tensor_copy(
                            ap_(PT, (b * 8) * 512 + (tc % 4) * 128,
                                [[8192, 128], [512, ns_], [1, 128]]),
                            ap_(ptr, 0, [[1024, 128], [128, ns_], [1, 128]]))
                        p.inc(v.engine_nop(), "n1_ftv", 1)
                for qr in range(4):
                    qi = hg * 4 + qr
                    for t4 in range(4):
                        tc = qr * 4 + t4
                        fi = f_i(K, g, tc)
                        v.wait_ge(p.sems["n1_fex"], fi)
                        v.wait_ge(p.sems["n1_selv"], 3 * (16 * K + tc) + 3)
                        if fi >= 3:
                            v.wait_ge(p.sems["n1_ftr"],
                                      FT_CUM[prev_tile(*prev_tile(K, g, tc))])
                        Pfs = Pf[tc % 2]
                        v.tensor_mul(Pfs[:, 0:(tc + 1) * 128],
                                     S_f[:, 0:(tc + 1) * 128],
                                     ap_(selm, tc * 128, [[2048, 128], [1, (tc + 1) * 8], [0, 16]]))
                        v.tensor_mul(m2t[:], S_f[:, tc * 128:(tc + 1) * 128], fownS[:])
                        v.tensor_add(Pfs[:, tc * 128:(tc + 1) * 128],
                                     Pfs[:, tc * 128:(tc + 1) * 128], m2t[:])
                        v.reduce_sum(fcol[:, 0:1], Pfs[:, 0:(tc + 1) * 128], axis=AX.X)
                        v.reciprocal(fcol[:, 1:2], fcol[:, 0:1])
                        v.tensor_mul(fcol[:, 1:2], fcol[:, 1:2],
                                     gtok[:, tc * 128 + 3 * hg + 1:tc * 128 + 3 * hg + 2])
                        v.tensor_scalar(Pfs[:, 0:(tc + 1) * 128],
                                        Pfs[:, 0:(tc + 1) * 128],
                                        fcol[:, 1:2], None, ALU.mult)
                        if tc == 0:
                            v.tensor_mul(ssb[:, 0:128], S_f[:, 0:128], sbF0[:])
                            r0, r1 = 0, 128
                            pf0 = 0
                        else:
                            v.tensor_mul(ssb[:, 0:256],
                                         S_f[:, (tc - 1) * 128:(tc + 1) * 128], sbF[:])
                            r0, r1 = 0, 256
                            pf0 = (tc - 1) * 128
                        v.reduce_sum(gcol[:, 0:1], ssb[:, r0:r1], axis=AX.X)
                        v.reciprocal(gcol[:, 1:2], gcol[:, 0:1])
                        v.tensor_mul(gcol[:, 1:2], gcol[:, 1:2],
                                     gtok[:, tc * 128 + 3 * hg + 2:tc * 128 + 3 * hg + 3])
                        v.tensor_scalar(ssb[:, r0:r1], ssb[:, r0:r1],
                                        gcol[:, 1:2], None, ALU.mult)
                        v.tensor_add(Pfs[:, pf0:pf0 + (r1 - r0)],
                                     Pfs[:, pf0:pf0 + (r1 - r0)], ssb[:, r0:r1])
                        p.inc(v.engine_nop(), "n1_pv", 1)
                        if t4 >= 1:
                            emit_ptc(qr, tc - 1)
                    emit_ptc(qr, qr * 4 + 3)
                    v.wait_ge(p.sems["n1_omm"], qi + 1)
                    if qi >= 2:
                        v.wait_ge(p.sems["n1_gst"], 16 * (qi - 1))
                    v.tensor_copy(ogat[qi % 2][:], pO[qi % 2][:])
                    p.inc(v.engine_nop(), "n1_oev", 1)

    @block.gpsimd
    def _(g_):
        g_ = p.wrap(g_, "g")
        for K in range(8):
            for g in range(4):
                hg = 4 * K + g
                g_.wait_ge(p.sems["n1_qld"], 16 * (hg + 1))
                if hg >= 2:
                    g_.wait_ge(p.sems["n1_fmm"], (hg - 1) * 16)
                qhi = ap_(q1[hg % 2], 0, [[T, 64], [1, T]])
                qlo = ap_(q1[hg % 2], 64 * T, [[T, 64], [1, T]])
                rhi = ap_(rq1[hg % 2], 0, [[T, 64], [1, T]])
                rlo = ap_(rq1[hg % 2], 64 * T, [[T, 64], [1, T]])
                cos0 = ap_(rcosf, 0, [[T, 64], [1, T]])
                cos1 = ap_(rcosf, 64 * T, [[T, 64], [1, T]])
                sin0 = ap_(rsinf, 0, [[T, 64], [1, T]])
                sin1 = ap_(rsinf, 64 * T, [[T, 64], [1, T]])
                tmp0 = ap_(rtmp, 0, [[T, 64], [1, T]])
                tmp1 = ap_(rtmp, 64 * T, [[T, 64], [1, T]])
                g_.tensor_mul(rhi, qhi, cos0)
                g_.tensor_mul(tmp0, qlo, sin1)
                g_.tensor_sub(rhi, rhi, tmp0)
                g_.tensor_mul(rlo, qlo, cos1)
                g_.tensor_mul(tmp1, qhi, sin0)
                g_.tensor_add(rlo, rlo, tmp1)
                p.inc(g_.engine_nop(), "n1_rp", 1)
    na.close()



def emit_nsa_out(p, d, res_old_idx, res_new_idx):
    block = p.block
    ms = p.scope()
    gatb = p.sbuf("no_gat", [128, 32 * 1024], BF16, scope=ms)
    wob = [p.sbuf(f"no_w{i}", [128, 32 * 128], BF16, scope=ms) for i in range(2)]
    rold = [p.sbuf(f"no_r{i}", [128, 1024], BF16, scope=ms) for i in range(2)]
    rnew = [p.sbuf(f"no_n{i}", [128, 1024], BF16, scope=ms) for i in range(2)]
    psE = [p.psum(f"no_ps{i}", [128, 1024], scope=ms) for i in range(2)]
    for n in ("no_gld", "no_wld", "no_rold", "no_mm", "no_ev", "no_st"):
        p.sem(n)

    for th in range(2):
        B = dict(p.cnt)

        @block.sync
        def _(s):
            if th == 1:
                s.wait_ge(p.sems["no_mm"], B["no_mm"])
            for h in range(32):
                p.dma(s, "no_gld", out=gatb[:, h * 1024:(h + 1) * 1024],
                      in_=ap_(d["gat_sp"], h * 128 * T + th * 1024, [[T, 128], [1, 1024]]))

        def _rold(g, o):
            if o >= 2:
                g.wait_ge(p.sems["no_ev"], B["no_ev"] + o - 1)
            elif B["no_ev"]:
                g.wait_ge(p.sems["no_ev"], B["no_ev"])
            src_ = ap_(d["res_sp"],
                       res_old_idx * DC * 128 * T + o * 128 * T + th * 1024,
                       [[T, 128], [1, 1024]])
            p.dma(g, "no_rold", out=rold[o % 2][:], in_=src_)

        @block.gpsimd
        def _(g):
            g = p.wrap(g, "g")
            if B["no_mm"]:
                g.wait_ge(p.sems["no_mm"], B["no_mm"])
            for o in range(32):
                if o >= 2:
                    g.wait_ge(p.sems["no_mm"], B["no_mm"] + o - 1)
                p.dma(g, "no_wld",
                      out=ap_(wob[o % 2], 0, [[32 * 128, 128], [128, 32], [1, 128]]),
                      in_=d["w_o"][o].rearrange("a p q -> p a q"))
                if o >= 2:
                    _rold(g, o - 2)
            _rold(g, 30)
            _rold(g, 31)

        @block.tensor
        def _(t):
            t.wait_ge(p.sems["no_gld"], B["no_gld"] + 16 * 32)
            if B["no_ev"]:
                t.wait_ge(p.sems["no_ev"], B["no_ev"])
            for o in range(32):
                t.wait_ge(p.sems["no_wld"], B["no_wld"] + 16 * (o + 1))
                if o >= 2:
                    t.wait_ge(p.sems["no_ev"], B["no_ev"] + o - 1)
                for k in range(32):
                    lt = wob[o % 2][:, k * 128:(k + 1) * 128]
                    for j in range(2):
                        mi = t.matmul(ap_(psE[o % 2], j * 512, [[1024, 128], [1, 512]]),
                                      lt, gatb[:, k * 1024 + j * 512:k * 1024 + (j + 1) * 512],
                                      start=(k == 0), stop=(k == 31))
                p.inc(mi, "no_mm", 1)

        @block.vector
        def _(v):
            v = p.wrap(v, "v")
            if B["no_st"]:
                v.wait_ge(p.sems["no_st"], B["no_st"])
            for o in range(32):
                v.wait_ge(p.sems["no_mm"], B["no_mm"] + o + 1)
                v.wait_ge(p.sems["no_rold"], B["no_rold"] + 16 * (o + 1))
                if o >= 2:
                    v.wait_ge(p.sems["no_st"], B["no_st"] + 16 * (o - 1))
                v.tensor_add(rnew[o % 2][:], psE[o % 2][:], rold[o % 2][:])
                p.inc(v.engine_nop(), "no_ev", 1)

        @block.sync
        def _(s):
            for o in range(32):
                s.wait_ge(p.sems["no_ev"], B["no_ev"] + o + 1)
                p.dma(s, "no_st",
                      out=ap_(d["res_sp"],
                              res_new_idx * DC * 128 * T + o * 128 * T + th * 1024,
                              [[T, 128], [1, 1024]]),
                      in_=rnew[o % 2][:])
            s.wait_ge(p.sems["no_st"], B["no_st"] + 16 * 32)
    ms.close()


# version 02
# speedup vs baseline: 1.0137x; 1.0137x over previous
import numpy as np
import ml_dtypes
from contextlib import ExitStack
import concourse.bass as bass
import concourse.mybir as mybir
from concourse.bass_utils import run_bass_kernel_spmd

F32 = mybir.dt.float32
BF16 = mybir.dt.bfloat16
AF = mybir.ActivationFunctionType
ALU = mybir.AluOpType
AX = mybir.AxisListType

T = 2048
D = 4096
DC = 32
NH = 64
DSSM = 8192
NCH = 16
NEGBIG = -1e30
BF = ml_dtypes.bfloat16
ENGS = ("sync", "gpsimd", "vector", "scalar", "tensor")


def ap_(t, off, dims):
    if isinstance(t, bass.AP):
        return bass.AP(t.tensor, t.offset + off, [list(x) for x in dims])
    return bass.AP(t, off, [list(x) for x in dims])


class Prog:
    def __init__(self, nc, ctx, block):
        self.nc = nc
        self.ctx = ctx
        self.block = block
        self.sems = {}
        self.cnt = {}
        self.R = {}
        self._scopes = []
        self.right = ExitStack()
        self.mam_res = ExitStack()
        self.lastop = {}
        self.marks = {}
        self.nbar = 0

    def sem(self, name):
        if name in self.sems:
            return self.sems[name]
        h = self.nc.alloc_semaphore(name=name)
        self.sems[name] = h
        self.cnt[name] = 0
        return h

    def phase_end(self, prefixes=()):
        self.barrier()
        names = [n for n in list(self.sems)
                 if any(n.startswith(pre) for pre in prefixes)]
        if names:
            handles = [self.sems[n] for n in names]
            blk = self.block
            nc = self.nc

            @blk.gpsimd
            def _(g):
                nc.clear_and_free_semaphores(handles)
            for n in names:
                del self.sems[n]
                del self.cnt[n]
            self.barrier()

    def inc(self, instr, name, n=1):
        if name not in self.sems:
            self.sem(name)
        instr.then_inc(self.sems[name], n)
        self.cnt[name] += n
        self.marks[id(instr)] = (instr, name, self.cnt[name])
        return self.cnt[name]

    def wait(self, eng, name, val=None):
        eng.wait_ge(self.sems[name], self.cnt[name] if val is None else val)

    OPNAMES = ("tensor_copy", "tensor_tensor", "tensor_add", "tensor_sub",
               "tensor_mul", "tensor_max", "tensor_scalar", "tensor_single_scalar",
               "tensor_scalar_add", "tensor_scalar_sub", "tensor_scalar_mul",
               "tensor_scalar_max", "tensor_scalar_min", "tensor_relu",
               "tensor_reduce", "reduce_max", "reduce_sum", "scalar_tensor_tensor",
               "tensor_tensor_scan", "memset", "memzero", "reciprocal", "select",
               "copy_predicated", "transpose", "bn_stats", "bn_aggr", "max",
               "max_index", "max_with_indices", "pool", "pool_avg", "pool_max",
               "stream_shuffle", "match_replace", "activation", "sign", "square",
               "sqrt", "mul", "add", "iota", "affine_select",
               "partition_broadcast", "engine_nop")

    def wrap(self, eng, tag):
        return _EngWrap(self, eng, tag)

    def dma(self, q, sem, **kw):
        if sem not in self.sems:
            self.sem(sem)
        if self.cnt[sem]:
            q.wait_ge(self.sems[sem], self.cnt[sem])
        return self.inc(q.dma_start(**kw), sem, 16)

    def sbuf(self, name, shape, dtype, scope=None, side=None):
        cm = self.nc.sbuf_tensor(name, shape, dtype, side=side)
        t = (scope or self.ctx).enter_context(cm)
        self.R[name] = t
        return t

    def psum(self, name, shape, dtype=F32, scope=None):
        cm = self.nc.psum_tensor(name, shape, dtype)
        t = (scope or self.ctx).enter_context(cm)
        self.R[name] = t
        return t

    def scope(self):
        s = ExitStack()
        self._scopes.append(s)
        return s

    def barrier(self):
        self.lastop = {}
        self.nbar += 1
        name = "barrier"
        if name not in self.sems:
            self.sem(name)
        tgt = self.nbar * 5
        blk = self.block

        @blk.sync
        def _(s):
            self.inc(s.nop(), name, 1)
            s.wait_ge(self.sems[name], tgt)

        @blk.gpsimd
        def _(g):
            self.inc(g.engine_nop(), name, 1)
            g.wait_ge(self.sems[name], tgt)

        @blk.vector
        def _(v):
            self.inc(v.engine_nop(), name, 1)
            v.wait_ge(self.sems[name], tgt)

        @blk.scalar
        def _(s):
            self.inc(s.nop(), name, 1)
            s.wait_ge(self.sems[name], tgt)

        @blk.tensor
        def _(t):
            self.inc(t.nop(), name, 1)
            t.wait_ge(self.sems[name], tgt)


class _EngWrap:

    def __init__(self, p, eng, tag):
        self._p = p
        self._e = eng
        self._s = "se_" + tag
        if self._s not in p.sems:
            p.sem(self._s)

    def _chain_wait(self):
        p, e, s = self._p, self._e, self._s
        li = p.lastop.get(self._s)
        if li is None:
            return
        mark = p.marks.get(id(li))
        if mark is None:
            li.then_inc(p.sems[s], 1)
            p.cnt[s] += 1
            mark = (li, s, p.cnt[s])
            p.marks[id(li)] = mark
        e.wait_ge(p.sems[mark[1]], mark[2])

    def __getattr__(self, name):
        e = self._e
        a = getattr(e, name)
        if name in Prog.OPNAMES:
            p = self._p

            def f(*args, **kw):
                self._chain_wait()
                instr = a(*args, **kw)
                p.lastop[self._s] = instr
                return instr
            return f
        if name == "dma_start":
            def f(*args, **kw):
                self._chain_wait()
                return a(*args, **kw)
            return f
        return a



def _pack_lhsT(w):
    K, O = w.shape
    nK = (K + 127) // 128
    nO = (O + 127) // 128
    wp = np.zeros((nK * 128, nO * 128), dtype=np.float32)
    wp[:K, :O] = w
    return np.ascontiguousarray(wp.reshape(nK, 128, nO, 128).transpose(2, 0, 1, 3))


def _rope_perm():
    return np.concatenate([np.arange(0, 128, 2), np.arange(1, 128, 2)])


def prep_consts():
    c = {}
    i = np.arange(128)
    c["mk_mask"] = np.where(i[:, None] <= i[None, :], 0.0, NEGBIG).astype(np.float32)
    c["ident"] = np.eye(128, dtype=np.float32)
    tg = np.arange(T).reshape(16, 128)
    j = np.arange(129)
    allow = (j[None, None, :] == 0) | (tg[:, :, None] >= 16 * j[None, None, :])
    c["cmask"] = np.where(allow, 0.0, NEGBIG).astype(np.float32)
    tl = np.arange(128)
    jl = np.arange(128)
    own = (jl[None, :] // 16 == tl[:, None] // 16) & (jl[None, :] % 16 <= tl[:, None] % 16)
    c["fown"] = own.astype(np.float32)
    c["iotajb"] = np.broadcast_to(np.arange(128, dtype=np.float32), (128, 128)).copy()
    tl = np.arange(128)
    j2 = np.arange(256)
    bf = (j2[None, :] >= tl[:, None] + 64) & (j2[None, :] <= tl[:, None] + 128)
    c["sbandF"] = bf.astype(np.float32)
    j1 = np.arange(128)
    bf0 = (j1[None, :] >= tl[:, None] - 64) & (j1[None, :] <= tl[:, None])
    c["sbandF0"] = bf0.astype(np.float32)
    inv = 1.0 / 10000.0 ** (np.arange(0, 128, 2, dtype=np.float64) / 128.0)
    ang = np.arange(T, dtype=np.float64)[:, None] * inv[None, :]
    c["rcos"] = np.ascontiguousarray(np.cos(ang).T.astype(np.float32))
    c["rsin"] = np.ascontiguousarray(np.sin(ang).T.astype(np.float32))
    return c


def prep_mamba(inp):
    w = {}
    w["w_in"] = _pack_lhsT(np.asarray(inp["L0_attn_a_W_in"], np.float32))
    cw = np.asarray(inp["L0_attn_a_conv_w"], np.float32)
    w["conv_w"] = np.ascontiguousarray(cw.reshape(66, 128, 4).transpose(1, 0, 2).reshape(128, 264))
    w["dt_bias"] = np.asarray(inp["L0_attn_a_dt_bias"], np.float32).reshape(64, 1)
    w["a_log"] = np.asarray(inp["L0_attn_a_A_log"], np.float32).reshape(64, 1)
    w["d_param"] = np.asarray(inp["L0_attn_a_D_param"], np.float32).reshape(1, 64)
    w["norm_w"] = np.ascontiguousarray(
        np.asarray(inp["L0_attn_a_norm_weight"], np.float32).reshape(64, 128).T)
    w["w_out"] = _pack_lhsT(np.asarray(inp["L0_attn_a_W_out"], np.float32))
    return w


def prep_ffn(inp):
    w = {}
    w["ffn_w"] = _pack_lhsT(np.asarray(inp["L0_ffn_W"], np.float32).T)
    w["ffn_b"] = np.ascontiguousarray(np.asarray(inp["L0_ffn_b"], np.float32).reshape(32, 128).T)
    A = np.asarray(inp["L0_ffn_A"], np.float32).reshape(128, 4096)
    w["ffn_a"] = _pack_lhsT(A.T)
    w["ffn_bmat"] = np.ascontiguousarray(
        np.asarray(inp["L0_ffn_B"], np.float32).transpose(0, 2, 1).reshape(128, 4096))
    w["ffn_rw"] = np.ascontiguousarray(
        np.asarray(inp["L0_ffn_router_W"], np.float32).T.reshape(32, 128, 8))
    w["ffn_rb"] = np.asarray(inp["L0_ffn_router_b"], np.float32).reshape(8, 1)
    return w


def prep_nsa(inp):
    w = {}
    perm = _rope_perm()
    wqkv = np.asarray(inp["L0_attn_b_w_qkv"], np.float32).copy()
    for h in range(32):
        wqkv[h * 128:(h + 1) * 128] = wqkv[h * 128 + perm]
    for h in range(8):
        base = 4096 + h * 128
        wqkv[base:base + 128] = wqkv[base + perm]
    wqkv[:4096] *= 128.0 ** -0.5
    w["w_qkv"] = _pack_lhsT(wqkv.T)
    w["w_o"] = _pack_lhsT(np.asarray(inp["L0_attn_b_w_o"], np.float32).T)
    kp = np.asarray(inp["L0_attn_b_k_pos"], np.float32)[:, :, perm]
    w["k_pos"] = np.ascontiguousarray(kp.transpose(2, 0, 1).reshape(128, 128))
    vp = np.asarray(inp["L0_attn_b_v_pos"], np.float32)
    w["v_pos"] = np.ascontiguousarray(vp.transpose(2, 0, 1).reshape(128, 128))
    iperm = (np.arange(2048).reshape(16, 128)[:, perm]).reshape(2048)
    w["k_w1"] = _pack_lhsT(np.asarray(inp["L0_attn_b_k_w1"], np.float32)[:, iperm].T)
    w["k_b1"] = np.ascontiguousarray(np.asarray(inp["L0_attn_b_k_b1"], np.float32).reshape(16, 128).T)
    w["k_w2"] = _pack_lhsT(np.asarray(inp["L0_attn_b_k_w2"], np.float32)[perm].T)
    w["k_b2"] = np.asarray(inp["L0_attn_b_k_b2"], np.float32)[perm].reshape(128, 1)
    w["v_w1"] = _pack_lhsT(np.asarray(inp["L0_attn_b_v_w1"], np.float32).T)
    w["v_b1"] = np.ascontiguousarray(np.asarray(inp["L0_attn_b_v_b1"], np.float32).reshape(16, 128).T)
    w["v_w2"] = _pack_lhsT(np.asarray(inp["L0_attn_b_v_w2"], np.float32).T)
    w["v_b2"] = np.asarray(inp["L0_attn_b_v_b2"], np.float32).reshape(128, 1)
    mem = np.asarray(inp["L0_attn_b_mem_kv"], np.float32)
    w["mem_k"] = np.ascontiguousarray(mem[0, :, 0, :][:, perm].T)
    w["mem_v"] = np.ascontiguousarray(mem[1, :, 0, :].T)
    w["w_gate"] = _pack_lhsT(np.asarray(inp["L0_attn_b_w_gate"], np.float32).T)
    w["b_gate"] = np.concatenate(
        [np.asarray(inp["L0_attn_b_b_gate"], np.float32), np.zeros(32, np.float32)]).reshape(128, 1)
    return w



def declare_dram(nc, parts):
    d = {}

    def inp(name, shape):
        d[name] = nc.dram_tensor(name, list(shape), F32, kind="ExternalInput").ap()

    def scr(name, shape, dtype=BF16):
        d[name] = nc.dram_tensor(name, list(shape), dtype, kind="Internal").ap()

    inp("x0", (D, T))
    inp("ident", (128, 128))
    if "mamba" in parts:
        inp("w_in", (131, 32, 128, 128)); inp("conv_w", (128, 264))
        inp("dt_bias", (64, 1)); inp("a_log", (64, 1)); inp("d_param", (1, 64))
        inp("norm_w", (128, 64)); inp("w_out", (32, 64, 128, 128))
        inp("mk_mask", (128, 128))
        scr("z_sp", (4, 64, 128, 512))
        scr("xact_sp", (16, DSSM, 128))
        scr("y_sp", (16, 64, 128, 128))
        scr("cum_sp", (2, 7, 64, 128))
        scr("dbg_sp", (64, T), F32)
    if "ffn" in parts:
        inp("ffn_w", (32, 32, 128, 128)); inp("ffn_b", (128, 32)); inp("ffn_a", (1, 32, 128, 128))
        inp("ffn_bmat", (128, 4096)); inp("ffn_rw", (32, 128, 8)); inp("ffn_rb", (8, 1))
        scr("r8", (1, 8), F32)
    if "nsa" in parts:
        inp("w_qkv", (48, 32, 128, 128)); inp("w_o", (32, 32, 128, 128))
        inp("k_pos", (128, 128)); inp("v_pos", (128, 128))
        inp("k_w1", (16, 16, 128, 128)); inp("k_b1", (128, 16))
        inp("k_w2", (1, 16, 128, 128)); inp("k_b2", (128, 1))
        inp("v_w1", (16, 16, 128, 128)); inp("v_b1", (128, 16))
        inp("v_w2", (1, 16, 128, 128)); inp("v_b2", (128, 1))
        inp("mem_k", (128, 8)); inp("mem_v", (128, 8))
        inp("w_gate", (1, 32, 128, 128)); inp("b_gate", (128, 1))
        inp("cmask", (16, 128, 129)); inp("fown", (128, 128)); inp("iotajb", (128, 128))
        inp("sbandF", (128, 256)); inp("sbandF0", (128, 128))
        inp("rcos", (64, T)); inp("rsin", (64, T))
        scr("q_sp", (32, 128, T))
        scr("k_sp", (8, 128, T))
        scr("v_sp", (8, 128, T))
        scr("gat_sp", (32, 128, T))
    scr("res_sp", (2, DC, 128, T))
    d["out"] = nc.dram_tensor("out", [D, T], F32, kind="ExternalOutput").ap()
    return d



def emit_init(p, d):
    block = p.block
    identf = p.sbuf("identf", [128, 128], F32, scope=p.right, side="right")
    identb = p.sbuf("identb", [128, 128], BF16, scope=p.right, side="right")
    p.sbuf("onesrowf", [1, 128], F32, scope=p.right, side="right")
    p.sbuf("onesrowb", [1, 128], BF16, scope=p.right, side="right")
    p.sbuf("onescolb", [128, 1], BF16, scope=p.right, side="right")
    p.sem("ini_s"); p.sem("ini_g"); p.sem("ini")
    if "norm_w" in d:
        nw = p.sbuf("m3_nw", [128, 64], F32, scope=p.right, side="right")

        @block.sync
        def _(s):
            p.dma(s, "ini_s", out=nw[:], in_=d["norm_w"])

    inis = p.cnt["ini_s"]

    @block.sync
    def _(s):
        p.dma(s, "ini_s", out=identf[:], in_=d["ident"])

    @block.gpsimd
    def _(g):
        g = p.wrap(g, "g")
        g.wait_ge(p.sems["ini_s"], inis + 16)
        p.dma(g, "ini_g", out=identb[:], in_=d["ident"])

    @block.vector
    def _(v):
        v = p.wrap(v, "v")
        v.memset(p.R["onesrowf"][:], 1.0)
        v.memset(p.R["onesrowb"][:], 1.0)
        v.memset(p.R["onescolb"][:], 1.0)
        p.inc(v.engine_nop(), "ini", 1)



def emit_boundary(p, d, idx, src, scope):
    import os
    BSTEP = int(os.environ.get("BSTEP", "9"))
    block = p.block
    nc = p.nc
    xln = p.sbuf(f"xln{idx}", [128, DC * T], BF16, scope=scope)
    bs = p.scope()
    x0a = p.sbuf(f"bn{idx}_x0a", [128, T], F32, scope=bs)
    x0b_ = p.sbuf(f"bn{idx}_x0b", [128, T], F32, scope=bs)
    sq = p.sbuf(f"bn{idx}_sq", [128, T], BF16, scope=bs)
    rows = p.sbuf(f"bn{idx}_rows", [1, 3 * T], F32, scope=bs)
    mub = p.sbuf(f"bn{idx}_mub", [128, T], BF16, scope=bs)
    rstdb = p.sbuf(f"bn{idx}_rstdb", [128, T], BF16, scope=bs)
    pstat = p.psum(f"bn{idx}_pstat", [128, 4096], scope=bs)
    ld = f"bn{idx}_ld"; cast = f"bn{idx}_cast"; mm = f"bn{idx}_mm"
    sqs = f"bn{idx}_sq"; fin = f"bn{idx}_fin"; fins = f"bn{idx}_fins"
    bct = f"bn{idx}_bct"; bcs = f"bn{idx}_bcs"; nrm = f"bn{idx}_nrm"
    for s_ in (ld, cast, mm, sqs, fin, fins, bct, bcs, nrm):
        p.sem(s_)

    if src[0] == "x0":
        @block.sync
        def _(s):
            for c in range(DC):
                if c >= 2:
                    p.wait(s, cast, 16 * (c - 1))
                p.dma(s, ld, out=(x0a if c % 2 == 0 else x0b_)[:],
                                  in_=d["x0"][c * 128:(c + 1) * 128, :])

        @block.gpsimd
        def _(g):
            g = p.wrap(g, "g")
            for c in range(DC):
                p.wait(g, ld, 16 * (c + 1))
                p.dma(g, cast, out=ap_(xln, c * T, [[DC * T, 128], [1, T]]),
                                  in_=(x0a if c % 2 == 0 else x0b_)[:])
    else:
        ri = src[1]

        @block.sync
        def _(s):
            for c in range(DC):
                p.dma(s, cast, out=ap_(xln, c * T, [[DC * T, 128], [1, T]]),
                                  in_=d["res_sp"][ri, c])

    if BSTEP < 2:
        return xln

    @block.scalar
    def _(s):
        s = p.wrap(s, "s")
        for c in range(DC):
            p.wait(s, cast, 16 * (c + 1))
            if c >= 1:
                p.wait(s, mm, 2 * c)
            p.inc(s.activation(sq[:], ap_(xln, c * T, [[DC * T, 128], [1, T]]), AF.Square),
                  sqs, 1)

    @block.tensor
    def _(t):
        t.wait_ge(p.sems["ini"], 1)
        for c in range(DC):
            p.wait(t, cast, 16 * (c + 1))
            for j in range(4):
                mi = t.matmul(ap_(pstat, j * 512, [[4096, 1], [1, 512]]),
                              p.R["onescolb"][:],
                              ap_(xln, c * T + j * 512, [[DC * T, 128], [1, 512]]),
                              start=(c == 0), stop=(c == DC - 1))
            p.inc(mi, mm, 1)
            p.wait(t, sqs, c + 1)
            for j in range(4):
                mi = t.matmul(ap_(pstat, 2048 + j * 512, [[4096, 1], [1, 512]]),
                              p.R["onescolb"][:], sq[:, j * 512:(j + 1) * 512],
                              start=(c == 0), stop=(c == DC - 1))
            p.inc(mi, mm, 1)

    if BSTEP < 3:
        return xln
    r_mu = rows[:, 0:T]
    r_v = rows[:, T:2 * T]
    r_t = rows[:, 2 * T:3 * T]

    @block.vector
    def _(v):
        v = p.wrap(v, "v")
        p.wait(v, mm, 2 * DC)
        v.tensor_copy(r_mu, ap_(pstat, 0, [[4096, 1], [1, T]]))
        v.tensor_copy(r_v, ap_(pstat, 2048, [[4096, 1], [1, T]]))
        v.tensor_scalar_mul(r_mu, r_mu, 1.0 / D)
        v.tensor_scalar_mul(r_v, r_v, 1.0 / D)
        v.tensor_mul(r_t, r_mu, r_mu)
        v.tensor_sub(r_v, r_v, r_t)
        v.tensor_scalar_add(r_v, r_v, 1e-5)
        p.inc(v.engine_nop(), fin, 1)

    @block.scalar
    def _(s):
        s = p.wrap(s, "s")
        p.wait(s, fin, 1)
        p.inc(s.activation(r_v, r_v, AF.Sqrt), fins, 1)

    @block.vector
    def _(v):
        v = p.wrap(v, "v")
        p.wait(v, fins, 1)
        v.reciprocal(r_v, r_v)
        v.tensor_mul(r_t, r_mu, r_v)
        p.inc(v.engine_nop(), fin, 1)

    if BSTEP < 4:
        return xln
    @block.tensor
    def _(t):
        p.wait(t, fin, 2)
        for j in range(4):
            t.matmul(ap_(pstat, j * 512, [[4096, 128], [1, 512]]),
                     p.R["onesrowf"][:], r_v[:, j * 512:(j + 1) * 512], start=True, stop=True)
            mi = t.matmul(ap_(pstat, 2048 + j * 512, [[4096, 128], [1, 512]]),
                          p.R["onesrowf"][:], r_t[:, j * 512:(j + 1) * 512],
                          start=True, stop=True)
        p.inc(mi, bct, 1)

    @block.scalar
    def _(s):
        s = p.wrap(s, "s")
        p.wait(s, bct, 1)
        s.copy(rstdb[:], ap_(pstat, 0, [[4096, 128], [1, T]]))
        p.inc(s.copy(mub[:], ap_(pstat, 2048, [[4096, 128], [1, T]])), bcs, 1)

    if BSTEP < 5:
        return xln
    @block.vector
    def _(v):
        v = p.wrap(v, "v")
        p.wait(v, bcs, 1)
        for c in range(DC):
            xv = ap_(xln, c * T, [[DC * T, 128], [1, T]])
            v.tensor_mul(xv, xv, rstdb[:])
            v.tensor_sub(xv, xv, mub[:])
        p.inc(v.engine_nop(), nrm, 1)

    bs.close()
    return xln



def emit_mamba_m1(p, d, xln):
    block = p.block
    ms = p.scope()
    wbuf = [p.sbuf(f"m1_w{i}", [128, 32 * 128], BF16, scope=ms) for i in range(2)]
    zrow = [p.sbuf(f"m1_z{i}", [128, T], BF16, scope=ms) for i in range(2)]
    xrow = [p.sbuf(f"m1_x{i}", [128, T], BF16, scope=ms) for i in range(2)]
    crow = [p.sbuf(f"m1_c{i}", [128, T], F32, scope=ms) for i in range(2)]
    convw = p.sbuf("m1_convw", [128, 264], F32, scope=ms)
    ps = [p.psum(f"m1_ps{i}", [128, 2048], scope=ms) for i in range(2)]
    Bfm = p.sbuf("Bfm", [128, T], BF16, scope=p.mam_res, side="right")
    Cfm = p.sbuf("Cfm", [128, T], BF16, scope=p.mam_res, side="right")
    dtraw = p.sbuf("dtraw", [64, T], F32, scope=p.mam_res, side="right")
    for s_ in ("m1_wld", "m1_mm", "m1_ev", "m1_evd", "m1_cv", "m1_sl", "m1_st",
               "m1_cwld"):
        p.sem(s_)

    NO = 131
    st_after = []
    tot = 0
    for u in range(128):
        tot += 64 if u < 64 else 16
        st_after.append(tot)

    @block.sync
    def _(s):
        p.dma(s, "m1_cwld", out=convw[:], in_=d["conv_w"])

    @block.gpsimd
    def _(g):
        g = p.wrap(g, "g")
        for o in range(NO):
            if o >= 2:
                g.wait_ge(p.sems["m1_mm"], o - 1)
            p.dma(g, "m1_wld", 
                out=ap_(wbuf[o % 2], 0, [[4096, 128], [128, 32], [1, 128]]),
                in_=d["w_in"][o].rearrange("a p q -> p a q"))

    @block.tensor
    def _(t):
        for o in range(NO):
            t.wait_ge(p.sems["m1_wld"], 16 * (o + 1))
            if o >= 2:
                t.wait_ge(p.sems["m1_ev"], o - 1)
            w = wbuf[o % 2]
            pso = ps[o % 2]
            for dd in range(32):
                lt = w[:, dd * 128:(dd + 1) * 128]
                for j in range(4):
                    mi = t.matmul(ap_(pso, j * 512, [[2048, 128], [1, 512]]),
                                  lt, ap_(xln, dd * T + j * 512, [[DC * T, 128], [1, 512]]),
                                  start=(dd == 0), stop=(dd == 31))
            p.inc(mi, "m1_mm", 1)

    @block.scalar
    def _(s):
        s = p.wrap(s, "s")
        for o in range(NO):
            if o < 130:
                s.wait_ge(p.sems["m1_mm"], o + 1)
                pso = ps[o % 2]
                if o < 64:
                    u = o
                    if u >= 2:
                        s.wait_ge(p.sems["m1_st"], st_after[u - 2])
                    p.inc(s.copy(zrow[u % 2][:], pso[:]), "m1_ev", 1)
                else:
                    nx = o - 64
                    if nx >= 2:
                        s.wait_ge(p.sems["m1_cv"], nx - 1)
                    p.inc(s.copy(xrow[nx % 2][:], pso[:]), "m1_ev", 1)
            cc = o - 65
            if 0 <= cc < 66:
                s.wait_ge(p.sems["m1_cv"], cc + 1)
                if cc < 64:
                    u = 64 + cc
                    if u >= 2:
                        s.wait_ge(p.sems["m1_st"], st_after[u - 2])
                    dst = zrow[u % 2]
                elif cc == 64:
                    dst = Bfm
                else:
                    dst = Cfm
                p.inc(s.activation(dst[:], crow[cc % 2][:], AF.Silu), "m1_sl", 1)

    @block.vector
    def _(v):
        v = p.wrap(v, "v")
        v.wait_ge(p.sems["m1_cwld"], 16)
        for o in range(64, NO):
            if o < 130:
                nx = o - 64
                cc = nx
                v.wait_ge(p.sems["m1_ev"], o + 1)
                if nx >= 2:
                    v.wait_ge(p.sems["m1_sl"], nx - 1)
                xr = xrow[nx % 2]
                cr = crow[nx % 2]
                v.tensor_scalar(cr[:], xr[:], convw[:, cc * 4 + 3:cc * 4 + 4], None, ALU.mult)
                for k_ in (2, 1, 0):
                    sh = 3 - k_
                    v.scalar_tensor_tensor(cr[:, sh:T], xr[:, 0:T - sh],
                                           convw[:, cc * 4 + k_:cc * 4 + k_ + 1],
                                           cr[:, sh:T], ALU.mult, ALU.add)
                p.inc(v.engine_nop(), "m1_cv", 1)
            else:
                v.wait_ge(p.sems["m1_mm"], NO)
                v.tensor_copy(dtraw[:], ap_(ps[o % 2], 0, [[2048, 64], [1, T]]))
                p.inc(v.engine_nop(), "m1_evd", 1)

    @block.sync
    def _(s):
        for o in range(NO):
            if o < 64:
                u = o
                s.wait_ge(p.sems["m1_ev"], o + 1)
                for j in range(4):
                    p.dma(s, "m1_st", out=d["z_sp"][j, o],
                                      in_=zrow[u % 2][:, j * 512:(j + 1) * 512])
            elif o < 128:
                cc = o - 64
                u = 64 + cc
                s.wait_ge(p.sems["m1_sl"], cc + 1)
                p.dma(s, "m1_st", 
                    out=ap_(d["xact_sp"], cc * 128 * 128,
                            [[128, 128], [DSSM * 128, 16], [1, 128]]),
                    in_=ap_(zrow[u % 2], 0, [[T, 128], [128, 16], [1, 128]]))
    ms.close()
    return Bfm, Cfm, dtraw



def emit_mamba_m3pre(p, d):
    block = p.block
    ps_ = p.scope()
    dtsp = p.sbuf("m3_dtsp", [64, T], F32, scope=ps_)
    tmpa = p.sbuf("m3_tmpa", [64, T], F32, scope=ps_)
    tmpb = p.sbuf("m3_tmpb", [64, T], F32, scope=ps_)
    pD1 = p.psum("m3pre_p", [128, 64], scope=ps_)
    dtec = p.sbuf("m3_dtec", [128, T], BF16, scope=p.mam_res, side="right")
    dtspb = dtec[0:64, :]
    ec_hm = dtec[64:128, :]
    p.R["m3_dtspb"] = dtspb
    p.R["m3_echm"] = ec_hm
    cumpair = p.sbuf("m3_cumpair", [128, T], F32, scope=p.mam_res, side="right")
    cum = cumpair[0:64, :]
    negcum = cumpair[64:128, :]
    p.R["m3_cum"] = cum
    p.R["m3_negcum"] = negcum
    cumL = p.sbuf("m3_cumL", [64, 16], F32, scope=p.mam_res, side="right")
    hpk = [p.sbuf(f"m3_hml{i}", [128, T], BF16, scope=p.mam_res, side="right")
           for i in range(3)]
    for i, n in enumerate(("phi", "pmid", "plo", "nhi", "nmid", "nlo")):
        p.R["m3_" + n] = hpk[i % 3][0 if i < 3 else 64:64 if i < 3 else 128, :]
    dbias = p.sbuf("m3_dbias", [64, 1], F32, scope=p.mam_res, side="right")
    alog = p.sbuf("m3_alog", [64, 1], F32, scope=p.mam_res, side="right")
    dprow = p.sbuf("m3_dprow", [1, 64], F32, scope=p.mam_res, side="right")
    D_bc = p.sbuf("m3_Dbc", [128, 64], F32, scope=p.mam_res, side="right")
    mkmask = p.sbuf("mkmask", [128, 128], F32, scope=p.mam_res, side="right")
    for s_ in ("m3p_ld", "m3p_as", "m3p_av", "m3p_bs", "m3p_bv", "m3p_cv",
               "m3p_cs", "m3p_ct"):
        p.sem(s_)
    hml = {n: p.R[f"m3_{n}"] for n in ("phi", "pmid", "plo", "nhi", "nmid", "nlo")}
    del dtec, cumpair

    @block.sync
    def _(s):
        p.dma(s, "m3p_ld", out=dbias[:], in_=d["dt_bias"])
        p.dma(s, "m3p_ld", out=alog[:], in_=d["a_log"])
        p.dma(s, "m3p_ld", out=dprow[:], in_=d["d_param"])
        p.dma(s, "m3p_ld", out=mkmask[:], in_=d["mk_mask"])

    @block.scalar
    def _(s):
        s = p.wrap(s, "s")
        s.wait_ge(p.sems["m3p_ld"], 64)
        s.wait_ge(p.sems["m1_ev"], 130)
        s.wait_ge(p.sems["m1_evd"], 1)
        s.add(dtsp[:], p.R["dtraw"][:], dbias[:])
        p.inc(s.activation(tmpa[:], dtsp[:], AF.Abs), "m3p_as", 1)

    @block.vector
    def _(v):
        v = p.wrap(v, "v")
        v.wait_ge(p.sems["m3p_as"], 1)
        v.tensor_scalar_mul(tmpa[:], tmpa[:], -1.0)
        p.inc(v.engine_nop(), "m3p_av", 1)

    @block.scalar
    def _(s):
        s = p.wrap(s, "s")
        s.wait_ge(p.sems["m3p_av"], 1)
        s.activation(tmpa[:], tmpa[:], AF.Exp)
        s.activation(tmpa[:], tmpa[:], AF.Ln, bias=1.0)
        p.inc(s.activation(alog[:], alog[:], AF.Exp), "m3p_as", 1)

    @block.vector
    def _(v):
        v = p.wrap(v, "v")
        v.wait_ge(p.sems["m3p_as"], 2)
        v.tensor_scalar_max(tmpb[:], dtsp[:], 0.0)
        v.tensor_add(dtsp[:], tmpa[:], tmpb[:])
        v.tensor_copy(dtspb[:], dtsp[:])
        v.tensor_scalar_mul(alog[:], alog[:], -1.0)
        v.tensor_scalar(tmpa[:], dtsp[:], alog[:], None, ALU.mult)
        for c in range(NCH):
            v.tensor_tensor_scan(cum[:, c * 128:(c + 1) * 128],
                                 tmpa[:, c * 128:(c + 1) * 128],
                                 tmpa[:, c * 128:(c + 1) * 128],
                                 0.0, ALU.add, ALU.bypass)
        v.tensor_copy(cum[:], cum[:])
        v.tensor_copy(cumL[:], ap_(cum, 127, [[T, 64], [128, 16]]))
        v.tensor_copy(hml["phi"][:], cum[:])
        v.tensor_copy(tmpa[:], hml["phi"][:])
        v.tensor_sub(tmpb[:], cum[:], tmpa[:])
        v.tensor_copy(hml["pmid"][:], tmpb[:])
        v.tensor_copy(tmpa[:], hml["pmid"][:])
        v.tensor_sub(tmpb[:], tmpb[:], tmpa[:])
        v.tensor_copy(hml["plo"][:], tmpb[:])
        for n in ("hi", "mid", "lo"):
            v.tensor_scalar_mul(hml["n" + n][:], hml["p" + n][:], -1.0)
        v.tensor_scalar_mul(negcum[:], cum[:], -1.0)
        p.inc(v.engine_nop(), "m3p_cv", 1)

    @block.scalar
    def _(s):
        s = p.wrap(s, "s")
        s.wait_ge(p.sems["m3p_cv"], 1)
        p.inc(s.activation(ec_hm[:], cum[:], AF.Exp), "m3p_cs", 1)

    @block.tensor
    def _(t):
        t.wait_ge(p.sems["m3p_cv"], 1)
        t.wait_ge(p.sems["ini"], 1)
        t.wait_ge(p.sems["ini_g"], 16)
        t.wait_ge(p.sems["ini_s"], p.cnt["ini_s"])
        t.wait_ge(p.sems["m3p_ld"], 64)
        p.inc(t.matmul(pD1[:], p.R["onesrowf"][:], dprow[:], start=True, stop=True),
              "m3p_ct", 1)

    @block.vector
    def _(v):
        v = p.wrap(v, "v")
        v.wait_ge(p.sems["m3p_ct"], 1)
        v.tensor_copy(D_bc[:], pD1[:])
        p.inc(v.engine_nop(), "m3p_cv", 1)
    ps_.close()



def emit_mamba_m3(p, d):
    block = p.block
    ms = p.scope()
    GW = 1024
    xtp = [p.sbuf(f"m3_xtp{i}", [128, GW], BF16, scope=ms) for i in range(2)]
    xfm = [p.sbuf(f"m3_xfm{i}", [128, GW], BF16, scope=ms) for i in range(2)]
    dtx = [p.sbuf(f"m3_dtx{i}", [128, GW], BF16, scope=ms) for i in range(2)]
    dtxs = [p.sbuf(f"m3_dtxs{i}", [128, GW], BF16, scope=ms) for i in range(2)]
    Cs = [p.sbuf(f"m3_Cs{i}", [128, GW], BF16, scope=ms) for i in range(2)]
    Sg = [p.sbuf(f"m3_S{i}", [128, GW], BF16, scope=ms) for i in range(2)]
    sfp = [p.sbuf(f"m3_sfp{i}", [128, GW], F32, scope=ms) for i in range(2)]
    xd = [p.sbuf(f"m3_xd{i}", [128, GW], BF16, scope=ms) for i in range(2)]
    ybuf = [p.sbuf(f"m3_yb{i}", [128, GW], BF16, scope=ms) for i in range(2)]
    stgA = p.sbuf("m3_stgA", [6, 8192], BF16, scope=ms)
    stgB = p.sbuf("m3_stgB", [6, 8192], BF16, scope=ms)
    ecrowT = p.sbuf("m3_ecrow", [1, 8192], BF16, scope=ms)
    dtT = [p.sbuf(f"m3_dtT{i}", [128, 64], BF16, scope=ms) for i in range(2)]
    edT = [p.sbuf(f"m3_edT{i}", [128, 64], BF16, scope=ms) for i in range(2)]
    Btp = [p.sbuf(f"m3_Btp{i}", [128, 128], BF16, scope=ms) for i in range(2)]
    Gf = [p.sbuf(f"m3_Gf{i}", [128, 128], F32, scope=ms) for i in range(2)]
    ed_hm = [p.sbuf(f"m3_edhm{i}", [64, 128], BF16, scope=ms) for i in range(2)]
    decayrow = [p.sbuf(f"m3_dr{i}", [1, 64], F32, scope=ms) for i in range(2)]
    decay_bc = [p.sbuf(f"m3_dbc{i}", [128, 64], F32, scope=ms) for i in range(2)]
    expL = [p.sbuf(f"m3_eL{i}", [64, 1], F32, scope=ms) for i in range(2)]
    state = p.sbuf("m3_state", [128, DSSM], F32, scope=ms)
    statebf = p.sbuf("m3_statebf", [128, DSSM], BF16, scope=ms)
    pBig = p.psum("m3_pBig", [128, 2048], scope=ms)
    pD = p.psum("m3_pD", [128, 1024], scope=ms)
    pG = p.psum("m3_pG", [128, 512], scope=ms)
    ptr = p.psum("m3_ptr", [128, 1024], BF16, scope=ms)
    for s_ in ("s3_sb", "s3_pre", "s3_ex", "s3_ld", "s3_yst", "s3_pg", "s3_pv",
               "s3_tg", "s3_tv", "s3_cs", "s3_csv", "s3_gam", "s3_sp", "s3_sS",
               "s3_ymm", "s3_dmm", "s3_ya", "s3_dx", "s3_xd", "s3_dts",
               "s3_smg", "s3_su", "s3_iv", "s3_ig"):
        p.sem(s_)

    cum = p.R["m3_cum"]; cumL = p.R["m3_cumL"]
    hml = {n: p.R[f"m3_{n}"] for n in ("phi", "pmid", "plo", "nhi", "nmid", "nlo")}
    ec_hm = p.R["m3_echm"]
    Bfm = p.R["Bfm"]; Cfm = p.R["Cfm"]
    mkm = p.R["mkmask"]
    D_bc = p.R["m3_Dbc"]
    dtspb = p.R["m3_dtspb"]

    @block.vector
    def _(v):
        v = p.wrap(v, "v")
        v.memset(state[:], 0.0)
        v.memset(statebf[:], 0.0)
        p.inc(v.engine_nop(), "s3_iv", 1)

    @block.gpsimd
    def _(g):
        g = p.wrap(g, "g")
        g.wait_ge(p.sems["m3p_cv"], 2)
        g.wait_ge(p.sems["m3p_cs"], 1)
        g.memset(stgA[0:6, :], 1.0)
        g.memset(stgB[0:6, :], 1.0)
        p.inc(g.engine_nop(), "s3_ig", 1)

    @block.scalar
    def _(s):
        s = p.wrap(s, "s")
        s.wait_ge(p.sems["s3_iv"], 1)
        s.wait_ge(p.sems["s3_ig"], 1)
        names = ["nhi", "nmid", "nlo", "phi", "pmid", "plo"]
        for c in range(NCH):
            sl = c % 2
            if c >= 1:
                s.wait_ge(p.sems["s3_gam"], 8 * c)
                s.wait_ge(p.sems["s3_cs"], 8 * c)
            if c >= 2:
                s.wait_ge(p.sems["s3_sb"], 224 * (c - 1))
            for i, n in enumerate(names):
                p.dma(s, "s3_sb", out=d["cum_sp"][sl, i],
                                  in_=hml[n][:, c * 128:(c + 1) * 128])
            p.dma(s, "s3_sb", out=d["cum_sp"][sl, 6],
                              in_=ap_(ec_hm, c * 128, [[T, 64], [1, 128]]))
            s.wait_ge(p.sems["s3_sb"], 224 * c + 112)
            for i in range(3):
                p.dma(s, "s3_sb", out=stgA[i:i + 1, :],
                                  in_=ap_(d["cum_sp"], (sl * 7 + i) * 8192,
                                          [[8192, 1], [1, 8192]]))
                p.dma(s, "s3_sb", out=stgB[3 + i:4 + i, :],
                                  in_=ap_(d["cum_sp"], (sl * 7 + 3 + i) * 8192,
                                          [[8192, 1], [1, 8192]]))
            p.dma(s, "s3_sb", out=ecrowT[:],
                              in_=ap_(d["cum_sp"], (sl * 7 + 6) * 8192,
                                      [[8192, 1], [1, 8192]]))
            if c >= 2:
                s.wait_ge(p.sems["s3_tg"], c - 1)
            s.activation(ed_hm[sl][:], p.R["m3_negcum"][:, c * 128:(c + 1) * 128], AF.Exp,
                         bias=cumL[:, c:c + 1])
            p.inc(s.activation(expL[sl][:], cumL[:, c:c + 1], AF.Exp), "s3_pre", 1)
            for g_ in range(8):
                n = 8 * c + g_
                s.wait_ge(p.sems["s3_sp"], n + 1)
                p.inc(s.activation(sfp[n % 2][:], sfp[n % 2][:], AF.Exp), "s3_ex", 1)

    @block.sync
    def _(s):
        s.wait_ge(p.sems["s3_iv"], 1)
        s.wait_ge(p.sems["s3_ig"], 1)
        s.wait_ge(p.sems["m1_st"], 64 * 64 + 16 * 64)
        for c in range(NCH):
            for g_ in range(8):
                n = 8 * c + g_
                if n >= 2:
                    s.wait_ge(p.sems["s3_dx"], n - 1)
                    s.wait_ge(p.sems["s3_xd"], n - 1)
                p.dma(s, "s3_ld", out=xtp[n % 2][:],
                                  in_=ap_(d["xact_sp"], c * DSSM * 128 + g_ * GW * 128,
                                          [[128, GW], [1, 128]]),
                                  transpose=True)
                p.dma(s, "s3_ld", out=xfm[n % 2][:],
                                  in_=ap_(d["xact_sp"], c * DSSM * 128 + g_ * GW * 128,
                                          [[128, 128], [16384, 8], [1, 128]]))
                if n >= 1:
                    m = n - 1
                    s.wait_ge(p.sems["s3_ya"], m + 1)
                    p.dma(s, "s3_yst", 
                        out=ap_(d["y_sp"], (m // 8) * 64 * 16384 + (m % 8) * 8 * 16384,
                                [[128, 128], [16384, 8], [1, 128]]),
                        in_=ap_(ybuf[m % 2], 0, [[GW, 128], [128, 8], [1, 128]]))
        m = NCH * 8 - 1
        s.wait_ge(p.sems["s3_ya"], m + 1)
        p.dma(s, "s3_yst", 
            out=ap_(d["y_sp"], (m // 8) * 64 * 16384 + (m % 8) * 8 * 16384,
                    [[128, 128], [16384, 8], [1, 128]]),
            in_=ap_(ybuf[m % 2], 0, [[GW, 128], [128, 8], [1, 128]]))
        s.wait_ge(p.sems["s3_yst"], 16 * 128)

    @block.tensor
    def _(t):
        t.wait_ge(p.sems["s3_iv"], 1)
        t.wait_ge(p.sems["s3_ig"], 1)
        t.wait_ge(p.sems["m3p_cv"], 2)
        t.wait_ge(p.sems["m3p_cs"], 1)
        t.wait_ge(p.sems["m1_sl"], 66)
        for c in range(NCH):
            sl = c % 2
            if c >= 1:
                t.wait_ge(p.sems["s3_pv"], 3 * c)
            mi = t.matmul(ap_(pG, 0, [[512, 128], [1, 128]]),
                          Bfm[:, c * 128:(c + 1) * 128], Cfm[:, c * 128:(c + 1) * 128],
                          start=True, stop=True)
            p.inc(mi, "s3_pg", 1)
            if c >= 1:
                t.wait_ge(p.sems["s3_tv"], c)
            t.wait_ge(p.sems["s3_pre"], c + 1)
            t.transpose(ap_(ptr, 0, [[1024, 128], [1, 64]]),
                        ap_(dtspb, c * 128, [[T, 64], [1, 128]]), p.R["identb"][0:64, 0:64])
            t.transpose(ap_(ptr, 64, [[1024, 128], [1, 64]]), ed_hm[sl][:],
                        p.R["identb"][0:64, 0:64])
            mi = t.transpose(ap_(ptr, 128, [[1024, 128], [1, 128]]),
                             ap_(Bfm, c * 128, [[T, 128], [1, 128]]), p.R["identb"][:])
            p.inc(mi, "s3_tg", 1)
            t.wait_ge(p.sems["s3_pv"], 3 * c + 1)
            mi = t.transpose(ap_(pG, 128, [[512, 1], [1, 64]]), expL[sl][:],
                             p.R["identf"][0:64, 0:64])
            p.inc(mi, "s3_pg", 1)
            t.wait_ge(p.sems["s3_pv"], 3 * c + 2)
            mi = t.matmul(ap_(pG, 192, [[512, 128], [1, 64]]), p.R["onesrowf"][:],
                          decayrow[sl][:], start=True, stop=True)
            p.inc(mi, "s3_pg", 1)
            for g_ in range(8):
                n = 8 * c + g_
                gsl = n % 2
                t.wait_ge(p.sems["s3_sb"], 224 * (c + 1))
                if n >= 1:
                    t.wait_ge(p.sems["s3_sp"], n)
                for q in range(2):
                    mi = t.matmul(ap_(pBig, q * 512, [[2048, 128], [1, 512]]),
                                  p.R["onesrowb"][:],
                                  ecrowT[:, g_ * GW + q * 512: g_ * GW + (q + 1) * 512],
                                  start=True, stop=True)
                p.inc(mi, "s3_cs", 1)
                t.wait_ge(p.sems["s3_csv"], n + 1)
                for hh in range(8):
                    h = g_ * 8 + hh
                    mi = t.matmul(ap_(pBig, hh * 128, [[2048, 128], [1, 128]]),
                                  stgA[0:6, h * 128:(h + 1) * 128],
                                  stgB[0:6, h * 128:(h + 1) * 128],
                                  start=True, stop=True)
                p.inc(mi, "s3_gam", 1)
                t.wait_ge(p.sems["s3_sS"], n + 1)
                t.wait_ge(p.sems["s3_dx"], n + 1)
                if n >= 1:
                    t.wait_ge(p.sems["s3_ya"], n)
                if c >= 1:
                    t.wait_ge(p.sems["s3_su"], n - 7)
                for hh in range(8):
                    t.matmul(ap_(pBig, 1024 + hh * 128, [[2048, 128], [1, 128]]),
                             dtx[gsl][:, hh * 128:(hh + 1) * 128],
                             Sg[gsl][:, hh * 128:(hh + 1) * 128],
                             start=True, stop=False)
                    mi = t.matmul(ap_(pBig, 1024 + hh * 128, [[2048, 128], [1, 128]]),
                                  statebf[:, (g_ * 8 + hh) * 128:(g_ * 8 + hh + 1) * 128],
                                  Cs[gsl][:, hh * 128:(hh + 1) * 128],
                                  start=False, stop=True)
                p.inc(mi, "s3_ymm", 1)
                t.wait_ge(p.sems["s3_tv"], c + 1)
                t.wait_ge(p.sems["s3_dts"], n + 1)
                if n >= 1:
                    t.wait_ge(p.sems["s3_su"], n)
                for hh in range(8):
                    mi = t.matmul(ap_(pD, hh * 128, [[1024, 128], [1, 128]]),
                                  Btp[sl][:], dtxs[gsl][:, hh * 128:(hh + 1) * 128],
                                  start=True, stop=True)
                p.inc(mi, "s3_dmm", 1)

    @block.vector
    def _(v):
        v = p.wrap(v, "v")
        for c in range(NCH):
            sl = c % 2
            v.wait_ge(p.sems["s3_pg"], 3 * c + 1)
            if c >= 2:
                v.wait_ge(p.sems["s3_sS"], 8 * (c - 1))
            v.tensor_copy(Gf[sl][:], ap_(pG, 0, [[512, 128], [1, 128]]))
            p.inc(v.engine_nop(), "s3_pv", 1)
            v.wait_ge(p.sems["s3_pg"], 3 * c + 2)
            v.tensor_copy(decayrow[sl][:], ap_(pG, 128, [[512, 1], [1, 64]]))
            p.inc(v.engine_nop(), "s3_pv", 1)
            v.wait_ge(p.sems["s3_pg"], 3 * c + 3)
            if c >= 2:
                v.wait_ge(p.sems["s3_smg"], 8 * (c - 1))
            v.tensor_copy(decay_bc[sl][:], ap_(pG, 192, [[512, 128], [1, 64]]))
            p.inc(v.engine_nop(), "s3_pv", 1)
            v.wait_ge(p.sems["s3_tg"], c + 1)
            v.tensor_copy(dtT[sl][:], ap_(ptr, 0, [[1024, 128], [1, 64]]))
            v.tensor_copy(edT[sl][:], ap_(ptr, 64, [[1024, 128], [1, 64]]))
            v.tensor_copy(Btp[sl][:], ap_(ptr, 128, [[1024, 128], [1, 128]]))
            p.inc(v.engine_nop(), "s3_tv", 1)
            for g_ in range(8):
                n = 8 * c + g_
                gsl = n % 2
                v.wait_ge(p.sems["s3_ld"], 32 * (n + 1))
                if n >= 2:
                    v.wait_ge(p.sems["s3_ymm"], n - 1)
                v.tensor_mul(dtx[gsl][:], xtp[gsl][:],
                             ap_(dtT[sl], g_ * 8, [[64, 128], [1, 8], [0, 128]]))
                p.inc(v.engine_nop(), "s3_dx", 1)
                v.wait_ge(p.sems["s3_cs"], n + 1)
                if n >= 2:
                    v.wait_ge(p.sems["s3_ymm"], n - 1)
                v.tensor_mul(Cs[gsl][:],
                             ap_(Cfm, c * 128, [[T, 128], [0, 8], [1, 128]]),
                             pBig[:, 0:1024])
                p.inc(v.engine_nop(), "s3_csv", 1)
                v.wait_ge(p.sems["s3_gam"], n + 1)
                if n >= 2:
                    v.wait_ge(p.sems["s3_sS"], n - 1)
                v.tensor_add(sfp[gsl][:], pBig[:, 0:1024],
                             ap_(mkm, 0, [[128, 128], [0, 8], [1, 128]]))
                p.inc(v.engine_nop(), "s3_sp", 1)
                if n >= 1:
                    m = n - 1
                    msl = m % 2
                    mc, mg = divmod(m, 8)
                    v.wait_ge(p.sems["s3_ymm"], m + 1)
                    v.wait_ge(p.sems["s3_xd"], m + 1)
                    if m >= 2:
                        v.wait_ge(p.sems["s3_yst"], 16 * (m - 1))
                    v.tensor_add(ybuf[msl][:], pBig[:, 1024:2048], xd[msl][:])
                    p.inc(v.engine_nop(), "s3_ya", 1)
                    v.wait_ge(p.sems["s3_dmm"], m + 1)
                    v.wait_ge(p.sems["s3_smg"], m + 1)
                    v.tensor_add(state[:, mg * GW:(mg + 1) * GW],
                                 state[:, mg * GW:(mg + 1) * GW], pD[:])
                    v.tensor_copy(statebf[:, mg * GW:(mg + 1) * GW],
                                  state[:, mg * GW:(mg + 1) * GW])
                    p.inc(v.engine_nop(), "s3_su", 1)
        m = NCH * 8 - 1
        msl = m % 2
        mg = m % 8
        v.wait_ge(p.sems["s3_ymm"], m + 1)
        v.wait_ge(p.sems["s3_xd"], m + 1)
        v.wait_ge(p.sems["s3_yst"], 16 * (m - 1))
        v.tensor_add(ybuf[msl][:], pBig[:, 1024:2048], xd[msl][:])
        p.inc(v.engine_nop(), "s3_ya", 1)
        v.wait_ge(p.sems["s3_dmm"], m + 1)
        v.wait_ge(p.sems["s3_smg"], m + 1)
        v.tensor_add(state[:, mg * GW:(mg + 1) * GW], state[:, mg * GW:(mg + 1) * GW], pD[:])
        v.tensor_copy(statebf[:, mg * GW:(mg + 1) * GW], state[:, mg * GW:(mg + 1) * GW])
        p.inc(v.engine_nop(), "s3_su", 1)

    @block.gpsimd
    def _(g):
        g = p.wrap(g, "g")
        for c in range(NCH):
            sl = c % 2
            for g_ in range(8):
                n = 8 * c + g_
                gsl = n % 2
                g.wait_ge(p.sems["s3_ld"], 32 * (n + 1))
                if n >= 2:
                    g.wait_ge(p.sems["s3_ya"], n - 1)
                g.tensor_mul(xd[gsl][:], xfm[gsl][:],
                             ap_(D_bc, g_ * 8, [[64, 128], [1, 8], [0, 128]]))
                p.inc(g.engine_nop(), "s3_xd", 1)
                g.wait_ge(p.sems["s3_dx"], n + 1)
                g.wait_ge(p.sems["s3_tv"], c + 1)
                if n >= 2:
                    g.wait_ge(p.sems["s3_dmm"], n - 1)
                g.tensor_mul(dtxs[gsl][:], dtx[gsl][:],
                             ap_(edT[sl], g_ * 8, [[64, 128], [1, 8], [0, 128]]))
                p.inc(g.engine_nop(), "s3_dts", 1)
                g.wait_ge(p.sems["s3_ex"], n + 1)
                g.wait_ge(p.sems["s3_pv"], 3 * c + 1)
                if n >= 2:
                    g.wait_ge(p.sems["s3_ymm"], n - 1)
                g.tensor_mul(Sg[gsl][:], sfp[gsl][:],
                             ap_(Gf[sl], 0, [[128, 128], [0, 8], [1, 128]]))
                p.inc(g.engine_nop(), "s3_sS", 1)
                g.wait_ge(p.sems["s3_pv"], 3 * c + 3)
                if c >= 1:
                    g.wait_ge(p.sems["s3_su"], n - 7)
                g.tensor_mul(state[:, g_ * GW:(g_ + 1) * GW],
                             state[:, g_ * GW:(g_ + 1) * GW],
                             ap_(decay_bc[sl], g_ * 8, [[64, 128], [1, 8], [0, 128]]))
                p.inc(g.engine_nop(), "s3_smg", 1)
    ms.close()



def emit_mamba_m4(p, d, res_old, res_new_idx):
    block = p.block
    ms = p.scope()
    yz = p.sbuf("m4_yz", [128, 64 * 1024], BF16, scope=ms)
    wob = [p.sbuf(f"m4_w{i}", [128, 64 * 128], BF16, scope=ms) for i in range(2)]
    zst = [p.sbuf(f"m4_z{i}", [128, 1024], BF16, scope=ms) for i in range(2)]
    zsil = [p.sbuf(f"m4_zs{i}", [128, 1024], BF16, scope=ms) for i in range(2)]
    sqb = [p.sbuf(f"m4_sq{i}", [128, 1024], BF16, scope=ms) for i in range(2)]
    rrow = p.sbuf("m4_rrow", [1, 1024], F32, scope=ms)
    rstdb = p.sbuf("m4_rstdb", [128, 1024], BF16, scope=ms)
    rold = [p.sbuf(f"m4_ro{i}", [128, 1024], F32, scope=ms) for i in range(2)]
    rnew = [p.sbuf(f"m4_rn{i}", [128, 1024], BF16, scope=ms) for i in range(2)]
    psE = [p.psum(f"m4_ps{i}", [128, 1024], scope=ms) for i in range(2)]
    pst = p.psum("m4_pst", [128, 1024], scope=ms)
    for s_ in ("m4_yld", "m4_zld", "m4_sil", "m4_yzs", "m4_sq", "m4_smm",
               "m4_rowv", "m4_rowss", "m4_bct", "m4_bcs", "m4_yn", "m4_wld",
               "m4_mm", "m4_rold", "m4_ev", "m4_st"):
        p.sem(s_)
    nw = p.R["m3_nw"]

    for th in range(2):
        B = {k: p.cnt[k] for k in ("m4_yld", "m4_zld", "m4_sil", "m4_yzs", "m4_sq",
                                   "m4_smm", "m4_rowv", "m4_rowss", "m4_bct", "m4_bcs",
                                   "m4_yn", "m4_wld", "m4_mm", "m4_rold", "m4_ev",
                                   "m4_st")}

        @block.sync
        def _(s):
            if th == 1:
                s.wait_ge(p.sems["m4_yn"], B["m4_yn"])
                s.wait_ge(p.sems["m4_mm"], B["m4_mm"])
            else:
                s.wait_ge(p.sems["s3_yst"], 16 * 128)
            for cc in range(8):
                c = th * 8 + cc
                p.dma(s, "m4_yld", 
                    out=ap_(yz, cc * 128, [[65536, 128], [1024, 64], [1, 128]]),
                    in_=ap_(d["y_sp"], c * 64 * 16384, [[128, 128], [16384, 64], [1, 128]]))
            for h in range(64):
                if h >= 2:
                    s.wait_ge(p.sems["m4_sil"], B["m4_sil"] + h - 1)
                for jj in range(2):
                    p.dma(s, "m4_zld", out=zst[h % 2][:, jj * 512:(jj + 1) * 512],
                                      in_=d["z_sp"][th * 2 + jj, h])

        @block.scalar
        def _(s):
            s = p.wrap(s, "s")
            if B["m4_yzs"]:
                s.wait_ge(p.sems["m4_yzs"], B["m4_yzs"])
            for h in range(64):
                s.wait_ge(p.sems["m4_zld"], B["m4_zld"] + 32 * (h + 1))
                if h >= 2:
                    s.wait_ge(p.sems["m4_yzs"], B["m4_yzs"] + h - 1)
                p.inc(s.activation(zsil[h % 2][:], zst[h % 2][:], AF.Silu), "m4_sil", 1)

        @block.vector
        def _(v):
            v = p.wrap(v, "v")
            v.wait_ge(p.sems["m4_yld"], B["m4_yld"] + 16 * 8)
            for h in range(64):
                v.wait_ge(p.sems["m4_sil"], B["m4_sil"] + h + 1)
                yv = yz[:, h * 1024:(h + 1) * 1024]
                v.tensor_mul(yv, yv, zsil[h % 2][:])
                p.inc(v.engine_nop(), "m4_yzs", 1)
                if h >= 2:
                    v.wait_ge(p.sems["m4_smm"], B["m4_smm"] + h - 1)
                v.tensor_mul(sqb[h % 2][:], yv, yv)
                p.inc(v.engine_nop(), "m4_sq", 1)

        @block.tensor
        def _(t):
            for h in range(64):
                t.wait_ge(p.sems["m4_sq"], B["m4_sq"] + h + 1)
                for q in range(2):
                    mi = t.matmul(ap_(pst, q * 512, [[1024, 1], [1, 512]]),
                                  p.R["onescolb"][:], sqb[h % 2][:, q * 512:(q + 1) * 512],
                                  start=(h == 0), stop=(h == 63))
                p.inc(mi, "m4_smm", 1)

        @block.vector
        def _(v):
            v = p.wrap(v, "v")
            v.wait_ge(p.sems["m4_smm"], B["m4_smm"] + 64)
            v.tensor_copy(rrow[:], ap_(pst, 0, [[1024, 1], [1, 1024]]))
            v.tensor_scalar(rrow[:], rrow[:], 1.0 / DSSM, None, ALU.mult)
            v.tensor_scalar_add(rrow[:], rrow[:], 1e-5)
            p.inc(v.engine_nop(), "m4_rowv", 1)

        @block.scalar
        def _(s):
            s = p.wrap(s, "s")
            s.wait_ge(p.sems["m4_rowv"], B["m4_rowv"] + 1)
            p.inc(s.activation(rrow[:], rrow[:], AF.Sqrt), "m4_rowss", 1)

        @block.vector
        def _(v):
            v = p.wrap(v, "v")
            v.wait_ge(p.sems["m4_rowss"], B["m4_rowss"] + 1)
            v.reciprocal(rrow[:], rrow[:])
            p.inc(v.engine_nop(), "m4_rowv", 1)

        @block.tensor
        def _(t):
            t.wait_ge(p.sems["m4_rowv"], B["m4_rowv"] + 2)
            for q in range(2):
                mi = t.matmul(ap_(pst, q * 512, [[1024, 128], [1, 512]]),
                              p.R["onesrowf"][:], rrow[:, q * 512:(q + 1) * 512],
                              start=True, stop=True)
            p.inc(mi, "m4_bct", 1)

        @block.scalar
        def _(s):
            s = p.wrap(s, "s")
            s.wait_ge(p.sems["m4_bct"], B["m4_bct"] + 1)
            p.inc(s.copy(rstdb[:], pst[:]), "m4_bcs", 1)

        @block.vector
        def _(v):
            v = p.wrap(v, "v")
            v.wait_ge(p.sems["m4_bcs"], B["m4_bcs"] + 1)
            for h in range(64):
                yv = yz[:, h * 1024:(h + 1) * 1024]
                v.scalar_tensor_tensor(yv, yv, nw[:, h:h + 1], rstdb[:],
                                       ALU.mult, ALU.mult)
            p.inc(v.engine_nop(), "m4_yn", 1)

        def _rold(g, o):
            if o >= 2:
                g.wait_ge(p.sems["m4_ev"], B["m4_ev"] + o - 1)
            elif B["m4_ev"]:
                g.wait_ge(p.sems["m4_ev"], B["m4_ev"])
            if res_old[0] == "x0":
                src_ = ap_(d["x0"], o * 128 * T + th * 1024, [[T, 128], [1, 1024]])
            else:
                src_ = ap_(d["res_sp"],
                           res_old[1] * DC * 128 * T + o * 128 * T + th * 1024,
                           [[T, 128], [1, 1024]])
            p.dma(g, "m4_rold", out=rold[o % 2][:], in_=src_)

        @block.gpsimd
        def _(g):
            g = p.wrap(g, "g")
            if B["m4_mm"]:
                g.wait_ge(p.sems["m4_mm"], B["m4_mm"])
            for o in range(32):
                if o >= 2:
                    g.wait_ge(p.sems["m4_mm"], B["m4_mm"] + o - 1)
                p.dma(g, "m4_wld", 
                    out=ap_(wob[o % 2], 0, [[64 * 128, 128], [128, 64], [1, 128]]),
                    in_=d["w_out"][o].rearrange("a p q -> p a q"))
                if o >= 2:
                    _rold(g, o - 2)
            _rold(g, 30)
            _rold(g, 31)

        @block.tensor
        def _(t):
            t.wait_ge(p.sems["m4_yn"], B["m4_yn"] + 1)
            for o in range(32):
                t.wait_ge(p.sems["m4_wld"], B["m4_wld"] + 16 * (o + 1))
                if o >= 2:
                    t.wait_ge(p.sems["m4_ev"], B["m4_ev"] + o - 1)
                for k in range(64):
                    lt = wob[o % 2][:, k * 128:(k + 1) * 128]
                    for j in range(2):
                        mi = t.matmul(ap_(psE[o % 2], j * 512, [[1024, 128], [1, 512]]),
                                      lt, yz[:, k * 1024 + j * 512: k * 1024 + (j + 1) * 512],
                                      start=(k == 0), stop=(k == 63))
                p.inc(mi, "m4_mm", 1)

        @block.vector
        def _(v):
            v = p.wrap(v, "v")
            if B["m4_st"]:
                v.wait_ge(p.sems["m4_st"], B["m4_st"])
            for o in range(32):
                v.wait_ge(p.sems["m4_mm"], B["m4_mm"] + o + 1)
                v.wait_ge(p.sems["m4_rold"], B["m4_rold"] + 16 * (o + 1))
                if o >= 2:
                    v.wait_ge(p.sems["m4_st"], B["m4_st"] + 16 * (o - 1))
                v.tensor_add(rnew[o % 2][:], psE[o % 2][:], rold[o % 2][:])
                p.inc(v.engine_nop(), "m4_ev", 1)

        @block.sync
        def _(s):
            for o in range(32):
                s.wait_ge(p.sems["m4_ev"], B["m4_ev"] + o + 1)
                p.dma(s, "m4_st", 
                    out=ap_(d["res_sp"], res_new_idx * DC * 128 * T + o * 128 * T + th * 1024,
                            [[T, 128], [1, 1024]]),
                    in_=rnew[o % 2][:])
            s.wait_ge(p.sems["m4_st"], B["m4_st"] + 16 * 32)
    ms.close()



def emit_dump_res(p, d, idx):
    block = p.block
    ms = p.scope()
    stage = [p.sbuf(f"du_{i}", [128, T], BF16, scope=ms) for i in range(2)]
    p.sem("du_ld"); p.sem("du_st")

    @block.sync
    def _(s):
        for c in range(DC):
            if c >= 2:
                s.wait_ge(p.sems["du_st"], 16 * (c - 1))
            p.dma(s, "du_ld", out=stage[c % 2][:], in_=d["res_sp"][idx, c])

    @block.gpsimd
    def _(g):
        g = p.wrap(g, "g")
        for c in range(DC):
            g.wait_ge(p.sems["du_ld"], 16 * (c + 1))
            p.dma(g, "du_st", out=d["out"][c * 128:(c + 1) * 128, :], in_=stage[c % 2][:])
        g.wait_ge(p.sems["du_st"], 16 * DC)
    ms.close()



def emit_dump_sbuf(p, d, src_tensor, rows, cols):
    block = p.block
    p.sem("dux")

    @block.gpsimd
    def _(g):
        g = p.wrap(g, "g")
        for s_ in list(p.sems):
            pass
        p.dma(g, "dux", out=d["out"][0:rows, 0:cols], in_=src_tensor[0:rows, 0:cols])
        g.wait_ge(p.sems["dux"], 16)


def emit_dump_dram(p, d, name, nrow, ncol, row_stride=None):
    block = p.block
    ms = p.scope()
    stage = p.sbuf("dud", [128, ncol], BF16, scope=ms)
    p.sem("dud_a"); p.sem("dud_b")
    nt = nrow // 128

    @block.sync
    def _(s):
        for i in range(nt):
            if i >= 1:
                s.wait_ge(p.sems["dud_b"], 16 * i)
            p.dma(s, "dud_a", out=stage[:],
                              in_=ap_(d[name], i * 128 * ncol, [[ncol, 128], [1, ncol]]))

    @block.gpsimd
    def _(g):
        g = p.wrap(g, "g")
        for i in range(nt):
            g.wait_ge(p.sems["dud_a"], 16 * (i + 1))
            p.dma(g, "dud_b", out=d["out"][i * 128:(i + 1) * 128, 0:ncol], in_=stage[:])
        g.wait_ge(p.sems["dud_b"], 16 * nt)
    ms.close()


def emit_final_drain(p):
    block = p.block
    cnts = dict(p.cnt)

    @block.sync
    def _(s):
        for name, v in cnts.items():
            if v > 0:
                s.wait_ge(p.sems[name], v)


def build_full(parts=("mamba", "ffn", "nsa"), upto="all", test_dump=None, n_cores=8):
    nc = bass.Bass("TRN2", target_bir_lowering=False)
    with ExitStack() as ctx:
        block = ctx.enter_context(nc.Block())
        p = Prog(nc, ctx, block)
        d = declare_dram(nc, parts)
        emit_init(p, d)
        xsc = p.scope()
        xln = emit_boundary(p, d, 0, ("x0",), xsc)
        p.phase_end(("bn0_",))
        if upto == "boundary":
            emit_dump_sbuf2(p, d, xln)
            emit_final_drain(p)
            p.right.close()
            return nc
        if "mamba" in parts:
            emit_mamba_m1(p, d, xln)
            p.barrier()
            if upto == "m1":
                emit_dump_dram(p, d, "xact_sp", 4096, 128)
                emit_final_drain(p)
                xsc.close(); p.mam_res.close(); p.right.close()
                return nc
            xsc.close()
            emit_mamba_m3pre(p, d)
            if upto == "m3pre":
                emit_dump_sbuf3(p, d)
                emit_final_drain(p)
                p.mam_res.close(); p.right.close()
                return nc
            emit_mamba_m3(p, d)
            p.barrier()
            if upto == "m3":
                emit_dump_dram(p, d, "y_sp", 4096, 128)
                emit_final_drain(p)
                p.mam_res.close(); p.right.close()
                return nc
            p.mam_res.close()
            emit_mamba_m4(p, d, ("x0",), 0)
            p.barrier()
        if "ffn" in parts and upto in ("x2", "x3", "full", "all", "nsaq", "nsag"):
            xs1 = p.scope()
            xln1 = emit_boundary(p, d, 1, ("res", 0), xs1)
            p.phase_end(("bn1_",))
            emit_ffn(p, d, xln1, 1, ("res", 0), 1, final=False)
            p.phase_end(("f1_",))
            xs1.close()
            if upto == "x2":
                emit_dump_res(p, d, 1)
                emit_final_drain(p)
                p.right.close()
                return nc
        if "nsa" in parts and upto in ("x3", "full", "all", "nsaq", "nsag"):
            xs2 = p.scope()
            xln2 = emit_boundary(p, d, 2, ("res", 1), xs2)
            p.phase_end(("bn2_",))
            nres = p.scope()
            emit_nsa_qkv(p, d, xln2, nres)
            p.phase_end(("nq_",))
            xs2.close()
            emit_nsa_prep(p, d, nres)
            p.phase_end(("n0_",))
            if upto == "nsaq":
                emit_dump_dram(p, d, "q_sp", 4096, 2048)
                emit_final_drain(p)
                nres.close(); p.right.close()
                return nc
            emit_nsa_attn(p, d, nres)
            p.phase_end(("n1_",))
            nres.close()
            if upto == "nsag":
                emit_dump_dram(p, d, "gat_sp", 4096, 2048)
                emit_final_drain(p)
                p.right.close()
                return nc
            emit_nsa_out(p, d, 1, 0)
            p.phase_end(("no_",))
            if upto == "x3":
                emit_dump_res(p, d, 0)
                emit_final_drain(p)
                p.right.close()
                return nc
            xs3 = p.scope()
            xln3 = emit_boundary(p, d, 3, ("res", 0), xs3)
            p.phase_end(("bn3_",))
            emit_ffn(p, d, xln3, 3, ("res", 0), None, final=True)
            xs3.close()
        if test_dump is not None:
            emit_dump_res(p, d, test_dump)
        emit_final_drain(p)
        p.right.close()
    return nc


def emit_dump_sbuf2(p, d, xln):
    block = p.block
    p.sem("dux")

    @block.gpsimd
    def _(g):
        g = p.wrap(g, "g")
        for c in range(DC):
            p.dma(g, "dux", out=d["out"][c * 128:(c + 1) * 128, :],
                              in_=ap_(xln, c * T, [[DC * T, 128], [1, T]]))
        g.wait_ge(p.sems["dux"], 16 * DC)


def emit_dump_sbuf3(p, d):
    block = p.block
    p.sem("dux"); p.sem("duy")

    @block.sync
    def _(s):
        p.dma(s, "duy", out=d["dbg_sp"], in_=p.R["m3_cum"][:])

    @block.gpsimd
    def _(g):
        g = p.wrap(g, "g")
        g.wait_ge(p.sems["duy"], 16)
        p.dma(g, "dux", out=d["out"][0:64, :], in_=d["dbg_sp"])
        g.wait_ge(p.sems["dux"], 16)


def host_inputs(inputs, parts=("mamba", "ffn", "nsa")):
    consts = prep_consts()
    w = {}
    if "mamba" in parts:
        w.update(prep_mamba(inputs))
    if "ffn" in parts:
        w.update(prep_ffn(inputs))
    if "nsa" in parts:
        w.update(prep_nsa(inputs))
    needed_consts = {"ident"}
    if "mamba" in parts:
        needed_consts |= {"mk_mask"}
    if "nsa" in parts:
        needed_consts |= {"cmask", "fown", "iotajb", "sbandF", "sbandF0", "rcos", "rsin"}
    base = {k: consts[k] for k in needed_consts}
    base.update(w)
    x = np.asarray(inputs["L0_attn_a_hidden_states"], np.float32)
    maps = []
    for b in range(x.shape[0]):
        m = dict(base)
        m["x0"] = np.ascontiguousarray(x[b].T)
        maps.append(m)
    return maps


def kernel(**inputs):
    parts = ("mamba", "ffn", "nsa")
    nc = build_full(parts=parts, test_dump=None)
    maps = host_inputs(inputs, parts)
    res = run_bass_kernel_spmd(nc, maps, core_ids=list(range(8)))
    return np.stack([np.asarray(r["out"]).T for r in res.results], axis=0)



def emit_ffn(p, d, xln, idx, res_old, res_new, final=False):
    block = p.block
    fs = p.scope()
    wfb = [p.sbuf(f"f{idx}_w{i}", [128, 32 * 128], BF16, scope=fs) for i in range(2)]
    brf = p.sbuf(f"f{idx}_brf", [128, 4096], F32, scope=fs)
    bsc = p.sbuf(f"f{idx}_bsc", [128, 4096], BF16, scope=fs)
    afb = p.sbuf(f"f{idx}_afb", [128, 32 * 128], BF16, scope=fs)
    ter = p.sbuf(f"f{idx}_ter", [128, T], BF16, scope=fs)
    qcol = p.sbuf(f"f{idx}_q", [128, 32], F32, scope=fs)
    rwf = p.sbuf(f"f{idx}_rw", [128, 256], F32, scope=fs)
    rbc = p.sbuf(f"f{idx}_rb", [8, 1], F32, scope=fs)
    lcol = p.sbuf(f"f{idx}_lc", [8, 1], F32, scope=fs)
    lrow = p.sbuf(f"f{idx}_lr", [1, 8], F32, scope=fs)
    lmax = p.sbuf(f"f{idx}_lm", [1, 1], F32, scope=fs)
    rcol = p.sbuf(f"f{idx}_rc", [128, 1], F32, scope=fs)
    fbc = p.sbuf(f"f{idx}_fb", [128, 32], F32, scope=fs)
    rold = [p.sbuf(f"f{idx}_ro{i}", [128, T],
                   F32 if res_old[0] == "x0" else BF16, scope=fs) for i in range(2)]
    rnew = [p.sbuf(f"f{idx}_rn{i}", [128, T], F32 if final else BF16, scope=fs)
            for i in range(2)]
    psF = [p.psum(f"f{idx}_ps{i}", [128, 2048], scope=fs) for i in range(2)]
    plog = psF[1]
    S = lambda n: f"f{idx}_" + n
    for n in ("ld", "qv", "lmm", "lev", "rt1", "rt2", "rt2v", "rt2s", "rt3", "rts",
              "bsc", "aldg", "ter", "tev", "wld", "mm", "rold", "ev", "st"):
        p.sem(S(n))

    @block.sync
    def _(s):
        p.dma(s, S("ld"), out=ap_(rwf, 0, [[256, 128], [8, 32], [1, 8]]),
              in_=d["ffn_rw"].rearrange("a p q -> p a q"))
        p.dma(s, S("ld"), out=rbc[:], in_=d["ffn_rb"])
        p.dma(s, S("ld"), out=fbc[:], in_=d["ffn_b"])
        p.dma(s, S("ld"), out=brf[:], in_=d["ffn_bmat"])
        s.wait_ge(p.sems[S("lev")], 1)
        p.dma(s, S("rt1"), out=d["r8"], in_=ap_(lcol, 0, [[1, 8], [1, 1]]))
        s.wait_ge(p.sems[S("rt1")], 16)
        p.dma(s, S("rt2"), out=lrow[:], in_=d["r8"])
        s.wait_ge(p.sems[S("rts")], 1)
        p.dma(s, S("rt1"), out=d["r8"], in_=lrow[:])
        s.wait_ge(p.sems[S("rt1")], 32)
        p.dma(s, S("rt3"), out=rcol[:], in_=ap_(d["r8"], 0, [[1, 8], [0, 16]]))

    @block.gpsimd
    def _(g):
        g = p.wrap(g, "g")
        p.dma(g, S("aldg"), out=afb[:],
              in_=d["ffn_a"][0].rearrange("a p q -> p a q"))
        for o in range(32):
            if o >= 2:
                g.wait_ge(p.sems[S("mm")], o - 1)
            p.dma(g, S("wld"), out=ap_(wfb[o % 2], 0, [[4096, 128], [128, 32], [1, 128]]),
                  in_=d["ffn_w"][o].rearrange("a p q -> p a q"))
            if o >= 2:
                oo = o - 2
                if oo >= 2:
                    g.wait_ge(p.sems[S("ev")], oo - 1)
                if res_old[0] == "x0":
                    p.dma(g, S("rold"), out=rold[oo % 2][:],
                          in_=ap_(d["x0"], oo * 128 * T, [[T, 128], [1, T]]))
                else:
                    p.dma(g, S("rold"), out=rold[oo % 2][:],
                          in_=ap_(d["res_sp"], res_old[1] * DC * 128 * T + oo * 128 * T,
                                  [[T, 128], [1, T]]))
        for oo in (30, 31):
            g.wait_ge(p.sems[S("ev")], oo - 1)
            if res_old[0] == "x0":
                p.dma(g, S("rold"), out=rold[oo % 2][:],
                      in_=ap_(d["x0"], oo * 128 * T, [[T, 128], [1, T]]))
            else:
                p.dma(g, S("rold"), out=rold[oo % 2][:],
                      in_=ap_(d["res_sp"], res_old[1] * DC * 128 * T + oo * 128 * T,
                              [[T, 128], [1, T]]))

    @block.vector
    def _(v):
        v = p.wrap(v, "v")
        for c in range(DC):
            v.reduce_sum(qcol[:, c:c + 1],
                         ap_(xln, c * T + 611, [[DC * T, 128], [1, 1436]]), axis=AX.X)
        v.tensor_scalar(qcol[:], qcol[:], 1.0 / 1436.0, None, ALU.mult)
        p.inc(v.engine_nop(), S("qv"), 1)
        v.wait_ge(p.sems[S("lmm")], 1)
        v.wait_ge(p.sems[S("ld")], 32)
        v.tensor_add(lcol[:], ap_(plog, 0, [[2048, 8], [1, 1]]), rbc[:])
        p.inc(v.engine_nop(), S("lev"), 1)
        v.wait_ge(p.sems[S("rt2")], 16)
        v.reduce_max(lmax[:], lrow[:], axis=AX.X)
        v.tensor_scalar(lrow[:], lrow[:], lmax[:], None, ALU.subtract)
        p.inc(v.engine_nop(), S("rt2v"), 1)

    @block.scalar
    def _(s):
        s = p.wrap(s, "s")
        s.wait_ge(p.sems[S("rt2v")], 1)
        p.inc(s.activation(lrow[:], lrow[:], AF.Exp), S("rt2s"), 1)

    @block.vector
    def _(v):
        v = p.wrap(v, "v")
        v.wait_ge(p.sems[S("rt2s")], 1)
        v.reduce_sum(lmax[:], lrow[:], axis=AX.X)
        v.reciprocal(lmax[:], lmax[:])
        v.tensor_scalar(lrow[:], lrow[:], lmax[:], None, ALU.mult)
        p.inc(v.engine_nop(), S("rts"), 1)
        v.wait_ge(p.sems[S("rt3")], 16)
        v.wait_ge(p.sems[S("ld")], 64)
        v.tensor_scalar(bsc[:], brf[:], rcol[:], 2.0, ALU.mult, ALU.mult)
        p.inc(v.engine_nop(), S("bsc"), 1)
        v.wait_ge(p.sems[S("ter")], 1)
        v.tensor_copy(ter[:], psF[0][:])
        p.inc(v.engine_nop(), S("tev"), 1)

    @block.tensor
    def _(t):
        t.wait_ge(p.sems[S("qv")], 1)
        t.wait_ge(p.sems[S("ld")], 16)
        for c in range(DC):
            mi = t.matmul(ap_(plog, 0, [[2048, 8], [1, 1]]),
                          rwf[:, c * 8:(c + 1) * 8], qcol[:, c:c + 1],
                          start=(c == 0), stop=(c == DC - 1))
        p.inc(mi, S("lmm"), 1)
        t.wait_ge(p.sems[S("aldg")], 16)
        for dd in range(DC):
            lt = afb[:, dd * 128:(dd + 1) * 128]
            for j in range(4):
                mi = t.matmul(ap_(psF[0], j * 512, [[2048, 128], [1, 512]]),
                              lt, ap_(xln, dd * T + j * 512, [[DC * T, 128], [1, 512]]),
                              start=(dd == 0), stop=(dd == DC - 1))
        p.inc(mi, S("ter"), 1)
        t.wait_ge(p.sems[S("tev")], 1)
        t.wait_ge(p.sems[S("bsc")], 1)
        for o in range(32):
            t.wait_ge(p.sems[S("wld")], 16 * (o + 1))
            if o >= 2:
                t.wait_ge(p.sems[S("ev")], o - 1)
            pso = psF[o % 2]
            for dd in range(DC):
                lt = wfb[o % 2][:, dd * 128:(dd + 1) * 128]
                for j in range(4):
                    t.matmul(ap_(pso, j * 512, [[2048, 128], [1, 512]]),
                             lt, ap_(xln, dd * T + j * 512, [[DC * T, 128], [1, 512]]),
                             start=(dd == 0), stop=False)
            for j in range(4):
                mi = t.matmul(ap_(pso, j * 512, [[2048, 128], [1, 512]]),
                              bsc[:, o * 128:(o + 1) * 128],
                              ter[:, j * 512:(j + 1) * 512],
                              start=False, stop=True)
            p.inc(mi, S("mm"), 1)

    @block.vector
    def _(v):
        v = p.wrap(v, "v")
        v.wait_ge(p.sems[S("ld")], 48)
        for o in range(32):
            v.wait_ge(p.sems[S("mm")], o + 1)
            v.wait_ge(p.sems[S("rold")], 16 * (o + 1))
            if o >= 2:
                v.wait_ge(p.sems[S("st")], 16 * (o - 1))
            v.scalar_tensor_tensor(rnew[o % 2][:], psF[o % 2][:], fbc[:, o:o + 1],
                                   rold[o % 2][:], ALU.add, ALU.add)
            p.inc(v.engine_nop(), S("ev"), 1)

    @block.sync
    def _(s):
        for o in range(32):
            s.wait_ge(p.sems[S("ev")], o + 1)
            if final:
                p.dma(s, S("st"), out=d["out"][o * 128:(o + 1) * 128, :], in_=rnew[o % 2][:])
            else:
                p.dma(s, S("st"),
                      out=ap_(d["res_sp"], res_new * DC * 128 * T + o * 128 * T,
                              [[T, 128], [1, T]]),
                      in_=rnew[o % 2][:])
        s.wait_ge(p.sems[S("st")], 16 * 32)
    fs.close()



def emit_nsa_qkv(p, d, xln, nres):
    block = p.block
    ns = p.scope()
    wqb = [p.sbuf(f"nq_w{i}", [128, 32 * 128], BF16, scope=ns) for i in range(2)]
    qst = [p.sbuf(f"nq_s{i}", [128, T], BF16, scope=ns) for i in range(2)]
    wgb = p.sbuf("nq_wg", [128, 32 * 128], BF16, scope=ns)
    bgc = p.sbuf("nq_bg", [128, 1], F32, scope=ns)
    gfm = p.sbuf("nq_gfm", [128, T], F32, scope=ns)
    psQ = [p.psum(f"nq_ps{i}", [128, 2048], scope=ns) for i in range(2)]
    gtok = p.sbuf("gtok", [128, 2048], F32, scope=nres, side="right")
    for n in ("nq_wld", "nq_gld", "nq_mm", "nq_ev", "nq_st", "nq_gmm", "nq_gev",
              "nq_gtr", "nq_gtv"):
        p.sem(n)

    @block.sync
    def _(s):
        p.dma(s, "nq_gld", out=bgc[:], in_=d["b_gate"])

    @block.gpsimd
    def _(g):
        g = p.wrap(g, "g")
        p.dma(g, "nq_wld", out=ap_(wgb, 0, [[4096, 128], [128, 32], [1, 128]]),
              in_=d["w_gate"][0].rearrange("a p q -> p a q"))
        for o in range(48):
            if o >= 2:
                g.wait_ge(p.sems["nq_mm"], o - 1)
            p.dma(g, "nq_wld", out=ap_(wqb[o % 2], 0, [[4096, 128], [128, 32], [1, 128]]),
                  in_=d["w_qkv"][o].rearrange("a p q -> p a q"))

    @block.tensor
    def _(t):
        t.wait_ge(p.sems["nq_wld"], 16)
        for dd in range(DC):
            lt = wgb[:, dd * 128:(dd + 1) * 128]
            for j in range(4):
                mi = t.matmul(ap_(psQ[1], j * 512, [[2048, 128], [1, 512]]),
                              lt, ap_(xln, dd * T + j * 512, [[DC * T, 128], [1, 512]]),
                              start=(dd == 0), stop=(dd == DC - 1))
        p.inc(mi, "nq_gmm", 1)
        t.wait_ge(p.sems["nq_gev"], 1)
        for tc in range(NCH):
            if tc >= 2:
                t.wait_ge(p.sems["nq_gtv"], tc - 1)
            mi = t.transpose(ap_(psQ[1], (tc % 2) * 512, [[2048, 128], [1, 128]]),
                             gfm[:, tc * 128:(tc + 1) * 128], p.R["identf"][:])
            p.inc(mi, "nq_gtr", 1)
        for o in range(48):
            t.wait_ge(p.sems["nq_wld"], 16 * (o + 2))
            if o <= 1:
                t.wait_ge(p.sems["nq_gtv"], NCH)
            if o >= 2:
                t.wait_ge(p.sems["nq_ev"], o - 1)
            pso = psQ[o % 2]
            for dd in range(DC):
                lt = wqb[o % 2][:, dd * 128:(dd + 1) * 128]
                for j in range(4):
                    mi = t.matmul(ap_(pso, j * 512, [[2048, 128], [1, 512]]),
                                  lt, ap_(xln, dd * T + j * 512, [[DC * T, 128], [1, 512]]),
                                  start=(dd == 0), stop=(dd == DC - 1))
            p.inc(mi, "nq_mm", 1)

    @block.scalar
    def _(s):
        s = p.wrap(s, "s")
        s.wait_ge(p.sems["nq_gmm"], 1)
        s.wait_ge(p.sems["nq_gld"], 16)
        p.inc(s.activation(gfm[:], psQ[1][:], AF.Sigmoid, bias=bgc[:]), "nq_gev", 1)
        for o in range(48):
            s.wait_ge(p.sems["nq_mm"], o + 1)
            if o >= 2:
                s.wait_ge(p.sems["nq_st"], 16 * (o - 1))
            p.inc(s.copy(qst[o % 2][:], psQ[o % 2][:]), "nq_ev", 1)

    @block.vector
    def _(v):
        v = p.wrap(v, "v")
        for tc in range(NCH):
            v.wait_ge(p.sems["nq_gtr"], tc + 1)
            v.tensor_copy(gtok[:, tc * 128:(tc + 1) * 128],
                          ap_(psQ[1], (tc % 2) * 512, [[2048, 128], [1, 128]]))
            p.inc(v.engine_nop(), "nq_gtv", 1)

    @block.sync
    def _(s):
        for o in range(48):
            s.wait_ge(p.sems["nq_ev"], o + 1)
            if o < 32:
                dst = ap_(d["q_sp"], o * 128 * T, [[T, 128], [1, T]])
            elif o < 40:
                dst = ap_(d["k_sp"], (o - 32) * 128 * T, [[T, 128], [1, T]])
            else:
                dst = ap_(d["v_sp"], (o - 40) * 128 * T, [[T, 128], [1, T]])
            p.dma(s, "nq_st", out=dst, in_=qst[o % 2][:])
        s.wait_ge(p.sems["nq_st"], 16 * 48)
    ns.close()



def emit_nsa_prep(p, d, nres):
    block = p.block
    n0 = p.scope()
    rk = p.sbuf("n_rk", [128, 16384], BF16, scope=nres, side="right")
    vT = p.sbuf("n_vT", [128, 16384], BF16, scope=nres, side="right")
    ckb = p.sbuf("n_ckb", [128, 1032], BF16, scope=nres, side="right")
    cvb = p.sbuf("n_cvb", [128, 1032], BF16, scope=nres, side="right")
    cvT = p.sbuf("n_cvT", [128, 1024], BF16, scope=nres, side="right")
    cvlast = p.sbuf("n_cvlast", [65, 1024], BF16, scope=nres, side="right")
    cmaskS = p.sbuf("n_cmask", [128, 2064], F32, scope=nres, side="right")
    fownS = p.sbuf("n_fown", [128, 128], F32, scope=nres, side="right")
    sbF = p.sbuf("n_sbF", [128, 256], F32, scope=nres, side="right")
    sbF0 = p.sbuf("n_sbF0", [128, 128], F32, scope=nres, side="right")
    rcosf = p.sbuf("n0_cos", [128, T], BF16, scope=nres, side="right")
    rsinf = p.sbuf("n0_sin", [128, T], BF16, scope=nres, side="right")
    kfm = p.sbuf("n0_kfm", [128, 16384], BF16, scope=n0)
    ckin = p.sbuf("n0_ckin", [128, 8192], BF16, scope=n0)
    h1 = p.sbuf("n0_h1", [128, 8192], BF16, scope=n0)
    kposf = p.sbuf("n0_kp", [128, 128], F32, scope=n0)
    rtmp = p.sbuf("n0_rt", [128, T], BF16, scope=n0)
    w1b = [p.sbuf(f"n0_w1{i}", [128, 2048], BF16, scope=n0) for i in range(2)]
    w2b = p.sbuf("n0_w2", [128, 2048], BF16, scope=n0)
    b1f = p.sbuf("n0_b1", [128, 16], F32, scope=n0)
    b2f = p.sbuf("n0_b2", [128, 1], F32, scope=n0)
    memf = p.sbuf("n0_mem", [128, 8], F32, scope=n0)
    psN = [p.psum(f"n0_ps{i}", [128, 512], scope=n0) for i in range(2)]
    ptrN = p.psum("n0_ptr", [128, 1024], BF16, scope=n0)
    for n in ("n0_cld", "n0_cldg", "n0_kld", "n0_rp", "n0_ci", "n0_w1", "n0_w2",
              "n0_mm1", "n0_h1", "n0_mm2", "n0_cke", "n0_meme", "n0_tr", "n0_tv",
              "n0_bld"):
        p.sem(n)

    def side(wname1, wname2, bname1, bname2, posname, memname, spname, is_k):
        SD = 0 if is_k else 1
        ckdst = ckb if is_k else cvb

        @block.sync
        def _(s):
            if is_k:
                p.dma(s, "n0_cld", out=cmaskS[:],
                      in_=ap_(d["cmask"], 0, [[129, 128], [128 * 129, 16], [1, 129]]))
                p.dma(s, "n0_cld", out=fownS[:], in_=d["fown"])
                p.dma(s, "n0_cld", out=sbF[:], in_=d["sbandF"])
                p.dma(s, "n0_cld", out=sbF0[:], in_=d["sbandF0"])
            else:
                s.wait_ge(p.sems["n0_ci"], 2)
                s.wait_ge(p.sems["n0_h1"], 32)
                s.wait_ge(p.sems["n0_cke"], 2)
                s.wait_ge(p.sems["n0_meme"], 1)
                s.wait_ge(p.sems["n0_rp"], 8)
            p.dma(s, "n0_bld", out=kposf[:], in_=d[posname])
            p.dma(s, "n0_bld", out=b1f[:], in_=d[bname1])
            p.dma(s, "n0_bld", out=b2f[:], in_=d[bname2])
            p.dma(s, "n0_bld", out=memf[:], in_=d[memname])
            for h in range(8):
                p.dma(s, "n0_kld", out=ap_(kfm, h * 2048, [[16384, 128], [1, T]]),
                      in_=ap_(d[spname], h * 128 * T, [[T, 128], [1, T]]))

        @block.gpsimd
        def _(g):
            g = p.wrap(g, "g")
            if is_k:
                p.dma(g, "n0_cldg", out=rcosf[0:64, :], in_=d["rcos"])
                p.dma(g, "n0_cldg", out=ap_(rcosf, 64 * T, [[T, 64], [1, T]]),
                      in_=d["rcos"])
                p.dma(g, "n0_cldg", out=rsinf[0:64, :], in_=d["rsin"])
                p.dma(g, "n0_cldg", out=ap_(rsinf, 64 * T, [[T, 64], [1, T]]),
                      in_=d["rsin"])
                g.wait_ge(p.sems["n0_cldg"], 64)
                cos0 = ap_(rcosf, 0, [[T, 64], [1, T]])
                cos1 = ap_(rcosf, 64 * T, [[T, 64], [1, T]])
                sin0 = ap_(rsinf, 0, [[T, 64], [1, T]])
                sin1 = ap_(rsinf, 64 * T, [[T, 64], [1, T]])
                tmp0 = ap_(rtmp, 0, [[T, 64], [1, T]])
                tmp1 = ap_(rtmp, 64 * T, [[T, 64], [1, T]])
                for h in range(8):
                    g.wait_ge(p.sems["n0_kld"], 16 * (h + 1))
                    khi = ap_(kfm, h * 2048, [[16384, 64], [1, T]])
                    klo = ap_(kfm, 64 * 16384 + h * 2048, [[16384, 64], [1, T]])
                    rhi = ap_(rk, h * 2048, [[16384, 64], [1, T]])
                    rlo = ap_(rk, 64 * 16384 + h * 2048, [[16384, 64], [1, T]])
                    g.tensor_mul(rhi, khi, cos0)
                    g.tensor_mul(tmp0, klo, sin1)
                    g.tensor_sub(rhi, rhi, tmp0)
                    g.tensor_mul(rlo, klo, cos1)
                    g.tensor_mul(tmp1, khi, sin0)
                    g.tensor_add(rlo, rlo, tmp1)
                    p.inc(g.engine_nop(), "n0_rp", 1)
            for half in range(2):
                for o in range(16):
                    oo = SD * 32 + half * 16 + o
                    if o >= 2:
                        g.wait_ge(p.sems["n0_mm1"], oo - 1)
                    elif oo >= 16:
                        g.wait_ge(p.sems["n0_mm1"], SD * 32 + half * 16)
                    p.dma(g, "n0_w1",
                          out=ap_(w1b[o % 2], 0, [[2048, 128], [128, 16], [1, 128]]),
                          in_=d[wname1][o].rearrange("a p q -> p a q"))
                if SD * 2 + half >= 1:
                    g.wait_ge(p.sems["n0_mm2"], SD * 2 + half)
                p.dma(g, "n0_w2",
                      out=ap_(w2b, 0, [[2048, 128], [128, 16], [1, 128]]),
                      in_=d[wname2][0].rearrange("a p q -> p a q"))

        @block.vector
        def _(v):
            v = p.wrap(v, "v")
            v.wait_ge(p.sems["n0_bld"], 64 * (SD + 1))
            if not is_k:
                v.wait_ge(p.sems["n0_mm1"], 32)
            for half in range(2):
                for h in range(8):
                    if half == 0:
                        v.wait_ge(p.sems["n0_kld"], 16 * (SD * 8 + h + 1))
                    v.tensor_add(
                        ap_(ckin, h * 64, [[8192, 128], [512, 16], [1, 64]]),
                        ap_(kfm, h * 2048 + half * 1024,
                            [[16384, 128], [1, 16], [16, 64]]),
                        ap_(kposf, h * 16, [[128, 128], [1, 16], [0, 64]]))
                p.inc(v.engine_nop(), "n0_ci", 1)
                if not is_k and half == 0:
                    v.wait_ge(p.sems["n0_mm2"], 2)
                for o in range(16):
                    oo = SD * 32 + half * 16 + o
                    v.wait_ge(p.sems["n0_mm1"], oo + 1)
                    v.tensor_scalar(h1[:, o * 512:(o + 1) * 512],
                                    psN[o % 2][:], b1f[:, o:o + 1], 0.0,
                                    ALU.add, ALU.max)
                    p.inc(v.engine_nop(), "n0_h1", 1)
                v.wait_ge(p.sems["n0_mm2"], SD * 2 + half + 1)
                v.tensor_scalar(ap_(ckdst, 1 + half * 64, [[1032, 128], [129, 8], [1, 64]]),
                                psN[0][:], b2f[:], None, ALU.add)
                p.inc(v.engine_nop(), "n0_cke", 1)
                if half == 0:
                    v.tensor_copy(ap_(ckdst, 0, [[1032, 128], [129, 8]]), memf[:])
                    p.inc(v.engine_nop(), "n0_meme", 1)
            if not is_k:
                for b in range(16 + 2):
                    v.wait_ge(p.sems["n0_tr"], b + 1)
                    if b < 16:
                        v.tensor_copy(vT[:, b * 1024:(b + 1) * 1024], ptrN[:])
                    elif b == 16:
                        v.tensor_copy(cvT[:], ap_(ptrN, 0, [[1024, 128], [1, 1024]]))
                    else:
                        v.tensor_copy(ap_(cvlast, 0, [[1024, 1], [1, 1024]]),
                                      ap_(ptrN, 0, [[1024, 1], [1, 1024]]))
                        v.tensor_copy(ap_(cvlast, 64 * 1024, [[1024, 1], [1, 1024]]),
                                      ap_(ptrN, 0, [[1024, 1], [1, 1024]]))
                    p.inc(v.engine_nop(), "n0_tv", 1)

        @block.tensor
        def _(t):
            for half in range(2):
                t.wait_ge(p.sems["n0_ci"], SD * 2 + half + 1)
                if SD * 2 + half >= 1:
                    t.wait_ge(p.sems["n0_cke"], SD * 2 + half)
                    t.wait_ge(p.sems["n0_h1"], (SD * 2 + half) * 16)
                for o in range(16):
                    oo = SD * 32 + half * 16 + o
                    t.wait_ge(p.sems["n0_w1"], 16 * (oo + 1))
                    if o >= 2:
                        t.wait_ge(p.sems["n0_h1"], (SD * 2 + half) * 16 + o - 1)
                    for i in range(16):
                        lt = w1b[o % 2][:, i * 128:(i + 1) * 128]
                        mi = t.matmul(psN[o % 2][:],
                                      lt, ckin[:, i * 512:(i + 1) * 512],
                                      start=(i == 0), stop=(i == 15))
                    p.inc(mi, "n0_mm1", 1)
                t.wait_ge(p.sems["n0_w2"], 16 * (SD * 2 + half + 1))
                t.wait_ge(p.sems["n0_h1"], (SD * 2 + half) * 16 + 16)
                for i2 in range(16):
                    lt = w2b[:, i2 * 128:(i2 + 1) * 128]
                    mi = t.matmul(psN[0][:],
                                  lt, h1[:, i2 * 512:(i2 + 1) * 512],
                                  start=(i2 == 0), stop=(i2 == 15))
                p.inc(mi, "n0_mm2", 1)
            if not is_k:
                t.wait_ge(p.sems["n0_kld"], 256)
                t.wait_ge(p.sems["n0_cke"], 4)
                t.wait_ge(p.sems["n0_meme"], 2)
                for b in range(16):
                    if b >= 1:
                        t.wait_ge(p.sems["n0_tv"], b)
                    for s8 in range(8):
                        ci = b * 8 + s8
                        mi = t.transpose(ap_(ptrN, s8 * 128, [[1024, 128], [1, 128]]),
                                         kfm[:, ci * 128:(ci + 1) * 128], p.R["identb"][:])
                    p.inc(mi, "n0_tr", 1)
                t.wait_ge(p.sems["n0_tv"], 16)
                for h in range(8):
                    mi = t.transpose(ap_(ptrN, h * 128, [[1024, 128], [1, 128]]),
                                     cvb[:, h * 129 + 1:h * 129 + 129], p.R["identb"][:])
                p.inc(mi, "n0_tr", 1)
                t.wait_ge(p.sems["n0_tv"], 17)
                for h in range(8):
                    mi = t.transpose(ap_(ptrN, h * 128, [[1024, 1], [1, 128]]),
                                     cvb[:, h * 129:h * 129 + 1], p.R["identb"][:])
                p.inc(mi, "n0_tr", 1)

    side("k_w1", "k_w2", "k_b1", "k_b2", "k_pos", "mem_k", "k_sp", True)
    side("v_w1", "v_w2", "v_b1", "v_b2", "v_pos", "mem_v", "v_sp", False)

    @block.sync
    def _(s):
        s.wait_ge(p.sems["n0_cld"], 64)
        s.wait_ge(p.sems["n0_cldg"], 64)
        s.wait_ge(p.sems["n0_tv"], 18)
    n0.close()



def emit_nsa_attn(p, d, nres):
    block = p.block
    na = p.scope()
    q1 = [p.sbuf(f"n1_q{i}", [128, T], BF16, scope=na) for i in range(2)]
    rq1 = [p.sbuf(f"n1_rq{i}", [128, T], BF16, scope=na) for i in range(2)]
    rtmp = p.sbuf("n1_rt", [128, T], BF16, scope=na)
    cst = p.sbuf("n1_cst", [128, 2064], F32, scope=na)
    Sc = p.sbuf("n1_Sc", [128, 2064], BF16, scope=na)
    imp = p.sbuf("n1_imp", [128, 2064], F32, scope=na)
    denc = p.sbuf("n1_denc", [128, 16], F32, scope=na)
    rcg = p.sbuf("n1_rcg", [128, 16], F32, scope=na)
    selm = p.sbuf("n1_selm", [128, 2048], BF16, scope=na)
    e2 = p.sbuf("n1_e2", [128, 128], F32, scope=na)
    v8 = p.sbuf("n1_v8", [128, 8], F32, scope=na)
    wcol = p.sbuf("n1_wc", [128, 8], F32, scope=na)
    m1t = p.sbuf("n1_m1", [128, 128], BF16, scope=na)
    m2t = p.sbuf("n1_m2", [128, 128], BF16, scope=na)
    S_f = [p.sbuf(f"n1_Sf{i}", [128, 2048], BF16, scope=na) for i in range(2)]
    Pf = [p.sbuf(f"n1_Pf{i}", [128, 2048], BF16, scope=na) for i in range(2)]
    ssb = p.sbuf("n1_ss", [128, 256], BF16, scope=na)
    fcol = p.sbuf("n1_fc", [128, 4], F32, scope=na)
    gcol = p.sbuf("n1_gc", [128, 4], F32, scope=na)
    PcT = p.sbuf("n1_PcT", [128, 8192], BF16, scope=na)
    pcrow = [p.sbuf(f"n1_pcr{i}", [65, 2048], BF16, scope=na) for i in range(2)]
    PT = p.sbuf("n1_PT", [128, 8192], BF16, scope=na)
    ogat = [p.sbuf(f"n1_og{i}", [128, 512], BF16, scope=na) for i in range(2)]
    pO = [p.psum(f"n1_pO{i}", [128, 512], scope=na) for i in range(2)]
    pS = p.psum("n1_pS", [128, 2048], scope=na)
    ptr = p.psum("n1_ptr", [128, 1024], BF16, scope=na)
    for n in ("n1_qld", "n1_gst", "n1_csmm", "n1_csev", "n1_imp", "n1_scx",
              "n1_pcb", "n1_ctr", "n1_ctv", "n1_rp", "n1_selv", "n1_sels",
              "n1_fmm", "n1_fex", "n1_pv", "n1_ftr", "n1_ftv", "n1_omm",
              "n1_oev"):
        p.sem(n)

    rk = p.R["n_rk"]; vT = p.R["n_vT"]; ckb = p.R["n_ckb"]
    cvT = p.R["n_cvT"]; cvlast = p.R["n_cvlast"]
    cmaskS = p.R["n_cmask"]; fownS = p.R["n_fown"]
    sbF = p.R["n_sbF"]; sbF0 = p.R["n_sbF0"]; gtok = p.R["gtok"]
    rcosf = p.R["n0_cos"]; rsinf = p.R["n0_sin"]

    def cs_i(K, g, tc):
        return (4 * K + g) * 16 + tc + 1

    f_i = cs_i

    FT_NB = {}
    FT_CUM = {}
    tot = 0
    for K in range(8):
        for g in range(4):
            for tc in range(16):
                nb = (tc + 8) // 8
                FT_NB[(K, g, tc)] = nb
                tot += nb
                FT_CUM[(K, g, tc)] = tot

    def prev_tile(K, g, tc):
        tc -= 1
        if tc < 0:
            tc = 15
            g -= 1
            if g < 0:
                g = 3
                K -= 1
        return (K, g, tc)

    def pCap(tc):
        return ap_(pS, (tc % 2) * 512, [[2048, 128], [1, 129]])

    def pcrow_ap(g, off, n):
        return ap_(pcrow[g // 2], (g % 2) * 64 * 2048 + off, [[2048, 1], [1, n]])

    @block.sync
    def _(s):
        for K in range(8):
            for g in range(4):
                hg = 4 * K + g
                if hg >= 2:
                    s.wait_ge(p.sems["n1_csmm"], (hg - 2) * 16 + 16)
                    s.wait_ge(p.sems["n1_rp"], hg - 1)
                p.dma(s, "n1_qld", out=q1[hg % 2][:],
                      in_=ap_(d["q_sp"], hg * 128 * T, [[T, 128], [1, T]]))
            for g in range(4):
                hg = 4 * K + g
                for qr in range(4):
                    qi = hg * 4 + qr
                    s.wait_ge(p.sems["n1_oev"], qi + 1)
                    p.dma(s, "n1_gst",
                          out=ap_(d["gat_sp"], hg * 128 * T + qr * 512,
                                  [[T, 128], [1, 512]]),
                          in_=ogat[qi % 2][:])
        s.wait_ge(p.sems["n1_gst"], 16 * 128)

    @block.tensor
    def _(t):
        for K in range(8):
            def emit_ctr(g):
                hg = 4 * K + g
                t.wait_ge(p.sems["n1_pcb"], hg + 1)
                if K >= 1 and g == 0:
                    t.wait_ge(p.sems["n1_ftv"], FT_CUM[(K - 1, 3, 15)])
                for b in range(4):
                    bi = hg * 4 + b + 1
                    if bi >= 2:
                        t.wait_ge(p.sems["n1_ctv"], bi - 1)
                    for q4 in range(4):
                        tc = b * 4 + q4
                        t.transpose(ap_(ptr, q4 * 128, [[1024, 128], [1, 128]]),
                                    Sc[:, tc * 129 + 1:tc * 129 + 129], p.R["identb"][:])
                        mi = t.transpose(ap_(ptr, 512 + q4 * 128, [[1024, 1], [1, 128]]),
                                         Sc[:, tc * 129:tc * 129 + 1],
                                         p.R["identb"][:])
                    p.inc(mi, "n1_ctr", 1)
            for g in range(4):
                hg = 4 * K + g
                t.wait_ge(p.sems["n1_qld"], 16 * (hg + 1))
                if K >= 1 and g == 0:
                    t.wait_ge(p.sems["n1_fex"], 64 * K)
                for tc in range(NCH):
                    ci = cs_i(K, g, tc)
                    if ci >= 3:
                        t.wait_ge(p.sems["n1_csev"], ci - 2)
                    mi = t.matmul(pCap(tc),
                                  q1[hg % 2][:, tc * 128:(tc + 1) * 128],
                                  ckb[:, K * 129:(K + 1) * 129],
                                  start=True, stop=True)
                    p.inc(mi, "n1_csmm", 1)
                if g >= 1:
                    emit_ctr(g - 1)
            emit_ctr(3)
            for g in range(4):
                hg = 4 * K + g
                def emit_ftr(tc):
                    fi = f_i(K, g, tc)
                    t.wait_ge(p.sems["n1_pv"], fi)
                    t.wait_ge(p.sems["n1_ctv"], 16 * (K + 1))
                    nb = FT_NB[(K, g, tc)]
                    for b in range(nb):
                        bti = FT_CUM[(K, g, tc)] - nb + b + 1
                        if bti >= 2:
                            t.wait_ge(p.sems["n1_ftv"], bti - 1)
                        ns_ = min(8, tc + 1 - b * 8)
                        for s8 in range(ns_):
                            mi = t.transpose(
                                ap_(ptr, s8 * 128, [[1024, 128], [1, 128]]),
                                Pf[tc % 2][:, (b * 8 + s8) * 128:(b * 8 + s8 + 1) * 128],
                                p.R["identb"][:])
                        p.inc(mi, "n1_ftr", 1)
                for qr in range(4):
                    qi = hg * 4 + qr
                    for t4 in range(4):
                        tc = qr * 4 + t4
                        fi = f_i(K, g, tc)
                        if fi >= 2:
                            t.wait_ge(p.sems["n1_fex"], fi - 1)
                        t.wait_ge(p.sems["n1_rp"], hg + 1)
                        lt = rq1[hg % 2][:, tc * 128:(tc + 1) * 128]
                        wtot = (tc + 1) * 128
                        for q_ in range((wtot + 511) // 512):
                            wv = min(512, wtot - q_ * 512)
                            mi = t.matmul(ap_(pS, q_ * 512, [[2048, 128], [1, wv]]),
                                          lt,
                                          rk[:, K * 2048 + q_ * 512:K * 2048 + q_ * 512 + wv],
                                          start=True, stop=True)
                        p.inc(mi, "n1_fmm", 1)
                        if t4 >= 1:
                            emit_ftr(tc - 1)
                    emit_ftr(qr * 4 + 3)
                    t.wait_ge(p.sems["n1_ctv"], 4 * (hg + 1))
                    t.wait_ge(p.sems["n1_ftv"], FT_CUM[(K, g, qr * 4 + 3)])
                    if qi >= 2:
                        t.wait_ge(p.sems["n1_oev"], qi - 1)
                    t.matmul(pO[qi % 2][:],
                             cvT[:, K * 128:(K + 1) * 128],
                             PcT[:, g * 2048 + qr * 512:g * 2048 + (qr + 1) * 512],
                             start=True, stop=False, skip_group_check=True)
                    t.matmul(pO[qi % 2][:],
                             ap_(cvlast, (g % 2) * 64 * 1024 + K * 128,
                                 [[1024, 1], [1, 128]]),
                             pcrow_ap(g, qr * 512, 512),
                             start=False, stop=False, skip_group_check=True)
                    njc = qr * 4 + 4
                    for jc in range(njc):
                        a0 = max((jc - qr * 4) * 128, 0)
                        mi = t.matmul(ap_(pO[qi % 2], a0, [[512, 128], [1, 512 - a0]]),
                                      vT[:, (K * 16 + jc) * 128:(K * 16 + jc + 1) * 128],
                                      PT[:, jc * 512 + a0:(jc + 1) * 512],
                                      start=False, stop=(jc == njc - 1),
                                      skip_group_check=True)
                    p.inc(mi, "n1_omm", 1)

    @block.scalar
    def _(s):
        s = p.wrap(s, "s")
        for K in range(8):
            for g in range(4):
                hg = 4 * K + g
                s.wait_ge(p.sems["n1_csev"], cs_i(K, g, 15))
                if hg >= 1:
                    s.wait_ge(p.sems["n1_ctr"], 4 * hg)
                p.inc(s.activation(Sc[:], cst[:], AF.Exp), "n1_scx", 1)
            for tc in range(NCH):
                si = 16 * K + tc
                s.wait_ge(p.sems["n1_selv"], 3 * si + 1)
                s.activation(e2[:], imp[:, tc * 129 + 1:tc * 129 + 129],
                             AF.Exp, bias=wcol[:, 1:2])
                p.inc(s.activation(wcol[:, 3:4], wcol[:, 2:3], AF.Exp), "n1_sels", 1)
                s.wait_ge(p.sems["n1_selv"], 3 * si + 2)
                p.inc(s.activation(wcol[:, 5:6], wcol[:, 4:5], AF.Ln), "n1_sels", 1)
            for g in range(4):
                for tc in range(NCH):
                    fi = f_i(K, g, tc)
                    s.wait_ge(p.sems["n1_fmm"], fi)
                    if fi >= 3:
                        s.wait_ge(p.sems["n1_pv"], fi - 2)
                    p.inc(s.activation(S_f[tc % 2][:, 0:(tc + 1) * 128],
                                       pS[:, 0:(tc + 1) * 128], AF.Exp),
                          "n1_fex", 1)

    @block.vector
    def _(v):
        v = p.wrap(v, "v")
        for K in range(8):
            def emit_ctv(g):
                hg = 4 * K + g
                for b in range(4):
                    bi = hg * 4 + b + 1
                    v.wait_ge(p.sems["n1_ctr"], bi)
                    if K >= 1:
                        v.wait_ge(p.sems["n1_omm"], (4 * (K - 1) + g) * 4 + 4)
                    v.tensor_copy(PcT[:, g * 2048 + b * 512:g * 2048 + (b + 1) * 512],
                                  ptr[:, 0:512])
                    v.tensor_copy(pcrow_ap(g, b * 512, 512), ptr[0:1, 512:1024])
                    p.inc(v.engine_nop(), "n1_ctv", 1)
            for g in range(4):
                hg = 4 * K + g
                for tc in range(NCH):
                    ci = cs_i(K, g, tc)
                    v.wait_ge(p.sems["n1_csmm"], ci)
                    if tc == 0:
                        v.wait_ge(p.sems["n1_scx"], hg)
                        v.wait_ge(p.sems["n1_imp"], hg)
                    v.tensor_add(cst[:, tc * 129:(tc + 1) * 129],
                                 pCap(tc), cmaskS[:, tc * 129:(tc + 1) * 129])
                    p.inc(v.engine_nop(), "n1_csev", 1)
                if g == 0:
                    v.wait_ge(p.sems["n1_sels"], 32 * K)
                    v.tensor_copy(imp[:], cst[:])
                else:
                    v.tensor_add(imp[:], imp[:], cst[:])
                    if g == 3:
                        v.tensor_scalar_mul(imp[:], imp[:], 0.25)
                p.inc(v.engine_nop(), "n1_imp", 1)
                if g >= 1:
                    emit_ctv(g - 1)
                v.wait_ge(p.sems["n1_scx"], hg + 1)
                for tc in range(NCH):
                    v.reduce_sum(denc[:, tc:tc + 1],
                                 Sc[:, tc * 129:(tc + 1) * 129], axis=AX.X)
                v.reciprocal(rcg[:], denc[:])
                v.tensor_mul(rcg[:], rcg[:],
                             ap_(gtok, 3 * hg, [[2048, 128], [128, 16]]))
                for tc in range(NCH):
                    v.tensor_scalar(Sc[:, tc * 129:(tc + 1) * 129],
                                    Sc[:, tc * 129:(tc + 1) * 129],
                                    rcg[:, tc:tc + 1], None, ALU.mult)
                p.inc(v.engine_nop(), "n1_pcb", 1)
            emit_ctv(3)
            for tc in range(NCH):
                si = 16 * K + tc
                v.wait_ge(p.sems["n1_imp"], 4 * K + 4)
                v.max(v8[:], imp[:, tc * 129 + 1:tc * 129 + 129])
                v.tensor_scalar_max(wcol[:, 0:1], v8[:, 0:1], -1000.0)
                v.tensor_scalar_mul(wcol[:, 1:2], wcol[:, 0:1], -1.0)
                v.tensor_scalar_sub(wcol[:, 2:3], wcol[:, 1:2], 1000.0)
                p.inc(v.engine_nop(), "n1_selv", 1)
                v.wait_ge(p.sems["n1_sels"], 2 * si + 1)
                v.reduce_sum(wcol[:, 4:5], e2[:], axis=AX.X)
                v.tensor_add(wcol[:, 4:5], wcol[:, 4:5], wcol[:, 3:4])
                p.inc(v.engine_nop(), "n1_selv", 1)
                v.wait_ge(p.sems["n1_sels"], 2 * si + 2)
                v.tensor_add(wcol[:, 6:7], wcol[:, 0:1], wcol[:, 5:6])
                v.tensor_scalar_sub(wcol[:, 6:7], wcol[:, 6:7], 23.02585093)
                v.tensor_scalar(m1t[:], imp[:, tc * 129 + 1:tc * 129 + 129],
                                v8[:, 3:4], None, ALU.is_ge)
                v.tensor_scalar(m2t[:], imp[:, tc * 129 + 1:tc * 129 + 129],
                                wcol[:, 6:7], None, ALU.is_gt)
                v.tensor_mul(selm[:, tc * 128:(tc + 1) * 128], m1t[:], m2t[:])
                p.inc(v.engine_nop(), "n1_selv", 1)
            for g in range(4):
                hg = 4 * K + g
                def emit_ptc(qr, tc):
                    qi = (4 * K + g) * 4 + qr
                    nb = FT_NB[(K, g, tc)]
                    for b in range(nb):
                        bti = FT_CUM[(K, g, tc)] - nb + b + 1
                        v.wait_ge(p.sems["n1_ftr"], bti)
                        if qi >= 1:
                            v.wait_ge(p.sems["n1_omm"], qi)
                        ns_ = min(8, tc + 1 - b * 8)
                        v.tensor_copy(
                            ap_(PT, (b * 8) * 512 + (tc % 4) * 128,
                                [[8192, 128], [512, ns_], [1, 128]]),
                            ap_(ptr, 0, [[1024, 128], [128, ns_], [1, 128]]))
                        p.inc(v.engine_nop(), "n1_ftv", 1)
                for qr in range(4):
                    qi = hg * 4 + qr
                    for t4 in range(4):
                        tc = qr * 4 + t4
                        fi = f_i(K, g, tc)
                        v.wait_ge(p.sems["n1_fex"], fi)
                        v.wait_ge(p.sems["n1_selv"], 3 * (16 * K + tc) + 3)
                        if fi >= 3:
                            v.wait_ge(p.sems["n1_ftr"],
                                      FT_CUM[prev_tile(*prev_tile(K, g, tc))])
                        Pfs = Pf[tc % 2]
                        Sfs = S_f[tc % 2]
                        v.tensor_mul(Pfs[:, 0:(tc + 1) * 128],
                                     Sfs[:, 0:(tc + 1) * 128],
                                     ap_(selm, tc * 128, [[2048, 128], [1, (tc + 1) * 8], [0, 16]]))
                        v.tensor_mul(m2t[:], Sfs[:, tc * 128:(tc + 1) * 128], fownS[:])
                        v.tensor_add(Pfs[:, tc * 128:(tc + 1) * 128],
                                     Pfs[:, tc * 128:(tc + 1) * 128], m2t[:])
                        v.reduce_sum(fcol[:, 0:1], Pfs[:, 0:(tc + 1) * 128], axis=AX.X)
                        v.reciprocal(fcol[:, 1:2], fcol[:, 0:1])
                        v.tensor_mul(fcol[:, 1:2], fcol[:, 1:2],
                                     gtok[:, tc * 128 + 3 * hg + 1:tc * 128 + 3 * hg + 2])
                        v.tensor_scalar(Pfs[:, 0:(tc + 1) * 128],
                                        Pfs[:, 0:(tc + 1) * 128],
                                        fcol[:, 1:2], None, ALU.mult)
                        if tc == 0:
                            v.tensor_mul(ssb[:, 0:128], Sfs[:, 0:128], sbF0[:])
                            r0, r1 = 0, 128
                            pf0 = 0
                        else:
                            v.tensor_mul(ssb[:, 0:256],
                                         Sfs[:, (tc - 1) * 128:(tc + 1) * 128], sbF[:])
                            r0, r1 = 0, 256
                            pf0 = (tc - 1) * 128
                        v.reduce_sum(gcol[:, 0:1], ssb[:, r0:r1], axis=AX.X)
                        v.reciprocal(gcol[:, 1:2], gcol[:, 0:1])
                        v.tensor_mul(gcol[:, 1:2], gcol[:, 1:2],
                                     gtok[:, tc * 128 + 3 * hg + 2:tc * 128 + 3 * hg + 3])
                        v.tensor_scalar(ssb[:, r0:r1], ssb[:, r0:r1],
                                        gcol[:, 1:2], None, ALU.mult)
                        v.tensor_add(Pfs[:, pf0:pf0 + (r1 - r0)],
                                     Pfs[:, pf0:pf0 + (r1 - r0)], ssb[:, r0:r1])
                        p.inc(v.engine_nop(), "n1_pv", 1)
                        if t4 >= 1:
                            emit_ptc(qr, tc - 1)
                    emit_ptc(qr, qr * 4 + 3)
                    v.wait_ge(p.sems["n1_omm"], qi + 1)
                    if qi >= 2:
                        v.wait_ge(p.sems["n1_gst"], 16 * (qi - 1))
                    v.tensor_copy(ogat[qi % 2][:], pO[qi % 2][:])
                    p.inc(v.engine_nop(), "n1_oev", 1)

    @block.gpsimd
    def _(g_):
        g_ = p.wrap(g_, "g")
        for K in range(8):
            for g in range(4):
                hg = 4 * K + g
                g_.wait_ge(p.sems["n1_qld"], 16 * (hg + 1))
                if hg >= 2:
                    g_.wait_ge(p.sems["n1_fmm"], (hg - 1) * 16)
                qhi = ap_(q1[hg % 2], 0, [[T, 64], [1, T]])
                qlo = ap_(q1[hg % 2], 64 * T, [[T, 64], [1, T]])
                rhi = ap_(rq1[hg % 2], 0, [[T, 64], [1, T]])
                rlo = ap_(rq1[hg % 2], 64 * T, [[T, 64], [1, T]])
                cos0 = ap_(rcosf, 0, [[T, 64], [1, T]])
                cos1 = ap_(rcosf, 64 * T, [[T, 64], [1, T]])
                sin0 = ap_(rsinf, 0, [[T, 64], [1, T]])
                sin1 = ap_(rsinf, 64 * T, [[T, 64], [1, T]])
                tmp0 = ap_(rtmp, 0, [[T, 64], [1, T]])
                tmp1 = ap_(rtmp, 64 * T, [[T, 64], [1, T]])
                g_.tensor_mul(rhi, qhi, cos0)
                g_.tensor_mul(tmp0, qlo, sin1)
                g_.tensor_sub(rhi, rhi, tmp0)
                g_.tensor_mul(rlo, qlo, cos1)
                g_.tensor_mul(tmp1, qhi, sin0)
                g_.tensor_add(rlo, rlo, tmp1)
                p.inc(g_.engine_nop(), "n1_rp", 1)
    na.close()



def emit_nsa_out(p, d, res_old_idx, res_new_idx):
    block = p.block
    ms = p.scope()
    gatb = p.sbuf("no_gat", [128, 32 * 1024], BF16, scope=ms)
    wob = [p.sbuf(f"no_w{i}", [128, 32 * 128], BF16, scope=ms) for i in range(2)]
    rold = [p.sbuf(f"no_r{i}", [128, 1024], BF16, scope=ms) for i in range(2)]
    rnew = [p.sbuf(f"no_n{i}", [128, 1024], BF16, scope=ms) for i in range(2)]
    psE = [p.psum(f"no_ps{i}", [128, 1024], scope=ms) for i in range(2)]
    for n in ("no_gld", "no_wld", "no_rold", "no_mm", "no_ev", "no_st"):
        p.sem(n)

    for th in range(2):
        B = dict(p.cnt)

        @block.sync
        def _(s):
            if th == 1:
                s.wait_ge(p.sems["no_mm"], B["no_mm"])
            for h in range(32):
                p.dma(s, "no_gld", out=gatb[:, h * 1024:(h + 1) * 1024],
                      in_=ap_(d["gat_sp"], h * 128 * T + th * 1024, [[T, 128], [1, 1024]]))

        def _rold(g, o):
            if o >= 2:
                g.wait_ge(p.sems["no_ev"], B["no_ev"] + o - 1)
            elif B["no_ev"]:
                g.wait_ge(p.sems["no_ev"], B["no_ev"])
            src_ = ap_(d["res_sp"],
                       res_old_idx * DC * 128 * T + o * 128 * T + th * 1024,
                       [[T, 128], [1, 1024]])
            p.dma(g, "no_rold", out=rold[o % 2][:], in_=src_)

        @block.gpsimd
        def _(g):
            g = p.wrap(g, "g")
            if B["no_mm"]:
                g.wait_ge(p.sems["no_mm"], B["no_mm"])
            for o in range(32):
                if o >= 2:
                    g.wait_ge(p.sems["no_mm"], B["no_mm"] + o - 1)
                p.dma(g, "no_wld",
                      out=ap_(wob[o % 2], 0, [[32 * 128, 128], [128, 32], [1, 128]]),
                      in_=d["w_o"][o].rearrange("a p q -> p a q"))
                if o >= 2:
                    _rold(g, o - 2)
            _rold(g, 30)
            _rold(g, 31)

        @block.tensor
        def _(t):
            t.wait_ge(p.sems["no_gld"], B["no_gld"] + 16 * 32)
            if B["no_ev"]:
                t.wait_ge(p.sems["no_ev"], B["no_ev"])
            for o in range(32):
                t.wait_ge(p.sems["no_wld"], B["no_wld"] + 16 * (o + 1))
                if o >= 2:
                    t.wait_ge(p.sems["no_ev"], B["no_ev"] + o - 1)
                for k in range(32):
                    lt = wob[o % 2][:, k * 128:(k + 1) * 128]
                    for j in range(2):
                        mi = t.matmul(ap_(psE[o % 2], j * 512, [[1024, 128], [1, 512]]),
                                      lt, gatb[:, k * 1024 + j * 512:k * 1024 + (j + 1) * 512],
                                      start=(k == 0), stop=(k == 31))
                p.inc(mi, "no_mm", 1)

        @block.vector
        def _(v):
            v = p.wrap(v, "v")
            if B["no_st"]:
                v.wait_ge(p.sems["no_st"], B["no_st"])
            for o in range(32):
                v.wait_ge(p.sems["no_mm"], B["no_mm"] + o + 1)
                v.wait_ge(p.sems["no_rold"], B["no_rold"] + 16 * (o + 1))
                if o >= 2:
                    v.wait_ge(p.sems["no_st"], B["no_st"] + 16 * (o - 1))
                v.tensor_add(rnew[o % 2][:], psE[o % 2][:], rold[o % 2][:])
                p.inc(v.engine_nop(), "no_ev", 1)

        @block.sync
        def _(s):
            for o in range(32):
                s.wait_ge(p.sems["no_ev"], B["no_ev"] + o + 1)
                p.dma(s, "no_st",
                      out=ap_(d["res_sp"],
                              res_new_idx * DC * 128 * T + o * 128 * T + th * 1024,
                              [[T, 128], [1, 1024]]),
                      in_=rnew[o % 2][:])
            s.wait_ge(p.sems["no_st"], B["no_st"] + 16 * 32)
    ms.close()


# version 03 (final)
# speedup vs baseline: 1.0423x; 1.0283x over previous
import numpy as np
import ml_dtypes
from contextlib import ExitStack
import concourse.bass as bass
import concourse.mybir as mybir
from concourse.bass_utils import run_bass_kernel_spmd

F32 = mybir.dt.float32
BF16 = mybir.dt.bfloat16
AF = mybir.ActivationFunctionType
ALU = mybir.AluOpType
AX = mybir.AxisListType

T = 2048
D = 4096
DC = 32
NH = 64
DSSM = 8192
NCH = 16
NEGBIG = -1e30
BF = ml_dtypes.bfloat16
ENGS = ("sync", "gpsimd", "vector", "scalar", "tensor")


def ap_(t, off, dims):
    if isinstance(t, bass.AP):
        return bass.AP(t.tensor, t.offset + off, [list(x) for x in dims])
    return bass.AP(t, off, [list(x) for x in dims])


class Prog:
    def __init__(self, nc, ctx, block):
        self.nc = nc
        self.ctx = ctx
        self.block = block
        self.sems = {}
        self.cnt = {}
        self.R = {}
        self._scopes = []
        self.right = ExitStack()
        self.mam_res = ExitStack()
        self.lastop = {}
        self.marks = {}
        self.nbar = 0

    def sem(self, name):
        if name in self.sems:
            return self.sems[name]
        h = self.nc.alloc_semaphore(name=name)
        self.sems[name] = h
        self.cnt[name] = 0
        return h

    def phase_end(self, prefixes=()):
        self.barrier()
        names = [n for n in list(self.sems)
                 if any(n.startswith(pre) for pre in prefixes)]
        if names:
            handles = [self.sems[n] for n in names]
            blk = self.block
            nc = self.nc

            @blk.gpsimd
            def _(g):
                nc.clear_and_free_semaphores(handles)
            for n in names:
                del self.sems[n]
                del self.cnt[n]
            self.barrier()

    def inc(self, instr, name, n=1):
        if name not in self.sems:
            self.sem(name)
        instr.then_inc(self.sems[name], n)
        self.cnt[name] += n
        self.marks[id(instr)] = (instr, name, self.cnt[name])
        return self.cnt[name]

    def wait(self, eng, name, val=None):
        eng.wait_ge(self.sems[name], self.cnt[name] if val is None else val)

    OPNAMES = ("tensor_copy", "tensor_tensor", "tensor_add", "tensor_sub",
               "tensor_mul", "tensor_max", "tensor_scalar", "tensor_single_scalar",
               "tensor_scalar_add", "tensor_scalar_sub", "tensor_scalar_mul",
               "tensor_scalar_max", "tensor_scalar_min", "tensor_relu",
               "tensor_reduce", "reduce_max", "reduce_sum", "scalar_tensor_tensor",
               "tensor_tensor_scan", "memset", "memzero", "reciprocal", "select",
               "copy_predicated", "transpose", "bn_stats", "bn_aggr", "max",
               "max_index", "max_with_indices", "pool", "pool_avg", "pool_max",
               "stream_shuffle", "match_replace", "activation", "sign", "square",
               "sqrt", "mul", "add", "iota", "affine_select",
               "partition_broadcast", "engine_nop")

    def wrap(self, eng, tag):
        return _EngWrap(self, eng, tag)

    def dma(self, q, sem, **kw):
        if sem not in self.sems:
            self.sem(sem)
        if self.cnt[sem]:
            q.wait_ge(self.sems[sem], self.cnt[sem])
        return self.inc(q.dma_start(**kw), sem, 16)

    def sbuf(self, name, shape, dtype, scope=None, side=None):
        cm = self.nc.sbuf_tensor(name, shape, dtype, side=side)
        t = (scope or self.ctx).enter_context(cm)
        self.R[name] = t
        return t

    def psum(self, name, shape, dtype=F32, scope=None):
        cm = self.nc.psum_tensor(name, shape, dtype)
        t = (scope or self.ctx).enter_context(cm)
        self.R[name] = t
        return t

    def scope(self):
        s = ExitStack()
        self._scopes.append(s)
        return s

    def barrier(self):
        self.lastop = {}
        self.nbar += 1
        name = "barrier"
        if name not in self.sems:
            self.sem(name)
        tgt = self.nbar * 5
        blk = self.block

        @blk.sync
        def _(s):
            self.inc(s.nop(), name, 1)
            s.wait_ge(self.sems[name], tgt)

        @blk.gpsimd
        def _(g):
            self.inc(g.engine_nop(), name, 1)
            g.wait_ge(self.sems[name], tgt)

        @blk.vector
        def _(v):
            self.inc(v.engine_nop(), name, 1)
            v.wait_ge(self.sems[name], tgt)

        @blk.scalar
        def _(s):
            self.inc(s.nop(), name, 1)
            s.wait_ge(self.sems[name], tgt)

        @blk.tensor
        def _(t):
            self.inc(t.nop(), name, 1)
            t.wait_ge(self.sems[name], tgt)


class _EngWrap:

    def __init__(self, p, eng, tag):
        self._p = p
        self._e = eng
        self._s = "se_" + tag
        if self._s not in p.sems:
            p.sem(self._s)

    def _chain_wait(self):
        p, e, s = self._p, self._e, self._s
        li = p.lastop.get(self._s)
        if li is None:
            return
        mark = p.marks.get(id(li))
        if mark is None:
            li.then_inc(p.sems[s], 1)
            p.cnt[s] += 1
            mark = (li, s, p.cnt[s])
            p.marks[id(li)] = mark
        e.wait_ge(p.sems[mark[1]], mark[2])

    def __getattr__(self, name):
        e = self._e
        a = getattr(e, name)
        if name in Prog.OPNAMES:
            p = self._p

            def f(*args, **kw):
                self._chain_wait()
                instr = a(*args, **kw)
                p.lastop[self._s] = instr
                return instr
            return f
        if name == "dma_start":
            def f(*args, **kw):
                self._chain_wait()
                return a(*args, **kw)
            return f
        return a



def _pack_lhsT(w):
    K, O = w.shape
    nK = (K + 127) // 128
    nO = (O + 127) // 128
    wp = np.zeros((nK * 128, nO * 128), dtype=np.float32)
    wp[:K, :O] = w
    return np.ascontiguousarray(wp.reshape(nK, 128, nO, 128).transpose(2, 0, 1, 3))


def _rope_perm():
    return np.concatenate([np.arange(0, 128, 2), np.arange(1, 128, 2)])


def prep_consts():
    c = {}
    i = np.arange(128)
    c["mk_mask"] = np.where(i[:, None] <= i[None, :], 0.0, NEGBIG).astype(np.float32)
    c["ident"] = np.eye(128, dtype=np.float32)
    tg = np.arange(T).reshape(16, 128)
    j = np.arange(129)
    allow = (j[None, None, :] == 0) | (tg[:, :, None] >= 16 * j[None, None, :])
    c["cmask"] = np.where(allow, 0.0, NEGBIG).astype(np.float32)
    tl = np.arange(128)
    jl = np.arange(128)
    own = (jl[None, :] // 16 == tl[:, None] // 16) & (jl[None, :] % 16 <= tl[:, None] % 16)
    c["fown"] = own.astype(np.float32)
    c["iotajb"] = np.broadcast_to(np.arange(128, dtype=np.float32), (128, 128)).copy()
    tl = np.arange(128)
    j2 = np.arange(256)
    bf = (j2[None, :] >= tl[:, None] + 64) & (j2[None, :] <= tl[:, None] + 128)
    c["sbandF"] = bf.astype(np.float32)
    j1 = np.arange(128)
    bf0 = (j1[None, :] >= tl[:, None] - 64) & (j1[None, :] <= tl[:, None])
    c["sbandF0"] = bf0.astype(np.float32)
    inv = 1.0 / 10000.0 ** (np.arange(0, 128, 2, dtype=np.float64) / 128.0)
    ang = np.arange(T, dtype=np.float64)[:, None] * inv[None, :]
    c["rcos"] = np.ascontiguousarray(np.cos(ang).T.astype(np.float32))
    c["rsin"] = np.ascontiguousarray(np.sin(ang).T.astype(np.float32))
    return c


def prep_mamba(inp):
    w = {}
    w["w_in"] = _pack_lhsT(np.asarray(inp["L0_attn_a_W_in"], np.float32))
    cw = np.asarray(inp["L0_attn_a_conv_w"], np.float32)
    w["conv_w"] = np.ascontiguousarray(cw.reshape(66, 128, 4).transpose(1, 0, 2).reshape(128, 264))
    w["dt_bias"] = np.asarray(inp["L0_attn_a_dt_bias"], np.float32).reshape(64, 1)
    w["a_log"] = np.asarray(inp["L0_attn_a_A_log"], np.float32).reshape(64, 1)
    w["d_param"] = np.asarray(inp["L0_attn_a_D_param"], np.float32).reshape(1, 64)
    w["norm_w"] = np.ascontiguousarray(
        np.asarray(inp["L0_attn_a_norm_weight"], np.float32).reshape(64, 128).T)
    w["w_out"] = _pack_lhsT(np.asarray(inp["L0_attn_a_W_out"], np.float32))
    return w


def prep_ffn(inp):
    w = {}
    w["ffn_w"] = _pack_lhsT(np.asarray(inp["L0_ffn_W"], np.float32).T)
    w["ffn_b"] = np.ascontiguousarray(np.asarray(inp["L0_ffn_b"], np.float32).reshape(32, 128).T)
    A = np.asarray(inp["L0_ffn_A"], np.float32).reshape(128, 4096)
    w["ffn_a"] = _pack_lhsT(A.T)
    w["ffn_bmat"] = np.ascontiguousarray(
        np.asarray(inp["L0_ffn_B"], np.float32).transpose(0, 2, 1).reshape(128, 4096))
    w["ffn_rw"] = np.ascontiguousarray(
        np.asarray(inp["L0_ffn_router_W"], np.float32).T.reshape(32, 128, 8))
    w["ffn_rb"] = np.asarray(inp["L0_ffn_router_b"], np.float32).reshape(8, 1)
    return w


def prep_nsa(inp):
    w = {}
    perm = _rope_perm()
    wqkv = np.asarray(inp["L0_attn_b_w_qkv"], np.float32).copy()
    for h in range(32):
        wqkv[h * 128:(h + 1) * 128] = wqkv[h * 128 + perm]
    for h in range(8):
        base = 4096 + h * 128
        wqkv[base:base + 128] = wqkv[base + perm]
    wqkv[:4096] *= 128.0 ** -0.5
    w["w_qkv"] = _pack_lhsT(wqkv.T)
    w["w_o"] = _pack_lhsT(np.asarray(inp["L0_attn_b_w_o"], np.float32).T)
    kp = np.asarray(inp["L0_attn_b_k_pos"], np.float32)[:, :, perm]
    w["k_pos"] = np.ascontiguousarray(kp.transpose(2, 0, 1).reshape(128, 128))
    vp = np.asarray(inp["L0_attn_b_v_pos"], np.float32)
    w["v_pos"] = np.ascontiguousarray(vp.transpose(2, 0, 1).reshape(128, 128))
    iperm = (np.arange(2048).reshape(16, 128)[:, perm]).reshape(2048)
    w["k_w1"] = _pack_lhsT(np.asarray(inp["L0_attn_b_k_w1"], np.float32)[:, iperm].T)
    w["k_b1"] = np.ascontiguousarray(np.asarray(inp["L0_attn_b_k_b1"], np.float32).reshape(16, 128).T)
    w["k_w2"] = _pack_lhsT(np.asarray(inp["L0_attn_b_k_w2"], np.float32)[perm].T)
    w["k_b2"] = np.asarray(inp["L0_attn_b_k_b2"], np.float32)[perm].reshape(128, 1)
    w["v_w1"] = _pack_lhsT(np.asarray(inp["L0_attn_b_v_w1"], np.float32).T)
    w["v_b1"] = np.ascontiguousarray(np.asarray(inp["L0_attn_b_v_b1"], np.float32).reshape(16, 128).T)
    w["v_w2"] = _pack_lhsT(np.asarray(inp["L0_attn_b_v_w2"], np.float32).T)
    w["v_b2"] = np.asarray(inp["L0_attn_b_v_b2"], np.float32).reshape(128, 1)
    mem = np.asarray(inp["L0_attn_b_mem_kv"], np.float32)
    w["mem_k"] = np.ascontiguousarray(mem[0, :, 0, :][:, perm].T)
    w["mem_v"] = np.ascontiguousarray(mem[1, :, 0, :].T)
    w["w_gate"] = _pack_lhsT(np.asarray(inp["L0_attn_b_w_gate"], np.float32).T)
    w["b_gate"] = np.concatenate(
        [np.asarray(inp["L0_attn_b_b_gate"], np.float32), np.zeros(32, np.float32)]).reshape(128, 1)
    return w



def declare_dram(nc, parts):
    d = {}

    def inp(name, shape):
        d[name] = nc.dram_tensor(name, list(shape), F32, kind="ExternalInput").ap()

    def scr(name, shape, dtype=BF16):
        d[name] = nc.dram_tensor(name, list(shape), dtype, kind="Internal").ap()

    inp("x0", (D, T))
    inp("ident", (128, 128))
    if "mamba" in parts:
        inp("w_in", (131, 32, 128, 128)); inp("conv_w", (128, 264))
        inp("dt_bias", (64, 1)); inp("a_log", (64, 1)); inp("d_param", (1, 64))
        inp("norm_w", (128, 64)); inp("w_out", (32, 64, 128, 128))
        inp("mk_mask", (128, 128))
        scr("z_sp", (4, 64, 128, 512))
        scr("xact_sp", (16, DSSM, 128))
        scr("y_sp", (16, 64, 128, 128))
        scr("cum_sp", (2, 7, 64, 128))
        scr("dbg_sp", (64, T), F32)
    if "ffn" in parts:
        inp("ffn_w", (32, 32, 128, 128)); inp("ffn_b", (128, 32)); inp("ffn_a", (1, 32, 128, 128))
        inp("ffn_bmat", (128, 4096)); inp("ffn_rw", (32, 128, 8)); inp("ffn_rb", (8, 1))
        scr("r8", (1, 8), F32)
    if "nsa" in parts:
        inp("w_qkv", (48, 32, 128, 128)); inp("w_o", (32, 32, 128, 128))
        inp("k_pos", (128, 128)); inp("v_pos", (128, 128))
        inp("k_w1", (16, 16, 128, 128)); inp("k_b1", (128, 16))
        inp("k_w2", (1, 16, 128, 128)); inp("k_b2", (128, 1))
        inp("v_w1", (16, 16, 128, 128)); inp("v_b1", (128, 16))
        inp("v_w2", (1, 16, 128, 128)); inp("v_b2", (128, 1))
        inp("mem_k", (128, 8)); inp("mem_v", (128, 8))
        inp("w_gate", (1, 32, 128, 128)); inp("b_gate", (128, 1))
        inp("cmask", (16, 128, 129)); inp("fown", (128, 128)); inp("iotajb", (128, 128))
        inp("sbandF", (128, 256)); inp("sbandF0", (128, 128))
        inp("rcos", (64, T)); inp("rsin", (64, T))
        scr("q_sp", (32, 128, T))
        scr("k_sp", (8, 128, T))
        scr("v_sp", (8, 128, T))
        scr("gat_sp", (32, 128, T))
    scr("res_sp", (2, DC, 128, T))
    d["out"] = nc.dram_tensor("out", [D, T], F32, kind="ExternalOutput").ap()
    return d



def emit_init(p, d):
    block = p.block
    identf = p.sbuf("identf", [128, 128], F32, scope=p.right, side="right")
    identb = p.sbuf("identb", [128, 128], BF16, scope=p.right, side="right")
    p.sbuf("onesrowf", [1, 128], F32, scope=p.right, side="right")
    p.sbuf("onesrowb", [1, 128], BF16, scope=p.right, side="right")
    p.sbuf("onescolb", [128, 1], BF16, scope=p.right, side="right")
    p.sem("ini_s"); p.sem("ini_g"); p.sem("ini")
    if "norm_w" in d:
        nw = p.sbuf("m3_nw", [128, 64], F32, scope=p.right, side="right")

        @block.sync
        def _(s):
            p.dma(s, "ini_s", out=nw[:], in_=d["norm_w"])

    inis = p.cnt["ini_s"]

    @block.sync
    def _(s):
        p.dma(s, "ini_s", out=identf[:], in_=d["ident"])

    @block.gpsimd
    def _(g):
        g = p.wrap(g, "g")
        g.wait_ge(p.sems["ini_s"], inis + 16)
        p.dma(g, "ini_g", out=identb[:], in_=d["ident"])

    @block.vector
    def _(v):
        v = p.wrap(v, "v")
        v.memset(p.R["onesrowf"][:], 1.0)
        v.memset(p.R["onesrowb"][:], 1.0)
        v.memset(p.R["onescolb"][:], 1.0)
        p.inc(v.engine_nop(), "ini", 1)



def emit_boundary(p, d, idx, src, scope):
    import os
    BSTEP = int(os.environ.get("BSTEP", "9"))
    block = p.block
    nc = p.nc
    xln = p.sbuf(f"xln{idx}", [128, DC * T], BF16, scope=scope)
    bs = p.scope()
    x0a = p.sbuf(f"bn{idx}_x0a", [128, T], F32, scope=bs)
    x0b_ = p.sbuf(f"bn{idx}_x0b", [128, T], F32, scope=bs)
    sq = p.sbuf(f"bn{idx}_sq", [128, T], BF16, scope=bs)
    rows = p.sbuf(f"bn{idx}_rows", [1, 3 * T], F32, scope=bs)
    mub = p.sbuf(f"bn{idx}_mub", [128, T], BF16, scope=bs)
    rstdb = p.sbuf(f"bn{idx}_rstdb", [128, T], BF16, scope=bs)
    pstat = p.psum(f"bn{idx}_pstat", [128, 4096], scope=bs)
    ld = f"bn{idx}_ld"; cast = f"bn{idx}_cast"; mm = f"bn{idx}_mm"
    sqs = f"bn{idx}_sq"; fin = f"bn{idx}_fin"; fins = f"bn{idx}_fins"
    bct = f"bn{idx}_bct"; bcs = f"bn{idx}_bcs"; nrm = f"bn{idx}_nrm"
    for s_ in (ld, cast, mm, sqs, fin, fins, bct, bcs, nrm):
        p.sem(s_)

    if src[0] == "x0":
        @block.sync
        def _(s):
            for c in range(DC):
                if c >= 2:
                    p.wait(s, cast, 16 * (c - 1))
                p.dma(s, ld, out=(x0a if c % 2 == 0 else x0b_)[:],
                                  in_=d["x0"][c * 128:(c + 1) * 128, :])

        @block.gpsimd
        def _(g):
            g = p.wrap(g, "g")
            for c in range(DC):
                p.wait(g, ld, 16 * (c + 1))
                p.dma(g, cast, out=ap_(xln, c * T, [[DC * T, 128], [1, T]]),
                                  in_=(x0a if c % 2 == 0 else x0b_)[:])
    else:
        ri = src[1]

        @block.sync
        def _(s):
            for c in range(DC):
                p.dma(s, cast, out=ap_(xln, c * T, [[DC * T, 128], [1, T]]),
                                  in_=d["res_sp"][ri, c])

    if BSTEP < 2:
        return xln

    @block.scalar
    def _(s):
        s = p.wrap(s, "s")
        for c in range(DC):
            p.wait(s, cast, 16 * (c + 1))
            if c >= 1:
                p.wait(s, mm, 2 * c)
            p.inc(s.activation(sq[:], ap_(xln, c * T, [[DC * T, 128], [1, T]]), AF.Square),
                  sqs, 1)

    @block.tensor
    def _(t):
        t.wait_ge(p.sems["ini"], 1)
        for c in range(DC):
            p.wait(t, cast, 16 * (c + 1))
            for j in range(4):
                mi = t.matmul(ap_(pstat, j * 512, [[4096, 1], [1, 512]]),
                              p.R["onescolb"][:],
                              ap_(xln, c * T + j * 512, [[DC * T, 128], [1, 512]]),
                              start=(c == 0), stop=(c == DC - 1))
            p.inc(mi, mm, 1)
            p.wait(t, sqs, c + 1)
            for j in range(4):
                mi = t.matmul(ap_(pstat, 2048 + j * 512, [[4096, 1], [1, 512]]),
                              p.R["onescolb"][:], sq[:, j * 512:(j + 1) * 512],
                              start=(c == 0), stop=(c == DC - 1))
            p.inc(mi, mm, 1)

    if BSTEP < 3:
        return xln
    r_mu = rows[:, 0:T]
    r_v = rows[:, T:2 * T]
    r_t = rows[:, 2 * T:3 * T]

    @block.vector
    def _(v):
        v = p.wrap(v, "v")
        p.wait(v, mm, 2 * DC)
        v.tensor_copy(r_mu, ap_(pstat, 0, [[4096, 1], [1, T]]))
        v.tensor_copy(r_v, ap_(pstat, 2048, [[4096, 1], [1, T]]))
        v.tensor_scalar_mul(r_mu, r_mu, 1.0 / D)
        v.tensor_scalar_mul(r_v, r_v, 1.0 / D)
        v.tensor_mul(r_t, r_mu, r_mu)
        v.tensor_sub(r_v, r_v, r_t)
        v.tensor_scalar_add(r_v, r_v, 1e-5)
        p.inc(v.engine_nop(), fin, 1)

    @block.scalar
    def _(s):
        s = p.wrap(s, "s")
        p.wait(s, fin, 1)
        p.inc(s.activation(r_v, r_v, AF.Sqrt), fins, 1)

    @block.vector
    def _(v):
        v = p.wrap(v, "v")
        p.wait(v, fins, 1)
        v.reciprocal(r_v, r_v)
        v.tensor_mul(r_t, r_mu, r_v)
        p.inc(v.engine_nop(), fin, 1)

    if BSTEP < 4:
        return xln
    @block.tensor
    def _(t):
        p.wait(t, fin, 2)
        for j in range(4):
            t.matmul(ap_(pstat, j * 512, [[4096, 128], [1, 512]]),
                     p.R["onesrowf"][:], r_v[:, j * 512:(j + 1) * 512], start=True, stop=True)
            mi = t.matmul(ap_(pstat, 2048 + j * 512, [[4096, 128], [1, 512]]),
                          p.R["onesrowf"][:], r_t[:, j * 512:(j + 1) * 512],
                          start=True, stop=True)
        p.inc(mi, bct, 1)

    @block.scalar
    def _(s):
        s = p.wrap(s, "s")
        p.wait(s, bct, 1)
        s.copy(rstdb[:], ap_(pstat, 0, [[4096, 128], [1, T]]))
        p.inc(s.copy(mub[:], ap_(pstat, 2048, [[4096, 128], [1, T]])), bcs, 1)

    if BSTEP < 5:
        return xln
    @block.vector
    def _(v):
        v = p.wrap(v, "v")
        p.wait(v, bcs, 1)
        for c in range(DC):
            xv = ap_(xln, c * T, [[DC * T, 128], [1, T]])
            v.tensor_mul(xv, xv, rstdb[:])
            v.tensor_sub(xv, xv, mub[:])
        p.inc(v.engine_nop(), nrm, 1)

    bs.close()
    return xln



def emit_mamba_m1(p, d, xln):
    block = p.block
    ms = p.scope()
    wbuf = [p.sbuf(f"m1_w{i}", [128, 32 * 128], BF16, scope=ms) for i in range(2)]
    zrow = [p.sbuf(f"m1_z{i}", [128, T], BF16, scope=ms) for i in range(2)]
    xrow = [p.sbuf(f"m1_x{i}", [128, T], BF16, scope=ms) for i in range(2)]
    crow = [p.sbuf(f"m1_c{i}", [128, T], F32, scope=ms) for i in range(2)]
    convw = p.sbuf("m1_convw", [128, 264], F32, scope=ms)
    ps = [p.psum(f"m1_ps{i}", [128, 2048], scope=ms) for i in range(2)]
    Bfm = p.sbuf("Bfm", [128, T], BF16, scope=p.mam_res, side="right")
    Cfm = p.sbuf("Cfm", [128, T], BF16, scope=p.mam_res, side="right")
    dtraw = p.sbuf("dtraw", [64, T], F32, scope=p.mam_res, side="right")
    for s_ in ("m1_wld", "m1_mm", "m1_ev", "m1_evd", "m1_cv", "m1_sl", "m1_st",
               "m1_cwld"):
        p.sem(s_)

    NO = 131
    st_after = []
    tot = 0
    for u in range(128):
        tot += 64 if u < 64 else 16
        st_after.append(tot)

    @block.sync
    def _(s):
        p.dma(s, "m1_cwld", out=convw[:], in_=d["conv_w"])

    @block.gpsimd
    def _(g):
        g = p.wrap(g, "g")
        for o in range(NO):
            if o >= 2:
                g.wait_ge(p.sems["m1_mm"], o - 1)
            p.dma(g, "m1_wld", 
                out=ap_(wbuf[o % 2], 0, [[4096, 128], [128, 32], [1, 128]]),
                in_=d["w_in"][o].rearrange("a p q -> p a q"))

    @block.tensor
    def _(t):
        for o in range(NO):
            t.wait_ge(p.sems["m1_wld"], 16 * (o + 1))
            if o >= 2:
                t.wait_ge(p.sems["m1_ev"], o - 1)
            w = wbuf[o % 2]
            pso = ps[o % 2]
            for dd in range(32):
                lt = w[:, dd * 128:(dd + 1) * 128]
                for j in range(4):
                    mi = t.matmul(ap_(pso, j * 512, [[2048, 128], [1, 512]]),
                                  lt, ap_(xln, dd * T + j * 512, [[DC * T, 128], [1, 512]]),
                                  start=(dd == 0), stop=(dd == 31))
            p.inc(mi, "m1_mm", 1)

    @block.scalar
    def _(s):
        s = p.wrap(s, "s")
        for o in range(NO):
            if o < 130:
                s.wait_ge(p.sems["m1_mm"], o + 1)
                pso = ps[o % 2]
                if o < 64:
                    u = o
                    if u >= 2:
                        s.wait_ge(p.sems["m1_st"], st_after[u - 2])
                    p.inc(s.copy(zrow[u % 2][:], pso[:]), "m1_ev", 1)
                else:
                    nx = o - 64
                    if nx >= 2:
                        s.wait_ge(p.sems["m1_cv"], nx - 1)
                    p.inc(s.copy(xrow[nx % 2][:], pso[:]), "m1_ev", 1)
            cc = o - 65
            if 0 <= cc < 66:
                s.wait_ge(p.sems["m1_cv"], cc + 1)
                if cc < 64:
                    u = 64 + cc
                    if u >= 2:
                        s.wait_ge(p.sems["m1_st"], st_after[u - 2])
                    dst = zrow[u % 2]
                elif cc == 64:
                    dst = Bfm
                else:
                    dst = Cfm
                p.inc(s.activation(dst[:], crow[cc % 2][:], AF.Silu), "m1_sl", 1)

    @block.vector
    def _(v):
        v = p.wrap(v, "v")
        v.wait_ge(p.sems["m1_cwld"], 16)
        for o in range(64, NO):
            if o < 130:
                nx = o - 64
                cc = nx
                v.wait_ge(p.sems["m1_ev"], o + 1)
                if nx >= 2:
                    v.wait_ge(p.sems["m1_sl"], nx - 1)
                xr = xrow[nx % 2]
                cr = crow[nx % 2]
                v.tensor_scalar(cr[:], xr[:], convw[:, cc * 4 + 3:cc * 4 + 4], None, ALU.mult)
                for k_ in (2, 1, 0):
                    sh = 3 - k_
                    v.scalar_tensor_tensor(cr[:, sh:T], xr[:, 0:T - sh],
                                           convw[:, cc * 4 + k_:cc * 4 + k_ + 1],
                                           cr[:, sh:T], ALU.mult, ALU.add)
                p.inc(v.engine_nop(), "m1_cv", 1)
            else:
                v.wait_ge(p.sems["m1_mm"], NO)
                v.tensor_copy(dtraw[:], ap_(ps[o % 2], 0, [[2048, 64], [1, T]]))
                p.inc(v.engine_nop(), "m1_evd", 1)

    @block.sync
    def _(s):
        for o in range(NO):
            if o < 64:
                u = o
                s.wait_ge(p.sems["m1_ev"], o + 1)
                for j in range(4):
                    p.dma(s, "m1_st", out=d["z_sp"][j, o],
                                      in_=zrow[u % 2][:, j * 512:(j + 1) * 512])
            elif o < 128:
                cc = o - 64
                u = 64 + cc
                s.wait_ge(p.sems["m1_sl"], cc + 1)
                p.dma(s, "m1_st", 
                    out=ap_(d["xact_sp"], cc * 128 * 128,
                            [[128, 128], [DSSM * 128, 16], [1, 128]]),
                    in_=ap_(zrow[u % 2], 0, [[T, 128], [128, 16], [1, 128]]))
    ms.close()
    return Bfm, Cfm, dtraw



def emit_mamba_m3pre(p, d):
    block = p.block
    ps_ = p.scope()
    dtsp = p.sbuf("m3_dtsp", [64, T], F32, scope=ps_)
    tmpa = p.sbuf("m3_tmpa", [64, T], F32, scope=ps_)
    tmpb = p.sbuf("m3_tmpb", [64, T], F32, scope=ps_)
    pD1 = p.psum("m3pre_p", [128, 64], scope=ps_)
    dtec = p.sbuf("m3_dtec", [128, T], BF16, scope=p.mam_res, side="right")
    dtspb = dtec[0:64, :]
    ec_hm = dtec[64:128, :]
    p.R["m3_dtspb"] = dtspb
    p.R["m3_echm"] = ec_hm
    cumpair = p.sbuf("m3_cumpair", [128, T], F32, scope=p.mam_res, side="right")
    cum = cumpair[0:64, :]
    negcum = cumpair[64:128, :]
    p.R["m3_cum"] = cum
    p.R["m3_negcum"] = negcum
    cumL = p.sbuf("m3_cumL", [64, 16], F32, scope=p.mam_res, side="right")
    hpk = [p.sbuf(f"m3_hml{i}", [128, T], BF16, scope=p.mam_res, side="right")
           for i in range(3)]
    for i, n in enumerate(("phi", "pmid", "plo", "nhi", "nmid", "nlo")):
        p.R["m3_" + n] = hpk[i % 3][0 if i < 3 else 64:64 if i < 3 else 128, :]
    dbias = p.sbuf("m3_dbias", [64, 1], F32, scope=p.mam_res, side="right")
    alog = p.sbuf("m3_alog", [64, 1], F32, scope=p.mam_res, side="right")
    dprow = p.sbuf("m3_dprow", [1, 64], F32, scope=p.mam_res, side="right")
    D_bc = p.sbuf("m3_Dbc", [128, 64], F32, scope=p.mam_res, side="right")
    mkmask = p.sbuf("mkmask", [128, 128], F32, scope=p.mam_res, side="right")
    for s_ in ("m3p_ld", "m3p_as", "m3p_av", "m3p_bs", "m3p_bv", "m3p_cv",
               "m3p_cs", "m3p_ct"):
        p.sem(s_)
    hml = {n: p.R[f"m3_{n}"] for n in ("phi", "pmid", "plo", "nhi", "nmid", "nlo")}
    del dtec, cumpair

    @block.sync
    def _(s):
        p.dma(s, "m3p_ld", out=dbias[:], in_=d["dt_bias"])
        p.dma(s, "m3p_ld", out=alog[:], in_=d["a_log"])
        p.dma(s, "m3p_ld", out=dprow[:], in_=d["d_param"])
        p.dma(s, "m3p_ld", out=mkmask[:], in_=d["mk_mask"])

    @block.scalar
    def _(s):
        s = p.wrap(s, "s")
        s.wait_ge(p.sems["m3p_ld"], 64)
        s.wait_ge(p.sems["m1_ev"], 130)
        s.wait_ge(p.sems["m1_evd"], 1)
        s.add(dtsp[:], p.R["dtraw"][:], dbias[:])
        p.inc(s.activation(tmpa[:], dtsp[:], AF.Abs), "m3p_as", 1)

    @block.vector
    def _(v):
        v = p.wrap(v, "v")
        v.wait_ge(p.sems["m3p_as"], 1)
        v.tensor_scalar_mul(tmpa[:], tmpa[:], -1.0)
        p.inc(v.engine_nop(), "m3p_av", 1)

    @block.scalar
    def _(s):
        s = p.wrap(s, "s")
        s.wait_ge(p.sems["m3p_av"], 1)
        s.activation(tmpa[:], tmpa[:], AF.Exp)
        s.activation(tmpa[:], tmpa[:], AF.Ln, bias=1.0)
        p.inc(s.activation(alog[:], alog[:], AF.Exp), "m3p_as", 1)

    @block.vector
    def _(v):
        v = p.wrap(v, "v")
        v.wait_ge(p.sems["m3p_as"], 2)
        v.tensor_scalar_max(tmpb[:], dtsp[:], 0.0)
        v.tensor_add(dtsp[:], tmpa[:], tmpb[:])
        v.tensor_copy(dtspb[:], dtsp[:])
        v.tensor_scalar_mul(alog[:], alog[:], -1.0)
        v.tensor_scalar(tmpa[:], dtsp[:], alog[:], None, ALU.mult)
        for c in range(NCH):
            v.tensor_tensor_scan(cum[:, c * 128:(c + 1) * 128],
                                 tmpa[:, c * 128:(c + 1) * 128],
                                 tmpa[:, c * 128:(c + 1) * 128],
                                 0.0, ALU.add, ALU.bypass)
        v.tensor_copy(cum[:], cum[:])
        v.tensor_copy(cumL[:], ap_(cum, 127, [[T, 64], [128, 16]]))
        v.tensor_copy(hml["phi"][:], cum[:])
        v.tensor_copy(tmpa[:], hml["phi"][:])
        v.tensor_sub(tmpb[:], cum[:], tmpa[:])
        v.tensor_copy(hml["pmid"][:], tmpb[:])
        v.tensor_copy(tmpa[:], hml["pmid"][:])
        v.tensor_sub(tmpb[:], tmpb[:], tmpa[:])
        v.tensor_copy(hml["plo"][:], tmpb[:])
        for n in ("hi", "mid", "lo"):
            v.tensor_scalar_mul(hml["n" + n][:], hml["p" + n][:], -1.0)
        v.tensor_scalar_mul(negcum[:], cum[:], -1.0)
        p.inc(v.engine_nop(), "m3p_cv", 1)

    @block.scalar
    def _(s):
        s = p.wrap(s, "s")
        s.wait_ge(p.sems["m3p_cv"], 1)
        p.inc(s.activation(ec_hm[:], cum[:], AF.Exp), "m3p_cs", 1)

    @block.tensor
    def _(t):
        t.wait_ge(p.sems["m3p_cv"], 1)
        t.wait_ge(p.sems["ini"], 1)
        t.wait_ge(p.sems["ini_g"], 16)
        t.wait_ge(p.sems["ini_s"], p.cnt["ini_s"])
        t.wait_ge(p.sems["m3p_ld"], 64)
        p.inc(t.matmul(pD1[:], p.R["onesrowf"][:], dprow[:], start=True, stop=True),
              "m3p_ct", 1)

    @block.vector
    def _(v):
        v = p.wrap(v, "v")
        v.wait_ge(p.sems["m3p_ct"], 1)
        v.tensor_copy(D_bc[:], pD1[:])
        p.inc(v.engine_nop(), "m3p_cv", 1)
    ps_.close()



def emit_mamba_m3(p, d):
    block = p.block
    ms = p.scope()
    GW = 1024
    xtp = [p.sbuf(f"m3_xtp{i}", [128, GW], BF16, scope=ms) for i in range(2)]
    xfm = [p.sbuf(f"m3_xfm{i}", [128, GW], BF16, scope=ms) for i in range(2)]
    dtx = [p.sbuf(f"m3_dtx{i}", [128, GW], BF16, scope=ms) for i in range(2)]
    dtxs = [p.sbuf(f"m3_dtxs{i}", [128, GW], BF16, scope=ms) for i in range(2)]
    Cs = [p.sbuf(f"m3_Cs{i}", [128, GW], BF16, scope=ms) for i in range(2)]
    Sg = [p.sbuf(f"m3_S{i}", [128, GW], BF16, scope=ms) for i in range(2)]
    sfp = [p.sbuf(f"m3_sfp{i}", [128, GW], F32, scope=ms) for i in range(2)]
    xd = [p.sbuf(f"m3_xd{i}", [128, GW], BF16, scope=ms) for i in range(2)]
    ybuf = [p.sbuf(f"m3_yb{i}", [128, GW], BF16, scope=ms) for i in range(2)]
    stgA = p.sbuf("m3_stgA", [6, 8192], BF16, scope=ms)
    stgB = p.sbuf("m3_stgB", [6, 8192], BF16, scope=ms)
    ecrowT = p.sbuf("m3_ecrow", [1, 8192], BF16, scope=ms)
    dtT = [p.sbuf(f"m3_dtT{i}", [128, 64], BF16, scope=ms) for i in range(2)]
    edT = [p.sbuf(f"m3_edT{i}", [128, 64], BF16, scope=ms) for i in range(2)]
    Btp = [p.sbuf(f"m3_Btp{i}", [128, 128], BF16, scope=ms) for i in range(2)]
    Gf = [p.sbuf(f"m3_Gf{i}", [128, 128], F32, scope=ms) for i in range(2)]
    ed_hm = [p.sbuf(f"m3_edhm{i}", [64, 128], BF16, scope=ms) for i in range(2)]
    decayrow = [p.sbuf(f"m3_dr{i}", [1, 64], F32, scope=ms) for i in range(2)]
    decay_bc = [p.sbuf(f"m3_dbc{i}", [128, 64], F32, scope=ms) for i in range(2)]
    expL = [p.sbuf(f"m3_eL{i}", [64, 1], F32, scope=ms) for i in range(2)]
    state = p.sbuf("m3_state", [128, DSSM], F32, scope=ms)
    statebf = p.sbuf("m3_statebf", [128, DSSM], BF16, scope=ms)
    pBig = p.psum("m3_pBig", [128, 2048], scope=ms)
    pD = p.psum("m3_pD", [128, 1024], scope=ms)
    pG = p.psum("m3_pG", [128, 512], scope=ms)
    ptr = p.psum("m3_ptr", [128, 1024], BF16, scope=ms)
    for s_ in ("s3_sb", "s3_pre", "s3_ex", "s3_ld", "s3_yst", "s3_pg", "s3_pv",
               "s3_tg", "s3_tv", "s3_cs", "s3_csv", "s3_gam", "s3_sp", "s3_sS",
               "s3_ymm", "s3_dmm", "s3_ya", "s3_dx", "s3_xd", "s3_dts",
               "s3_smg", "s3_su", "s3_iv", "s3_ig"):
        p.sem(s_)

    cum = p.R["m3_cum"]; cumL = p.R["m3_cumL"]
    hml = {n: p.R[f"m3_{n}"] for n in ("phi", "pmid", "plo", "nhi", "nmid", "nlo")}
    ec_hm = p.R["m3_echm"]
    Bfm = p.R["Bfm"]; Cfm = p.R["Cfm"]
    mkm = p.R["mkmask"]
    D_bc = p.R["m3_Dbc"]
    dtspb = p.R["m3_dtspb"]

    @block.vector
    def _(v):
        v = p.wrap(v, "v")
        v.memset(state[:], 0.0)
        v.memset(statebf[:], 0.0)
        p.inc(v.engine_nop(), "s3_iv", 1)

    @block.gpsimd
    def _(g):
        g = p.wrap(g, "g")
        g.wait_ge(p.sems["m3p_cv"], 2)
        g.wait_ge(p.sems["m3p_cs"], 1)
        g.memset(stgA[0:6, :], 1.0)
        g.memset(stgB[0:6, :], 1.0)
        p.inc(g.engine_nop(), "s3_ig", 1)

    @block.scalar
    def _(s):
        s = p.wrap(s, "s")
        s.wait_ge(p.sems["s3_iv"], 1)
        s.wait_ge(p.sems["s3_ig"], 1)
        names = ["nhi", "nmid", "nlo", "phi", "pmid", "plo"]
        for c in range(NCH):
            sl = c % 2
            if c >= 1:
                s.wait_ge(p.sems["s3_gam"], 8 * c)
                s.wait_ge(p.sems["s3_cs"], 8 * c)
            if c >= 2:
                s.wait_ge(p.sems["s3_sb"], 224 * (c - 1))
            for i, n in enumerate(names):
                p.dma(s, "s3_sb", out=d["cum_sp"][sl, i],
                                  in_=hml[n][:, c * 128:(c + 1) * 128])
            p.dma(s, "s3_sb", out=d["cum_sp"][sl, 6],
                              in_=ap_(ec_hm, c * 128, [[T, 64], [1, 128]]))
            s.wait_ge(p.sems["s3_sb"], 224 * c + 112)
            for i in range(3):
                p.dma(s, "s3_sb", out=stgA[i:i + 1, :],
                                  in_=ap_(d["cum_sp"], (sl * 7 + i) * 8192,
                                          [[8192, 1], [1, 8192]]))
                p.dma(s, "s3_sb", out=stgB[3 + i:4 + i, :],
                                  in_=ap_(d["cum_sp"], (sl * 7 + 3 + i) * 8192,
                                          [[8192, 1], [1, 8192]]))
            p.dma(s, "s3_sb", out=ecrowT[:],
                              in_=ap_(d["cum_sp"], (sl * 7 + 6) * 8192,
                                      [[8192, 1], [1, 8192]]))
            if c >= 2:
                s.wait_ge(p.sems["s3_tg"], c - 1)
            s.activation(ed_hm[sl][:], p.R["m3_negcum"][:, c * 128:(c + 1) * 128], AF.Exp,
                         bias=cumL[:, c:c + 1])
            p.inc(s.activation(expL[sl][:], cumL[:, c:c + 1], AF.Exp), "s3_pre", 1)
            for g_ in range(8):
                n = 8 * c + g_
                s.wait_ge(p.sems["s3_sp"], n + 1)
                p.inc(s.activation(sfp[n % 2][:], sfp[n % 2][:], AF.Exp), "s3_ex", 1)

    @block.sync
    def _(s):
        s.wait_ge(p.sems["s3_iv"], 1)
        s.wait_ge(p.sems["s3_ig"], 1)
        s.wait_ge(p.sems["m1_st"], 64 * 64 + 16 * 64)
        for c in range(NCH):
            for g_ in range(8):
                n = 8 * c + g_
                if n >= 2:
                    s.wait_ge(p.sems["s3_dx"], n - 1)
                    s.wait_ge(p.sems["s3_xd"], n - 1)
                p.dma(s, "s3_ld", out=xtp[n % 2][:],
                                  in_=ap_(d["xact_sp"], c * DSSM * 128 + g_ * GW * 128,
                                          [[128, GW], [1, 128]]),
                                  transpose=True)
                p.dma(s, "s3_ld", out=xfm[n % 2][:],
                                  in_=ap_(d["xact_sp"], c * DSSM * 128 + g_ * GW * 128,
                                          [[128, 128], [16384, 8], [1, 128]]))
                if n >= 1:
                    m = n - 1
                    s.wait_ge(p.sems["s3_ya"], m + 1)
                    p.dma(s, "s3_yst", 
                        out=ap_(d["y_sp"], (m // 8) * 64 * 16384 + (m % 8) * 8 * 16384,
                                [[128, 128], [16384, 8], [1, 128]]),
                        in_=ap_(ybuf[m % 2], 0, [[GW, 128], [128, 8], [1, 128]]))
        m = NCH * 8 - 1
        s.wait_ge(p.sems["s3_ya"], m + 1)
        p.dma(s, "s3_yst", 
            out=ap_(d["y_sp"], (m // 8) * 64 * 16384 + (m % 8) * 8 * 16384,
                    [[128, 128], [16384, 8], [1, 128]]),
            in_=ap_(ybuf[m % 2], 0, [[GW, 128], [128, 8], [1, 128]]))
        s.wait_ge(p.sems["s3_yst"], 16 * 128)

    @block.tensor
    def _(t):
        t.wait_ge(p.sems["s3_iv"], 1)
        t.wait_ge(p.sems["s3_ig"], 1)
        t.wait_ge(p.sems["m3p_cv"], 2)
        t.wait_ge(p.sems["m3p_cs"], 1)
        t.wait_ge(p.sems["m1_sl"], 66)
        for c in range(NCH):
            sl = c % 2
            if c >= 1:
                t.wait_ge(p.sems["s3_pv"], 3 * c)
            mi = t.matmul(ap_(pG, 0, [[512, 128], [1, 128]]),
                          Bfm[:, c * 128:(c + 1) * 128], Cfm[:, c * 128:(c + 1) * 128],
                          start=True, stop=True)
            p.inc(mi, "s3_pg", 1)
            if c >= 1:
                t.wait_ge(p.sems["s3_tv"], c)
            t.wait_ge(p.sems["s3_pre"], c + 1)
            t.transpose(ap_(ptr, 0, [[1024, 128], [1, 64]]),
                        ap_(dtspb, c * 128, [[T, 64], [1, 128]]), p.R["identb"][0:64, 0:64])
            t.transpose(ap_(ptr, 64, [[1024, 128], [1, 64]]), ed_hm[sl][:],
                        p.R["identb"][0:64, 0:64])
            mi = t.transpose(ap_(ptr, 128, [[1024, 128], [1, 128]]),
                             ap_(Bfm, c * 128, [[T, 128], [1, 128]]), p.R["identb"][:])
            p.inc(mi, "s3_tg", 1)
            t.wait_ge(p.sems["s3_pv"], 3 * c + 1)
            mi = t.transpose(ap_(pG, 128, [[512, 1], [1, 64]]), expL[sl][:],
                             p.R["identf"][0:64, 0:64])
            p.inc(mi, "s3_pg", 1)
            t.wait_ge(p.sems["s3_pv"], 3 * c + 2)
            mi = t.matmul(ap_(pG, 192, [[512, 128], [1, 64]]), p.R["onesrowf"][:],
                          decayrow[sl][:], start=True, stop=True)
            p.inc(mi, "s3_pg", 1)
            for g_ in range(8):
                n = 8 * c + g_
                gsl = n % 2
                t.wait_ge(p.sems["s3_sb"], 224 * (c + 1))
                if n >= 1:
                    t.wait_ge(p.sems["s3_sp"], n)
                for q in range(2):
                    mi = t.matmul(ap_(pBig, q * 512, [[2048, 128], [1, 512]]),
                                  p.R["onesrowb"][:],
                                  ecrowT[:, g_ * GW + q * 512: g_ * GW + (q + 1) * 512],
                                  start=True, stop=True)
                p.inc(mi, "s3_cs", 1)
                t.wait_ge(p.sems["s3_csv"], n + 1)
                for hh in range(8):
                    h = g_ * 8 + hh
                    mi = t.matmul(ap_(pBig, hh * 128, [[2048, 128], [1, 128]]),
                                  stgA[0:6, h * 128:(h + 1) * 128],
                                  stgB[0:6, h * 128:(h + 1) * 128],
                                  start=True, stop=True)
                p.inc(mi, "s3_gam", 1)
                t.wait_ge(p.sems["s3_sS"], n + 1)
                t.wait_ge(p.sems["s3_dx"], n + 1)
                if n >= 1:
                    t.wait_ge(p.sems["s3_ya"], n)
                if c >= 1:
                    t.wait_ge(p.sems["s3_su"], n - 7)
                for hh in range(8):
                    t.matmul(ap_(pBig, 1024 + hh * 128, [[2048, 128], [1, 128]]),
                             dtx[gsl][:, hh * 128:(hh + 1) * 128],
                             Sg[gsl][:, hh * 128:(hh + 1) * 128],
                             start=True, stop=False)
                    mi = t.matmul(ap_(pBig, 1024 + hh * 128, [[2048, 128], [1, 128]]),
                                  statebf[:, (g_ * 8 + hh) * 128:(g_ * 8 + hh + 1) * 128],
                                  Cs[gsl][:, hh * 128:(hh + 1) * 128],
                                  start=False, stop=True)
                p.inc(mi, "s3_ymm", 1)
                t.wait_ge(p.sems["s3_tv"], c + 1)
                t.wait_ge(p.sems["s3_dts"], n + 1)
                if n >= 1:
                    t.wait_ge(p.sems["s3_su"], n)
                for hh in range(8):
                    mi = t.matmul(ap_(pD, hh * 128, [[1024, 128], [1, 128]]),
                                  Btp[sl][:], dtxs[gsl][:, hh * 128:(hh + 1) * 128],
                                  start=True, stop=True)
                p.inc(mi, "s3_dmm", 1)

    @block.vector
    def _(v):
        v = p.wrap(v, "v")
        for c in range(NCH):
            sl = c % 2
            v.wait_ge(p.sems["s3_pg"], 3 * c + 1)
            if c >= 2:
                v.wait_ge(p.sems["s3_sS"], 8 * (c - 1))
            v.tensor_copy(Gf[sl][:], ap_(pG, 0, [[512, 128], [1, 128]]))
            p.inc(v.engine_nop(), "s3_pv", 1)
            v.wait_ge(p.sems["s3_pg"], 3 * c + 2)
            v.tensor_copy(decayrow[sl][:], ap_(pG, 128, [[512, 1], [1, 64]]))
            p.inc(v.engine_nop(), "s3_pv", 1)
            v.wait_ge(p.sems["s3_pg"], 3 * c + 3)
            if c >= 2:
                v.wait_ge(p.sems["s3_smg"], 8 * (c - 1))
            v.tensor_copy(decay_bc[sl][:], ap_(pG, 192, [[512, 128], [1, 64]]))
            p.inc(v.engine_nop(), "s3_pv", 1)
            v.wait_ge(p.sems["s3_tg"], c + 1)
            v.tensor_copy(dtT[sl][:], ap_(ptr, 0, [[1024, 128], [1, 64]]))
            v.tensor_copy(edT[sl][:], ap_(ptr, 64, [[1024, 128], [1, 64]]))
            v.tensor_copy(Btp[sl][:], ap_(ptr, 128, [[1024, 128], [1, 128]]))
            p.inc(v.engine_nop(), "s3_tv", 1)
            for g_ in range(8):
                n = 8 * c + g_
                gsl = n % 2
                v.wait_ge(p.sems["s3_ld"], 32 * (n + 1))
                if n >= 2:
                    v.wait_ge(p.sems["s3_ymm"], n - 1)
                v.tensor_mul(dtx[gsl][:], xtp[gsl][:],
                             ap_(dtT[sl], g_ * 8, [[64, 128], [1, 8], [0, 128]]))
                p.inc(v.engine_nop(), "s3_dx", 1)
                v.wait_ge(p.sems["s3_cs"], n + 1)
                if n >= 2:
                    v.wait_ge(p.sems["s3_ymm"], n - 1)
                v.tensor_mul(Cs[gsl][:],
                             ap_(Cfm, c * 128, [[T, 128], [0, 8], [1, 128]]),
                             pBig[:, 0:1024])
                p.inc(v.engine_nop(), "s3_csv", 1)
                v.wait_ge(p.sems["s3_gam"], n + 1)
                if n >= 2:
                    v.wait_ge(p.sems["s3_sS"], n - 1)
                v.tensor_add(sfp[gsl][:], pBig[:, 0:1024],
                             ap_(mkm, 0, [[128, 128], [0, 8], [1, 128]]))
                p.inc(v.engine_nop(), "s3_sp", 1)
                if n >= 1:
                    m = n - 1
                    msl = m % 2
                    mc, mg = divmod(m, 8)
                    v.wait_ge(p.sems["s3_ymm"], m + 1)
                    v.wait_ge(p.sems["s3_xd"], m + 1)
                    if m >= 2:
                        v.wait_ge(p.sems["s3_yst"], 16 * (m - 1))
                    v.tensor_add(ybuf[msl][:], pBig[:, 1024:2048], xd[msl][:])
                    p.inc(v.engine_nop(), "s3_ya", 1)
                    v.wait_ge(p.sems["s3_dmm"], m + 1)
                    v.wait_ge(p.sems["s3_smg"], m + 1)
                    v.tensor_add(state[:, mg * GW:(mg + 1) * GW],
                                 state[:, mg * GW:(mg + 1) * GW], pD[:])
                    v.tensor_copy(statebf[:, mg * GW:(mg + 1) * GW],
                                  state[:, mg * GW:(mg + 1) * GW])
                    p.inc(v.engine_nop(), "s3_su", 1)
        m = NCH * 8 - 1
        msl = m % 2
        mg = m % 8
        v.wait_ge(p.sems["s3_ymm"], m + 1)
        v.wait_ge(p.sems["s3_xd"], m + 1)
        v.wait_ge(p.sems["s3_yst"], 16 * (m - 1))
        v.tensor_add(ybuf[msl][:], pBig[:, 1024:2048], xd[msl][:])
        p.inc(v.engine_nop(), "s3_ya", 1)
        v.wait_ge(p.sems["s3_dmm"], m + 1)
        v.wait_ge(p.sems["s3_smg"], m + 1)
        v.tensor_add(state[:, mg * GW:(mg + 1) * GW], state[:, mg * GW:(mg + 1) * GW], pD[:])
        v.tensor_copy(statebf[:, mg * GW:(mg + 1) * GW], state[:, mg * GW:(mg + 1) * GW])
        p.inc(v.engine_nop(), "s3_su", 1)

    @block.gpsimd
    def _(g):
        g = p.wrap(g, "g")
        for c in range(NCH):
            sl = c % 2
            for g_ in range(8):
                n = 8 * c + g_
                gsl = n % 2
                g.wait_ge(p.sems["s3_ld"], 32 * (n + 1))
                if n >= 2:
                    g.wait_ge(p.sems["s3_ya"], n - 1)
                g.tensor_mul(xd[gsl][:], xfm[gsl][:],
                             ap_(D_bc, g_ * 8, [[64, 128], [1, 8], [0, 128]]))
                p.inc(g.engine_nop(), "s3_xd", 1)
                g.wait_ge(p.sems["s3_dx"], n + 1)
                g.wait_ge(p.sems["s3_tv"], c + 1)
                if n >= 2:
                    g.wait_ge(p.sems["s3_dmm"], n - 1)
                g.tensor_mul(dtxs[gsl][:], dtx[gsl][:],
                             ap_(edT[sl], g_ * 8, [[64, 128], [1, 8], [0, 128]]))
                p.inc(g.engine_nop(), "s3_dts", 1)
                g.wait_ge(p.sems["s3_ex"], n + 1)
                g.wait_ge(p.sems["s3_pv"], 3 * c + 1)
                if n >= 2:
                    g.wait_ge(p.sems["s3_ymm"], n - 1)
                g.tensor_mul(Sg[gsl][:], sfp[gsl][:],
                             ap_(Gf[sl], 0, [[128, 128], [0, 8], [1, 128]]))
                p.inc(g.engine_nop(), "s3_sS", 1)
                g.wait_ge(p.sems["s3_pv"], 3 * c + 3)
                if c >= 1:
                    g.wait_ge(p.sems["s3_su"], n - 7)
                g.tensor_mul(state[:, g_ * GW:(g_ + 1) * GW],
                             state[:, g_ * GW:(g_ + 1) * GW],
                             ap_(decay_bc[sl], g_ * 8, [[64, 128], [1, 8], [0, 128]]))
                p.inc(g.engine_nop(), "s3_smg", 1)
    ms.close()



def emit_mamba_m4(p, d, res_old, res_new_idx):
    block = p.block
    ms = p.scope()
    yz = p.sbuf("m4_yz", [128, 64 * 1024], BF16, scope=ms)
    wob = [p.sbuf(f"m4_w{i}", [128, 64 * 128], BF16, scope=ms) for i in range(2)]
    zst = [p.sbuf(f"m4_z{i}", [128, 1024], BF16, scope=ms) for i in range(2)]
    zsil = [p.sbuf(f"m4_zs{i}", [128, 1024], BF16, scope=ms) for i in range(2)]
    sqb = [p.sbuf(f"m4_sq{i}", [128, 1024], BF16, scope=ms) for i in range(2)]
    rrow = p.sbuf("m4_rrow", [1, 1024], F32, scope=ms)
    rstdb = p.sbuf("m4_rstdb", [128, 1024], BF16, scope=ms)
    rold = [p.sbuf(f"m4_ro{i}", [128, 1024], F32, scope=ms) for i in range(2)]
    rnew = [p.sbuf(f"m4_rn{i}", [128, 1024], BF16, scope=ms) for i in range(2)]
    psE = [p.psum(f"m4_ps{i}", [128, 1024], scope=ms) for i in range(2)]
    pst = p.psum("m4_pst", [128, 1024], scope=ms)
    for s_ in ("m4_yld", "m4_zld", "m4_sil", "m4_yzs", "m4_sq", "m4_smm",
               "m4_rowv", "m4_rowss", "m4_bct", "m4_bcs", "m4_yn", "m4_wld",
               "m4_mm", "m4_rold", "m4_ev", "m4_st"):
        p.sem(s_)
    nw = p.R["m3_nw"]

    for th in range(2):
        B = {k: p.cnt[k] for k in ("m4_yld", "m4_zld", "m4_sil", "m4_yzs", "m4_sq",
                                   "m4_smm", "m4_rowv", "m4_rowss", "m4_bct", "m4_bcs",
                                   "m4_yn", "m4_wld", "m4_mm", "m4_rold", "m4_ev",
                                   "m4_st")}

        @block.sync
        def _(s):
            if th == 1:
                s.wait_ge(p.sems["m4_yn"], B["m4_yn"])
                s.wait_ge(p.sems["m4_mm"], B["m4_mm"])
            else:
                s.wait_ge(p.sems["s3_yst"], 16 * 128)
            for cc in range(8):
                c = th * 8 + cc
                p.dma(s, "m4_yld", 
                    out=ap_(yz, cc * 128, [[65536, 128], [1024, 64], [1, 128]]),
                    in_=ap_(d["y_sp"], c * 64 * 16384, [[128, 128], [16384, 64], [1, 128]]))
            for h in range(64):
                if h >= 2:
                    s.wait_ge(p.sems["m4_sil"], B["m4_sil"] + h - 1)
                for jj in range(2):
                    p.dma(s, "m4_zld", out=zst[h % 2][:, jj * 512:(jj + 1) * 512],
                                      in_=d["z_sp"][th * 2 + jj, h])

        @block.scalar
        def _(s):
            s = p.wrap(s, "s")
            if B["m4_yzs"]:
                s.wait_ge(p.sems["m4_yzs"], B["m4_yzs"])
            for h in range(64):
                s.wait_ge(p.sems["m4_zld"], B["m4_zld"] + 32 * (h + 1))
                if h >= 2:
                    s.wait_ge(p.sems["m4_yzs"], B["m4_yzs"] + h - 1)
                p.inc(s.activation(zsil[h % 2][:], zst[h % 2][:], AF.Silu), "m4_sil", 1)

        @block.vector
        def _(v):
            v = p.wrap(v, "v")
            v.wait_ge(p.sems["m4_yld"], B["m4_yld"] + 16 * 8)
            for h in range(64):
                v.wait_ge(p.sems["m4_sil"], B["m4_sil"] + h + 1)
                yv = yz[:, h * 1024:(h + 1) * 1024]
                v.tensor_mul(yv, yv, zsil[h % 2][:])
                p.inc(v.engine_nop(), "m4_yzs", 1)
                if h >= 2:
                    v.wait_ge(p.sems["m4_smm"], B["m4_smm"] + h - 1)
                v.tensor_mul(sqb[h % 2][:], yv, yv)
                p.inc(v.engine_nop(), "m4_sq", 1)

        @block.tensor
        def _(t):
            for h in range(64):
                t.wait_ge(p.sems["m4_sq"], B["m4_sq"] + h + 1)
                for q in range(2):
                    mi = t.matmul(ap_(pst, q * 512, [[1024, 1], [1, 512]]),
                                  p.R["onescolb"][:], sqb[h % 2][:, q * 512:(q + 1) * 512],
                                  start=(h == 0), stop=(h == 63))
                p.inc(mi, "m4_smm", 1)

        @block.vector
        def _(v):
            v = p.wrap(v, "v")
            v.wait_ge(p.sems["m4_smm"], B["m4_smm"] + 64)
            v.tensor_copy(rrow[:], ap_(pst, 0, [[1024, 1], [1, 1024]]))
            v.tensor_scalar(rrow[:], rrow[:], 1.0 / DSSM, None, ALU.mult)
            v.tensor_scalar_add(rrow[:], rrow[:], 1e-5)
            p.inc(v.engine_nop(), "m4_rowv", 1)

        @block.scalar
        def _(s):
            s = p.wrap(s, "s")
            s.wait_ge(p.sems["m4_rowv"], B["m4_rowv"] + 1)
            p.inc(s.activation(rrow[:], rrow[:], AF.Sqrt), "m4_rowss", 1)

        @block.vector
        def _(v):
            v = p.wrap(v, "v")
            v.wait_ge(p.sems["m4_rowss"], B["m4_rowss"] + 1)
            v.reciprocal(rrow[:], rrow[:])
            p.inc(v.engine_nop(), "m4_rowv", 1)

        @block.tensor
        def _(t):
            t.wait_ge(p.sems["m4_rowv"], B["m4_rowv"] + 2)
            for q in range(2):
                mi = t.matmul(ap_(pst, q * 512, [[1024, 128], [1, 512]]),
                              p.R["onesrowf"][:], rrow[:, q * 512:(q + 1) * 512],
                              start=True, stop=True)
            p.inc(mi, "m4_bct", 1)

        @block.scalar
        def _(s):
            s = p.wrap(s, "s")
            s.wait_ge(p.sems["m4_bct"], B["m4_bct"] + 1)
            p.inc(s.copy(rstdb[:], pst[:]), "m4_bcs", 1)

        @block.vector
        def _(v):
            v = p.wrap(v, "v")
            v.wait_ge(p.sems["m4_bcs"], B["m4_bcs"] + 1)
            for h in range(64):
                yv = yz[:, h * 1024:(h + 1) * 1024]
                v.scalar_tensor_tensor(yv, yv, nw[:, h:h + 1], rstdb[:],
                                       ALU.mult, ALU.mult)
            p.inc(v.engine_nop(), "m4_yn", 1)

        def _rold(g, o):
            if o >= 2:
                g.wait_ge(p.sems["m4_ev"], B["m4_ev"] + o - 1)
            elif B["m4_ev"]:
                g.wait_ge(p.sems["m4_ev"], B["m4_ev"])
            if res_old[0] == "x0":
                src_ = ap_(d["x0"], o * 128 * T + th * 1024, [[T, 128], [1, 1024]])
            else:
                src_ = ap_(d["res_sp"],
                           res_old[1] * DC * 128 * T + o * 128 * T + th * 1024,
                           [[T, 128], [1, 1024]])
            p.dma(g, "m4_rold", out=rold[o % 2][:], in_=src_)

        @block.gpsimd
        def _(g):
            g = p.wrap(g, "g")
            if B["m4_mm"]:
                g.wait_ge(p.sems["m4_mm"], B["m4_mm"])
            for o in range(32):
                if o >= 2:
                    g.wait_ge(p.sems["m4_mm"], B["m4_mm"] + o - 1)
                p.dma(g, "m4_wld", 
                    out=ap_(wob[o % 2], 0, [[64 * 128, 128], [128, 64], [1, 128]]),
                    in_=d["w_out"][o].rearrange("a p q -> p a q"))
                if o >= 2:
                    _rold(g, o - 2)
            _rold(g, 30)
            _rold(g, 31)

        @block.tensor
        def _(t):
            t.wait_ge(p.sems["m4_yn"], B["m4_yn"] + 1)
            for o in range(32):
                t.wait_ge(p.sems["m4_wld"], B["m4_wld"] + 16 * (o + 1))
                if o >= 2:
                    t.wait_ge(p.sems["m4_ev"], B["m4_ev"] + o - 1)
                for k in range(64):
                    lt = wob[o % 2][:, k * 128:(k + 1) * 128]
                    for j in range(2):
                        mi = t.matmul(ap_(psE[o % 2], j * 512, [[1024, 128], [1, 512]]),
                                      lt, yz[:, k * 1024 + j * 512: k * 1024 + (j + 1) * 512],
                                      start=(k == 0), stop=(k == 63))
                p.inc(mi, "m4_mm", 1)

        @block.vector
        def _(v):
            v = p.wrap(v, "v")
            if B["m4_st"]:
                v.wait_ge(p.sems["m4_st"], B["m4_st"])
            for o in range(32):
                v.wait_ge(p.sems["m4_mm"], B["m4_mm"] + o + 1)
                v.wait_ge(p.sems["m4_rold"], B["m4_rold"] + 16 * (o + 1))
                if o >= 2:
                    v.wait_ge(p.sems["m4_st"], B["m4_st"] + 16 * (o - 1))
                v.tensor_add(rnew[o % 2][:], psE[o % 2][:], rold[o % 2][:])
                p.inc(v.engine_nop(), "m4_ev", 1)

        @block.sync
        def _(s):
            for o in range(32):
                s.wait_ge(p.sems["m4_ev"], B["m4_ev"] + o + 1)
                p.dma(s, "m4_st", 
                    out=ap_(d["res_sp"], res_new_idx * DC * 128 * T + o * 128 * T + th * 1024,
                            [[T, 128], [1, 1024]]),
                    in_=rnew[o % 2][:])
            s.wait_ge(p.sems["m4_st"], B["m4_st"] + 16 * 32)
    ms.close()



def emit_dump_res(p, d, idx):
    block = p.block
    ms = p.scope()
    stage = [p.sbuf(f"du_{i}", [128, T], BF16, scope=ms) for i in range(2)]
    p.sem("du_ld"); p.sem("du_st")

    @block.sync
    def _(s):
        for c in range(DC):
            if c >= 2:
                s.wait_ge(p.sems["du_st"], 16 * (c - 1))
            p.dma(s, "du_ld", out=stage[c % 2][:], in_=d["res_sp"][idx, c])

    @block.gpsimd
    def _(g):
        g = p.wrap(g, "g")
        for c in range(DC):
            g.wait_ge(p.sems["du_ld"], 16 * (c + 1))
            p.dma(g, "du_st", out=d["out"][c * 128:(c + 1) * 128, :], in_=stage[c % 2][:])
        g.wait_ge(p.sems["du_st"], 16 * DC)
    ms.close()



def emit_dump_sbuf(p, d, src_tensor, rows, cols):
    block = p.block
    p.sem("dux")

    @block.gpsimd
    def _(g):
        g = p.wrap(g, "g")
        for s_ in list(p.sems):
            pass
        p.dma(g, "dux", out=d["out"][0:rows, 0:cols], in_=src_tensor[0:rows, 0:cols])
        g.wait_ge(p.sems["dux"], 16)


def emit_dump_dram(p, d, name, nrow, ncol, row_stride=None):
    block = p.block
    ms = p.scope()
    stage = p.sbuf("dud", [128, ncol], BF16, scope=ms)
    p.sem("dud_a"); p.sem("dud_b")
    nt = nrow // 128

    @block.sync
    def _(s):
        for i in range(nt):
            if i >= 1:
                s.wait_ge(p.sems["dud_b"], 16 * i)
            p.dma(s, "dud_a", out=stage[:],
                              in_=ap_(d[name], i * 128 * ncol, [[ncol, 128], [1, ncol]]))

    @block.gpsimd
    def _(g):
        g = p.wrap(g, "g")
        for i in range(nt):
            g.wait_ge(p.sems["dud_a"], 16 * (i + 1))
            p.dma(g, "dud_b", out=d["out"][i * 128:(i + 1) * 128, 0:ncol], in_=stage[:])
        g.wait_ge(p.sems["dud_b"], 16 * nt)
    ms.close()


def emit_final_drain(p):
    block = p.block
    cnts = dict(p.cnt)

    @block.sync
    def _(s):
        for name, v in cnts.items():
            if v > 0:
                s.wait_ge(p.sems[name], v)


def build_full(parts=("mamba", "ffn", "nsa"), upto="all", test_dump=None, n_cores=8):
    nc = bass.Bass("TRN2", target_bir_lowering=False)
    with ExitStack() as ctx:
        block = ctx.enter_context(nc.Block())
        p = Prog(nc, ctx, block)
        d = declare_dram(nc, parts)
        emit_init(p, d)
        xsc = p.scope()
        xln = emit_boundary(p, d, 0, ("x0",), xsc)
        p.phase_end(("bn0_",))
        if upto == "boundary":
            emit_dump_sbuf2(p, d, xln)
            emit_final_drain(p)
            p.right.close()
            return nc
        if "mamba" in parts:
            emit_mamba_m1(p, d, xln)
            p.barrier()
            if upto == "m1":
                emit_dump_dram(p, d, "xact_sp", 4096, 128)
                emit_final_drain(p)
                xsc.close(); p.mam_res.close(); p.right.close()
                return nc
            xsc.close()
            emit_mamba_m3pre(p, d)
            if upto == "m3pre":
                emit_dump_sbuf3(p, d)
                emit_final_drain(p)
                p.mam_res.close(); p.right.close()
                return nc
            emit_mamba_m3(p, d)
            p.barrier()
            if upto == "m3":
                emit_dump_dram(p, d, "y_sp", 4096, 128)
                emit_final_drain(p)
                p.mam_res.close(); p.right.close()
                return nc
            p.mam_res.close()
            emit_mamba_m4(p, d, ("x0",), 0)
            p.barrier()
        if "ffn" in parts and upto in ("x2", "x3", "full", "all", "nsaq", "nsag"):
            xs1 = p.scope()
            xln1 = emit_boundary(p, d, 1, ("res", 0), xs1)
            p.phase_end(("bn1_",))
            emit_ffn(p, d, xln1, 1, ("res", 0), 1, final=False)
            p.phase_end(("f1_",))
            xs1.close()
            if upto == "x2":
                emit_dump_res(p, d, 1)
                emit_final_drain(p)
                p.right.close()
                return nc
        if "nsa" in parts and upto in ("x3", "full", "all", "nsaq", "nsag"):
            xs2 = p.scope()
            xln2 = emit_boundary(p, d, 2, ("res", 1), xs2)
            p.phase_end(("bn2_",))
            nres = p.scope()
            emit_nsa_qkv(p, d, xln2, nres)
            p.phase_end(("nq_",))
            xs2.close()
            emit_nsa_prep(p, d, nres)
            p.phase_end(("n0_",))
            if upto == "nsaq":
                emit_dump_dram(p, d, "q_sp", 4096, 2048)
                emit_final_drain(p)
                nres.close(); p.right.close()
                return nc
            emit_nsa_attn(p, d, nres)
            p.phase_end(("n1_",))
            nres.close()
            if upto == "nsag":
                emit_dump_dram(p, d, "gat_sp", 4096, 2048)
                emit_final_drain(p)
                p.right.close()
                return nc
            emit_nsa_out(p, d, 1, 0)
            p.phase_end(("no_",))
            if upto == "x3":
                emit_dump_res(p, d, 0)
                emit_final_drain(p)
                p.right.close()
                return nc
            xs3 = p.scope()
            xln3 = emit_boundary(p, d, 3, ("res", 0), xs3)
            p.phase_end(("bn3_",))
            emit_ffn(p, d, xln3, 3, ("res", 0), None, final=True)
            xs3.close()
        if test_dump is not None:
            emit_dump_res(p, d, test_dump)
        emit_final_drain(p)
        p.right.close()
    return nc


def emit_dump_sbuf2(p, d, xln):
    block = p.block
    p.sem("dux")

    @block.gpsimd
    def _(g):
        g = p.wrap(g, "g")
        for c in range(DC):
            p.dma(g, "dux", out=d["out"][c * 128:(c + 1) * 128, :],
                              in_=ap_(xln, c * T, [[DC * T, 128], [1, T]]))
        g.wait_ge(p.sems["dux"], 16 * DC)


def emit_dump_sbuf3(p, d):
    block = p.block
    p.sem("dux"); p.sem("duy")

    @block.sync
    def _(s):
        p.dma(s, "duy", out=d["dbg_sp"], in_=p.R["m3_cum"][:])

    @block.gpsimd
    def _(g):
        g = p.wrap(g, "g")
        g.wait_ge(p.sems["duy"], 16)
        p.dma(g, "dux", out=d["out"][0:64, :], in_=d["dbg_sp"])
        g.wait_ge(p.sems["dux"], 16)


def host_inputs(inputs, parts=("mamba", "ffn", "nsa")):
    consts = prep_consts()
    w = {}
    if "mamba" in parts:
        w.update(prep_mamba(inputs))
    if "ffn" in parts:
        w.update(prep_ffn(inputs))
    if "nsa" in parts:
        w.update(prep_nsa(inputs))
    needed_consts = {"ident"}
    if "mamba" in parts:
        needed_consts |= {"mk_mask"}
    if "nsa" in parts:
        needed_consts |= {"cmask", "fown", "iotajb", "sbandF", "sbandF0", "rcos", "rsin"}
    base = {k: consts[k] for k in needed_consts}
    base.update(w)
    x = np.asarray(inputs["L0_attn_a_hidden_states"], np.float32)
    maps = []
    for b in range(x.shape[0]):
        m = dict(base)
        m["x0"] = np.ascontiguousarray(x[b].T)
        maps.append(m)
    return maps


def kernel(**inputs):
    parts = ("mamba", "ffn", "nsa")
    nc = build_full(parts=parts, test_dump=None)
    maps = host_inputs(inputs, parts)
    res = run_bass_kernel_spmd(nc, maps, core_ids=list(range(8)))
    return np.stack([np.asarray(r["out"]).T for r in res.results], axis=0)



def emit_ffn(p, d, xln, idx, res_old, res_new, final=False):
    block = p.block
    fs = p.scope()
    wfb = [p.sbuf(f"f{idx}_w{i}", [128, 32 * 128], BF16, scope=fs) for i in range(2)]
    brf = p.sbuf(f"f{idx}_brf", [128, 4096], F32, scope=fs)
    bsc = p.sbuf(f"f{idx}_bsc", [128, 4096], BF16, scope=fs)
    afb = p.sbuf(f"f{idx}_afb", [128, 32 * 128], BF16, scope=fs)
    ter = p.sbuf(f"f{idx}_ter", [128, T], BF16, scope=fs)
    qcol = p.sbuf(f"f{idx}_q", [128, 32], F32, scope=fs)
    rwf = p.sbuf(f"f{idx}_rw", [128, 256], F32, scope=fs)
    rbc = p.sbuf(f"f{idx}_rb", [8, 1], F32, scope=fs)
    lcol = p.sbuf(f"f{idx}_lc", [8, 1], F32, scope=fs)
    lrow = p.sbuf(f"f{idx}_lr", [1, 8], F32, scope=fs)
    lmax = p.sbuf(f"f{idx}_lm", [1, 1], F32, scope=fs)
    rcol = p.sbuf(f"f{idx}_rc", [128, 1], F32, scope=fs)
    fbc = p.sbuf(f"f{idx}_fb", [128, 32], F32, scope=fs)
    rold = [p.sbuf(f"f{idx}_ro{i}", [128, T],
                   F32 if res_old[0] == "x0" else BF16, scope=fs) for i in range(2)]
    rnew = [p.sbuf(f"f{idx}_rn{i}", [128, T], F32 if final else BF16, scope=fs)
            for i in range(2)]
    psF = [p.psum(f"f{idx}_ps{i}", [128, 2048], scope=fs) for i in range(2)]
    plog = psF[1]
    S = lambda n: f"f{idx}_" + n
    for n in ("ld", "qv", "lmm", "lev", "rt1", "rt2", "rt2v", "rt2s", "rt3", "rts",
              "bsc", "aldg", "ter", "tev", "wld", "mm", "rold", "ev", "st"):
        p.sem(S(n))

    @block.sync
    def _(s):
        p.dma(s, S("ld"), out=ap_(rwf, 0, [[256, 128], [8, 32], [1, 8]]),
              in_=d["ffn_rw"].rearrange("a p q -> p a q"))
        p.dma(s, S("ld"), out=rbc[:], in_=d["ffn_rb"])
        p.dma(s, S("ld"), out=fbc[:], in_=d["ffn_b"])
        p.dma(s, S("ld"), out=brf[:], in_=d["ffn_bmat"])
        s.wait_ge(p.sems[S("lev")], 1)
        p.dma(s, S("rt1"), out=d["r8"], in_=ap_(lcol, 0, [[1, 8], [1, 1]]))
        s.wait_ge(p.sems[S("rt1")], 16)
        p.dma(s, S("rt2"), out=lrow[:], in_=d["r8"])
        s.wait_ge(p.sems[S("rts")], 1)
        p.dma(s, S("rt1"), out=d["r8"], in_=lrow[:])
        s.wait_ge(p.sems[S("rt1")], 32)
        p.dma(s, S("rt3"), out=rcol[:], in_=ap_(d["r8"], 0, [[1, 8], [0, 16]]))

    @block.gpsimd
    def _(g):
        g = p.wrap(g, "g")
        p.dma(g, S("aldg"), out=afb[:],
              in_=d["ffn_a"][0].rearrange("a p q -> p a q"))
        for o in range(32):
            if o >= 2:
                g.wait_ge(p.sems[S("mm")], o - 1)
            p.dma(g, S("wld"), out=ap_(wfb[o % 2], 0, [[4096, 128], [128, 32], [1, 128]]),
                  in_=d["ffn_w"][o].rearrange("a p q -> p a q"))
            if o >= 2:
                oo = o - 2
                if oo >= 2:
                    g.wait_ge(p.sems[S("ev")], oo - 1)
                if res_old[0] == "x0":
                    p.dma(g, S("rold"), out=rold[oo % 2][:],
                          in_=ap_(d["x0"], oo * 128 * T, [[T, 128], [1, T]]))
                else:
                    p.dma(g, S("rold"), out=rold[oo % 2][:],
                          in_=ap_(d["res_sp"], res_old[1] * DC * 128 * T + oo * 128 * T,
                                  [[T, 128], [1, T]]))
        for oo in (30, 31):
            g.wait_ge(p.sems[S("ev")], oo - 1)
            if res_old[0] == "x0":
                p.dma(g, S("rold"), out=rold[oo % 2][:],
                      in_=ap_(d["x0"], oo * 128 * T, [[T, 128], [1, T]]))
            else:
                p.dma(g, S("rold"), out=rold[oo % 2][:],
                      in_=ap_(d["res_sp"], res_old[1] * DC * 128 * T + oo * 128 * T,
                              [[T, 128], [1, T]]))

    @block.vector
    def _(v):
        v = p.wrap(v, "v")
        for c in range(DC):
            v.reduce_sum(qcol[:, c:c + 1],
                         ap_(xln, c * T + 611, [[DC * T, 128], [1, 1436]]), axis=AX.X)
        v.tensor_scalar(qcol[:], qcol[:], 1.0 / 1436.0, None, ALU.mult)
        p.inc(v.engine_nop(), S("qv"), 1)
        v.wait_ge(p.sems[S("lmm")], 1)
        v.wait_ge(p.sems[S("ld")], 32)
        v.tensor_add(lcol[:], ap_(plog, 0, [[2048, 8], [1, 1]]), rbc[:])
        p.inc(v.engine_nop(), S("lev"), 1)
        v.wait_ge(p.sems[S("rt2")], 16)
        v.reduce_max(lmax[:], lrow[:], axis=AX.X)
        v.tensor_scalar(lrow[:], lrow[:], lmax[:], None, ALU.subtract)
        p.inc(v.engine_nop(), S("rt2v"), 1)

    @block.scalar
    def _(s):
        s = p.wrap(s, "s")
        s.wait_ge(p.sems[S("rt2v")], 1)
        p.inc(s.activation(lrow[:], lrow[:], AF.Exp), S("rt2s"), 1)

    @block.vector
    def _(v):
        v = p.wrap(v, "v")
        v.wait_ge(p.sems[S("rt2s")], 1)
        v.reduce_sum(lmax[:], lrow[:], axis=AX.X)
        v.reciprocal(lmax[:], lmax[:])
        v.tensor_scalar(lrow[:], lrow[:], lmax[:], None, ALU.mult)
        p.inc(v.engine_nop(), S("rts"), 1)
        v.wait_ge(p.sems[S("rt3")], 16)
        v.wait_ge(p.sems[S("ld")], 64)
        v.tensor_scalar(bsc[:], brf[:], rcol[:], 2.0, ALU.mult, ALU.mult)
        p.inc(v.engine_nop(), S("bsc"), 1)
        v.wait_ge(p.sems[S("ter")], 1)
        v.tensor_copy(ter[:], psF[0][:])
        p.inc(v.engine_nop(), S("tev"), 1)

    @block.tensor
    def _(t):
        t.wait_ge(p.sems[S("qv")], 1)
        t.wait_ge(p.sems[S("ld")], 16)
        for c in range(DC):
            mi = t.matmul(ap_(plog, 0, [[2048, 8], [1, 1]]),
                          rwf[:, c * 8:(c + 1) * 8], qcol[:, c:c + 1],
                          start=(c == 0), stop=(c == DC - 1))
        p.inc(mi, S("lmm"), 1)
        t.wait_ge(p.sems[S("aldg")], 16)
        for dd in range(DC):
            lt = afb[:, dd * 128:(dd + 1) * 128]
            for j in range(4):
                mi = t.matmul(ap_(psF[0], j * 512, [[2048, 128], [1, 512]]),
                              lt, ap_(xln, dd * T + j * 512, [[DC * T, 128], [1, 512]]),
                              start=(dd == 0), stop=(dd == DC - 1))
        p.inc(mi, S("ter"), 1)
        t.wait_ge(p.sems[S("tev")], 1)
        t.wait_ge(p.sems[S("bsc")], 1)
        for o in range(32):
            t.wait_ge(p.sems[S("wld")], 16 * (o + 1))
            if o >= 2:
                t.wait_ge(p.sems[S("ev")], o - 1)
            pso = psF[o % 2]
            for dd in range(DC):
                lt = wfb[o % 2][:, dd * 128:(dd + 1) * 128]
                for j in range(4):
                    t.matmul(ap_(pso, j * 512, [[2048, 128], [1, 512]]),
                             lt, ap_(xln, dd * T + j * 512, [[DC * T, 128], [1, 512]]),
                             start=(dd == 0), stop=False)
            for j in range(4):
                mi = t.matmul(ap_(pso, j * 512, [[2048, 128], [1, 512]]),
                              bsc[:, o * 128:(o + 1) * 128],
                              ter[:, j * 512:(j + 1) * 512],
                              start=False, stop=True)
            p.inc(mi, S("mm"), 1)

    @block.vector
    def _(v):
        v = p.wrap(v, "v")
        v.wait_ge(p.sems[S("ld")], 48)
        for o in range(32):
            v.wait_ge(p.sems[S("mm")], o + 1)
            v.wait_ge(p.sems[S("rold")], 16 * (o + 1))
            if o >= 2:
                v.wait_ge(p.sems[S("st")], 16 * (o - 1))
            v.scalar_tensor_tensor(rnew[o % 2][:], psF[o % 2][:], fbc[:, o:o + 1],
                                   rold[o % 2][:], ALU.add, ALU.add)
            p.inc(v.engine_nop(), S("ev"), 1)

    @block.sync
    def _(s):
        for o in range(32):
            s.wait_ge(p.sems[S("ev")], o + 1)
            if final:
                p.dma(s, S("st"), out=d["out"][o * 128:(o + 1) * 128, :], in_=rnew[o % 2][:])
            else:
                p.dma(s, S("st"),
                      out=ap_(d["res_sp"], res_new * DC * 128 * T + o * 128 * T,
                              [[T, 128], [1, T]]),
                      in_=rnew[o % 2][:])
        s.wait_ge(p.sems[S("st")], 16 * 32)
    fs.close()



def emit_nsa_qkv(p, d, xln, nres):
    block = p.block
    ns = p.scope()
    wqb = [p.sbuf(f"nq_w{i}", [128, 32 * 128], BF16, scope=ns) for i in range(2)]
    qst = [p.sbuf(f"nq_s{i}", [128, T], BF16, scope=ns) for i in range(2)]
    wgb = p.sbuf("nq_wg", [128, 32 * 128], BF16, scope=ns)
    bgc = p.sbuf("nq_bg", [128, 1], F32, scope=ns)
    gfm = p.sbuf("nq_gfm", [128, T], F32, scope=ns)
    psQ = [p.psum(f"nq_ps{i}", [128, 2048], scope=ns) for i in range(2)]
    gtok = p.sbuf("gtok", [128, 2048], F32, scope=nres, side="right")
    for n in ("nq_wld", "nq_gld", "nq_mm", "nq_ev", "nq_st", "nq_gmm", "nq_gev",
              "nq_gtr", "nq_gtv"):
        p.sem(n)

    @block.sync
    def _(s):
        p.dma(s, "nq_gld", out=bgc[:], in_=d["b_gate"])

    @block.gpsimd
    def _(g):
        g = p.wrap(g, "g")
        p.dma(g, "nq_wld", out=ap_(wgb, 0, [[4096, 128], [128, 32], [1, 128]]),
              in_=d["w_gate"][0].rearrange("a p q -> p a q"))
        for o in range(48):
            if o >= 2:
                g.wait_ge(p.sems["nq_mm"], o - 1)
            p.dma(g, "nq_wld", out=ap_(wqb[o % 2], 0, [[4096, 128], [128, 32], [1, 128]]),
                  in_=d["w_qkv"][o].rearrange("a p q -> p a q"))

    @block.tensor
    def _(t):
        t.wait_ge(p.sems["nq_wld"], 16)
        for dd in range(DC):
            lt = wgb[:, dd * 128:(dd + 1) * 128]
            for j in range(4):
                mi = t.matmul(ap_(psQ[1], j * 512, [[2048, 128], [1, 512]]),
                              lt, ap_(xln, dd * T + j * 512, [[DC * T, 128], [1, 512]]),
                              start=(dd == 0), stop=(dd == DC - 1))
        p.inc(mi, "nq_gmm", 1)
        t.wait_ge(p.sems["nq_gev"], 1)
        for tc in range(NCH):
            if tc >= 2:
                t.wait_ge(p.sems["nq_gtv"], tc - 1)
            mi = t.transpose(ap_(psQ[1], (tc % 2) * 512, [[2048, 128], [1, 128]]),
                             gfm[:, tc * 128:(tc + 1) * 128], p.R["identf"][:])
            p.inc(mi, "nq_gtr", 1)
        for o in range(48):
            t.wait_ge(p.sems["nq_wld"], 16 * (o + 2))
            if o <= 1:
                t.wait_ge(p.sems["nq_gtv"], NCH)
            if o >= 2:
                t.wait_ge(p.sems["nq_ev"], o - 1)
            pso = psQ[o % 2]
            for dd in range(DC):
                lt = wqb[o % 2][:, dd * 128:(dd + 1) * 128]
                for j in range(4):
                    mi = t.matmul(ap_(pso, j * 512, [[2048, 128], [1, 512]]),
                                  lt, ap_(xln, dd * T + j * 512, [[DC * T, 128], [1, 512]]),
                                  start=(dd == 0), stop=(dd == DC - 1))
            p.inc(mi, "nq_mm", 1)

    @block.scalar
    def _(s):
        s = p.wrap(s, "s")
        s.wait_ge(p.sems["nq_gmm"], 1)
        s.wait_ge(p.sems["nq_gld"], 16)
        p.inc(s.activation(gfm[:], psQ[1][:], AF.Sigmoid, bias=bgc[:]), "nq_gev", 1)
        for o in range(48):
            s.wait_ge(p.sems["nq_mm"], o + 1)
            if o >= 2:
                s.wait_ge(p.sems["nq_st"], 16 * (o - 1))
            p.inc(s.copy(qst[o % 2][:], psQ[o % 2][:]), "nq_ev", 1)

    @block.vector
    def _(v):
        v = p.wrap(v, "v")
        for tc in range(NCH):
            v.wait_ge(p.sems["nq_gtr"], tc + 1)
            v.tensor_copy(gtok[:, tc * 128:(tc + 1) * 128],
                          ap_(psQ[1], (tc % 2) * 512, [[2048, 128], [1, 128]]))
            p.inc(v.engine_nop(), "nq_gtv", 1)

    @block.sync
    def _(s):
        for o in range(48):
            s.wait_ge(p.sems["nq_ev"], o + 1)
            if o < 32:
                dst = ap_(d["q_sp"], o * 128 * T, [[T, 128], [1, T]])
            elif o < 40:
                dst = ap_(d["k_sp"], (o - 32) * 128 * T, [[T, 128], [1, T]])
            else:
                dst = ap_(d["v_sp"], (o - 40) * 128 * T, [[T, 128], [1, T]])
            p.dma(s, "nq_st", out=dst, in_=qst[o % 2][:])
        s.wait_ge(p.sems["nq_st"], 16 * 48)
    ns.close()



def emit_nsa_prep(p, d, nres):
    block = p.block
    n0 = p.scope()
    rk = p.sbuf("n_rk", [128, 16384], BF16, scope=nres, side="right")
    vT = p.sbuf("n_vT", [128, 16384], BF16, scope=nres, side="right")
    ckb = p.sbuf("n_ckb", [128, 1032], BF16, scope=nres, side="right")
    cvb = p.sbuf("n_cvb", [128, 1032], BF16, scope=nres, side="right")
    cvT = p.sbuf("n_cvT", [128, 1024], BF16, scope=nres, side="right")
    cvlast = p.sbuf("n_cvlast", [65, 1024], BF16, scope=nres, side="right")
    cmaskS = p.sbuf("n_cmask", [128, 2064], F32, scope=nres, side="right")
    fownS = p.sbuf("n_fown", [128, 128], F32, scope=nres, side="right")
    sbF = p.sbuf("n_sbF", [128, 256], F32, scope=nres, side="right")
    sbF0 = p.sbuf("n_sbF0", [128, 128], F32, scope=nres, side="right")
    rcosf = p.sbuf("n0_cos", [128, T], BF16, scope=nres, side="right")
    rsinf = p.sbuf("n0_sin", [128, T], BF16, scope=nres, side="right")
    kfm = p.sbuf("n0_kfm", [128, 16384], BF16, scope=n0)
    ckin = p.sbuf("n0_ckin", [128, 8192], BF16, scope=n0)
    h1 = p.sbuf("n0_h1", [128, 8192], BF16, scope=n0)
    kposf = p.sbuf("n0_kp", [128, 128], F32, scope=n0)
    rtmp = p.sbuf("n0_rt", [128, T], BF16, scope=n0)
    w1b = [p.sbuf(f"n0_w1{i}", [128, 2048], BF16, scope=n0) for i in range(2)]
    w2b = p.sbuf("n0_w2", [128, 2048], BF16, scope=n0)
    b1f = p.sbuf("n0_b1", [128, 16], F32, scope=n0)
    b2f = p.sbuf("n0_b2", [128, 1], F32, scope=n0)
    memf = p.sbuf("n0_mem", [128, 8], F32, scope=n0)
    psN = [p.psum(f"n0_ps{i}", [128, 512], scope=n0) for i in range(2)]
    ptrN = p.psum("n0_ptr", [128, 1024], BF16, scope=n0)
    for n in ("n0_cld", "n0_cldg", "n0_kld", "n0_rp", "n0_ci", "n0_w1", "n0_w2",
              "n0_mm1", "n0_h1", "n0_mm2", "n0_cke", "n0_meme", "n0_tr", "n0_tv",
              "n0_bld"):
        p.sem(n)

    def side(wname1, wname2, bname1, bname2, posname, memname, spname, is_k):
        SD = 0 if is_k else 1
        ckdst = ckb if is_k else cvb

        @block.sync
        def _(s):
            if is_k:
                p.dma(s, "n0_cld", out=cmaskS[:],
                      in_=ap_(d["cmask"], 0, [[129, 128], [128 * 129, 16], [1, 129]]))
                p.dma(s, "n0_cld", out=fownS[:], in_=d["fown"])
                p.dma(s, "n0_cld", out=sbF[:], in_=d["sbandF"])
                p.dma(s, "n0_cld", out=sbF0[:], in_=d["sbandF0"])
            else:
                s.wait_ge(p.sems["n0_ci"], 2)
                s.wait_ge(p.sems["n0_h1"], 32)
                s.wait_ge(p.sems["n0_cke"], 2)
                s.wait_ge(p.sems["n0_meme"], 1)
                s.wait_ge(p.sems["n0_rp"], 8)
            p.dma(s, "n0_bld", out=kposf[:], in_=d[posname])
            p.dma(s, "n0_bld", out=b1f[:], in_=d[bname1])
            p.dma(s, "n0_bld", out=b2f[:], in_=d[bname2])
            p.dma(s, "n0_bld", out=memf[:], in_=d[memname])
            for h in range(8):
                p.dma(s, "n0_kld", out=ap_(kfm, h * 2048, [[16384, 128], [1, T]]),
                      in_=ap_(d[spname], h * 128 * T, [[T, 128], [1, T]]))

        @block.gpsimd
        def _(g):
            g = p.wrap(g, "g")
            if is_k:
                p.dma(g, "n0_cldg", out=rcosf[0:64, :], in_=d["rcos"])
                p.dma(g, "n0_cldg", out=ap_(rcosf, 64 * T, [[T, 64], [1, T]]),
                      in_=d["rcos"])
                p.dma(g, "n0_cldg", out=rsinf[0:64, :], in_=d["rsin"])
                p.dma(g, "n0_cldg", out=ap_(rsinf, 64 * T, [[T, 64], [1, T]]),
                      in_=d["rsin"])
                g.wait_ge(p.sems["n0_cldg"], 64)
                cos0 = ap_(rcosf, 0, [[T, 64], [1, T]])
                cos1 = ap_(rcosf, 64 * T, [[T, 64], [1, T]])
                sin0 = ap_(rsinf, 0, [[T, 64], [1, T]])
                sin1 = ap_(rsinf, 64 * T, [[T, 64], [1, T]])
                tmp0 = ap_(rtmp, 0, [[T, 64], [1, T]])
                tmp1 = ap_(rtmp, 64 * T, [[T, 64], [1, T]])
                for h in range(8):
                    g.wait_ge(p.sems["n0_kld"], 16 * (h + 1))
                    khi = ap_(kfm, h * 2048, [[16384, 64], [1, T]])
                    klo = ap_(kfm, 64 * 16384 + h * 2048, [[16384, 64], [1, T]])
                    rhi = ap_(rk, h * 2048, [[16384, 64], [1, T]])
                    rlo = ap_(rk, 64 * 16384 + h * 2048, [[16384, 64], [1, T]])
                    g.tensor_mul(rhi, khi, cos0)
                    g.tensor_mul(tmp0, klo, sin1)
                    g.tensor_sub(rhi, rhi, tmp0)
                    g.tensor_mul(rlo, klo, cos1)
                    g.tensor_mul(tmp1, khi, sin0)
                    g.tensor_add(rlo, rlo, tmp1)
                    p.inc(g.engine_nop(), "n0_rp", 1)
            for half in range(2):
                for o in range(16):
                    oo = SD * 32 + half * 16 + o
                    if o >= 2:
                        g.wait_ge(p.sems["n0_mm1"], oo - 1)
                    elif oo >= 16:
                        g.wait_ge(p.sems["n0_mm1"], SD * 32 + half * 16)
                    p.dma(g, "n0_w1",
                          out=ap_(w1b[o % 2], 0, [[2048, 128], [128, 16], [1, 128]]),
                          in_=d[wname1][o].rearrange("a p q -> p a q"))
                if SD * 2 + half >= 1:
                    g.wait_ge(p.sems["n0_mm2"], SD * 2 + half)
                p.dma(g, "n0_w2",
                      out=ap_(w2b, 0, [[2048, 128], [128, 16], [1, 128]]),
                      in_=d[wname2][0].rearrange("a p q -> p a q"))

        @block.vector
        def _(v):
            v = p.wrap(v, "v")
            v.wait_ge(p.sems["n0_bld"], 64 * (SD + 1))
            if not is_k:
                v.wait_ge(p.sems["n0_mm1"], 32)
            for half in range(2):
                for h in range(8):
                    if half == 0:
                        v.wait_ge(p.sems["n0_kld"], 16 * (SD * 8 + h + 1))
                    v.tensor_add(
                        ap_(ckin, h * 64, [[8192, 128], [512, 16], [1, 64]]),
                        ap_(kfm, h * 2048 + half * 1024,
                            [[16384, 128], [1, 16], [16, 64]]),
                        ap_(kposf, h * 16, [[128, 128], [1, 16], [0, 64]]))
                p.inc(v.engine_nop(), "n0_ci", 1)
                if not is_k and half == 0:
                    v.wait_ge(p.sems["n0_mm2"], 2)
                for o in range(16):
                    oo = SD * 32 + half * 16 + o
                    v.wait_ge(p.sems["n0_mm1"], oo + 1)
                    v.tensor_scalar(h1[:, o * 512:(o + 1) * 512],
                                    psN[o % 2][:], b1f[:, o:o + 1], 0.0,
                                    ALU.add, ALU.max)
                    p.inc(v.engine_nop(), "n0_h1", 1)
                v.wait_ge(p.sems["n0_mm2"], SD * 2 + half + 1)
                v.tensor_scalar(ap_(ckdst, 1 + half * 64, [[1032, 128], [129, 8], [1, 64]]),
                                psN[0][:], b2f[:], None, ALU.add)
                p.inc(v.engine_nop(), "n0_cke", 1)
                if half == 0:
                    v.tensor_copy(ap_(ckdst, 0, [[1032, 128], [129, 8]]), memf[:])
                    p.inc(v.engine_nop(), "n0_meme", 1)
            if not is_k:
                for b in range(16 + 2):
                    v.wait_ge(p.sems["n0_tr"], b + 1)
                    if b < 16:
                        v.tensor_copy(vT[:, b * 1024:(b + 1) * 1024], ptrN[:])
                    elif b == 16:
                        v.tensor_copy(cvT[:], ap_(ptrN, 0, [[1024, 128], [1, 1024]]))
                    else:
                        v.tensor_copy(ap_(cvlast, 0, [[1024, 1], [1, 1024]]),
                                      ap_(ptrN, 0, [[1024, 1], [1, 1024]]))
                        v.tensor_copy(ap_(cvlast, 64 * 1024, [[1024, 1], [1, 1024]]),
                                      ap_(ptrN, 0, [[1024, 1], [1, 1024]]))
                    p.inc(v.engine_nop(), "n0_tv", 1)

        @block.tensor
        def _(t):
            for half in range(2):
                t.wait_ge(p.sems["n0_ci"], SD * 2 + half + 1)
                if SD * 2 + half >= 1:
                    t.wait_ge(p.sems["n0_cke"], SD * 2 + half)
                    t.wait_ge(p.sems["n0_h1"], (SD * 2 + half) * 16)
                for o in range(16):
                    oo = SD * 32 + half * 16 + o
                    t.wait_ge(p.sems["n0_w1"], 16 * (oo + 1))
                    if o >= 2:
                        t.wait_ge(p.sems["n0_h1"], (SD * 2 + half) * 16 + o - 1)
                    for i in range(16):
                        lt = w1b[o % 2][:, i * 128:(i + 1) * 128]
                        mi = t.matmul(psN[o % 2][:],
                                      lt, ckin[:, i * 512:(i + 1) * 512],
                                      start=(i == 0), stop=(i == 15))
                    p.inc(mi, "n0_mm1", 1)
                t.wait_ge(p.sems["n0_w2"], 16 * (SD * 2 + half + 1))
                t.wait_ge(p.sems["n0_h1"], (SD * 2 + half) * 16 + 16)
                for i2 in range(16):
                    lt = w2b[:, i2 * 128:(i2 + 1) * 128]
                    mi = t.matmul(psN[0][:],
                                  lt, h1[:, i2 * 512:(i2 + 1) * 512],
                                  start=(i2 == 0), stop=(i2 == 15))
                p.inc(mi, "n0_mm2", 1)
            if not is_k:
                t.wait_ge(p.sems["n0_kld"], 256)
                t.wait_ge(p.sems["n0_cke"], 4)
                t.wait_ge(p.sems["n0_meme"], 2)
                for b in range(16):
                    if b >= 1:
                        t.wait_ge(p.sems["n0_tv"], b)
                    for s8 in range(8):
                        ci = b * 8 + s8
                        mi = t.transpose(ap_(ptrN, s8 * 128, [[1024, 128], [1, 128]]),
                                         kfm[:, ci * 128:(ci + 1) * 128], p.R["identb"][:])
                    p.inc(mi, "n0_tr", 1)
                t.wait_ge(p.sems["n0_tv"], 16)
                for h in range(8):
                    mi = t.transpose(ap_(ptrN, h * 128, [[1024, 128], [1, 128]]),
                                     cvb[:, h * 129 + 1:h * 129 + 129], p.R["identb"][:])
                p.inc(mi, "n0_tr", 1)
                t.wait_ge(p.sems["n0_tv"], 17)
                for h in range(8):
                    mi = t.transpose(ap_(ptrN, h * 128, [[1024, 1], [1, 128]]),
                                     cvb[:, h * 129:h * 129 + 1], p.R["identb"][:])
                p.inc(mi, "n0_tr", 1)

    side("k_w1", "k_w2", "k_b1", "k_b2", "k_pos", "mem_k", "k_sp", True)
    side("v_w1", "v_w2", "v_b1", "v_b2", "v_pos", "mem_v", "v_sp", False)

    @block.sync
    def _(s):
        s.wait_ge(p.sems["n0_cld"], 64)
        s.wait_ge(p.sems["n0_cldg"], 64)
        s.wait_ge(p.sems["n0_tv"], 18)
    n0.close()



def emit_nsa_attn(p, d, nres):
    block = p.block
    na = p.scope()
    q1 = [p.sbuf(f"n1_q{i}", [128, T], BF16, scope=na) for i in range(2)]
    rq1 = [p.sbuf(f"n1_rq{i}", [128, T], BF16, scope=na) for i in range(2)]
    rtmp = p.sbuf("n1_rt", [128, T], BF16, scope=na)
    cst = p.sbuf("n1_cst", [128, 2064], F32, scope=na)
    Sc = p.sbuf("n1_Sc", [128, 2064], BF16, scope=na)
    imp = p.sbuf("n1_imp", [128, 2064], F32, scope=na)
    denc = p.sbuf("n1_denc", [128, 16], F32, scope=na)
    rcg = p.sbuf("n1_rcg", [128, 16], F32, scope=na)
    selm = p.sbuf("n1_selm", [128, 2048], BF16, scope=na)
    e2 = p.sbuf("n1_e2", [128, 128], F32, scope=na)
    v8 = p.sbuf("n1_v8", [128, 8], F32, scope=na)
    wcol = p.sbuf("n1_wc", [128, 8], F32, scope=na)
    m1t = p.sbuf("n1_m1", [128, 128], BF16, scope=na)
    m2t = p.sbuf("n1_m2", [128, 128], BF16, scope=na)
    S_f = [p.sbuf(f"n1_Sf{i}", [128, 2048], BF16, scope=na) for i in range(2)]
    Pf = [p.sbuf(f"n1_Pf{i}", [128, 2048], BF16, scope=na) for i in range(2)]
    ssb = p.sbuf("n1_ss", [128, 256], BF16, scope=na)
    fcol = p.sbuf("n1_fc", [128, 4], F32, scope=na)
    gcol = p.sbuf("n1_gc", [128, 4], F32, scope=na)
    PcT = p.sbuf("n1_PcT", [128, 8192], BF16, scope=na)
    pcrow = [p.sbuf(f"n1_pcr{i}", [65, 2048], BF16, scope=na) for i in range(2)]
    PT = p.sbuf("n1_PT", [128, 8192], BF16, scope=na)
    ogat = [p.sbuf(f"n1_og{i}", [128, 512], BF16, scope=na) for i in range(2)]
    pO = [p.psum(f"n1_pO{i}", [128, 512], scope=na) for i in range(2)]
    pS = p.psum("n1_pS", [128, 2048], scope=na)
    ptr2 = [p.psum(f"n1_ptr{i}", [128, 1024], BF16, scope=na) for i in range(2)]
    for n in ("n1_qld", "n1_gst", "n1_csmm", "n1_csev", "n1_imp", "n1_scx",
              "n1_pcb", "n1_ctr", "n1_ctv", "n1_rp", "n1_selv", "n1_sels",
              "n1_fmm", "n1_fex", "n1_pv", "n1_ftr", "n1_ftv", "n1_omm",
              "n1_oev"):
        p.sem(n)

    rk = p.R["n_rk"]; vT = p.R["n_vT"]; ckb = p.R["n_ckb"]
    cvT = p.R["n_cvT"]; cvlast = p.R["n_cvlast"]
    cmaskS = p.R["n_cmask"]; fownS = p.R["n_fown"]
    sbF = p.R["n_sbF"]; sbF0 = p.R["n_sbF0"]; gtok = p.R["gtok"]
    rcosf = p.R["n0_cos"]; rsinf = p.R["n0_sin"]

    def cs_i(K, g, tc):
        return (4 * K + g) * 16 + tc + 1

    f_i = cs_i

    FT_NB = {}
    FT_CUM = {}
    tot = 0
    for K in range(8):
        for g in range(4):
            for tc in range(16):
                nb = (tc + 8) // 8
                FT_NB[(K, g, tc)] = nb
                tot += nb
                FT_CUM[(K, g, tc)] = tot

    def prev_tile(K, g, tc):
        tc -= 1
        if tc < 0:
            tc = 15
            g -= 1
            if g < 0:
                g = 3
                K -= 1
        return (K, g, tc)

    def pCap(tc):
        return ap_(pS, (tc % 2) * 512, [[2048, 128], [1, 129]])

    def pcrow_ap(g, off, n):
        return ap_(pcrow[g // 2], (g % 2) * 64 * 2048 + off, [[2048, 1], [1, n]])

    @block.sync
    def _(s):
        for K in range(8):
            for g in range(4):
                hg = 4 * K + g
                if hg >= 2:
                    s.wait_ge(p.sems["n1_csmm"], (hg - 2) * 16 + 16)
                    s.wait_ge(p.sems["n1_rp"], hg - 1)
                p.dma(s, "n1_qld", out=q1[hg % 2][:],
                      in_=ap_(d["q_sp"], hg * 128 * T, [[T, 128], [1, T]]))
            for g in range(4):
                hg = 4 * K + g
                for qr in range(4):
                    qi = hg * 4 + qr
                    s.wait_ge(p.sems["n1_oev"], qi + 1)
                    p.dma(s, "n1_gst",
                          out=ap_(d["gat_sp"], hg * 128 * T + qr * 512,
                                  [[T, 128], [1, 512]]),
                          in_=ogat[qi % 2][:])
        s.wait_ge(p.sems["n1_gst"], 16 * 128)

    @block.tensor
    def _(t):
        for K in range(8):
            def emit_ctr(g):
                hg = 4 * K + g
                t.wait_ge(p.sems["n1_pcb"], hg + 1)
                if K >= 1 and g == 0:
                    t.wait_ge(p.sems["n1_ftv"], FT_CUM[(K - 1, 3, 15)])
                for b in range(4):
                    bi = hg * 4 + b + 1
                    if bi >= 3:
                        t.wait_ge(p.sems["n1_ctv"], bi - 2)
                    ptr = ptr2[bi % 2]
                    for q4 in range(4):
                        tc = b * 4 + q4
                        t.transpose(ap_(ptr, q4 * 128, [[1024, 128], [1, 128]]),
                                    Sc[:, tc * 129 + 1:tc * 129 + 129], p.R["identb"][:])
                        mi = t.transpose(ap_(ptr, 512 + q4 * 128, [[1024, 1], [1, 128]]),
                                         Sc[:, tc * 129:tc * 129 + 1],
                                         p.R["identb"][:])
                    p.inc(mi, "n1_ctr", 1)
            for g in range(4):
                hg = 4 * K + g
                t.wait_ge(p.sems["n1_qld"], 16 * (hg + 1))
                if K >= 1 and g == 0:
                    t.wait_ge(p.sems["n1_fex"], 64 * K)
                for tc in range(NCH):
                    ci = cs_i(K, g, tc)
                    if ci >= 3:
                        t.wait_ge(p.sems["n1_csev"], ci - 2)
                    mi = t.matmul(pCap(tc),
                                  q1[hg % 2][:, tc * 128:(tc + 1) * 128],
                                  ckb[:, K * 129:(K + 1) * 129],
                                  start=True, stop=True)
                    p.inc(mi, "n1_csmm", 1)
                if g >= 1:
                    emit_ctr(g - 1)
            emit_ctr(3)
            for g in range(4):
                hg = 4 * K + g
                def emit_ftr(tc):
                    fi = f_i(K, g, tc)
                    t.wait_ge(p.sems["n1_pv"], fi)
                    t.wait_ge(p.sems["n1_ctv"], 16 * (K + 1))
                    nb = FT_NB[(K, g, tc)]
                    for b in range(nb):
                        bti = FT_CUM[(K, g, tc)] - nb + b + 1
                        if bti >= 3:
                            t.wait_ge(p.sems["n1_ftv"], bti - 2)
                        ptr = ptr2[bti % 2]
                        ns_ = min(8, tc + 1 - b * 8)
                        for s8 in range(ns_):
                            mi = t.transpose(
                                ap_(ptr, s8 * 128, [[1024, 128], [1, 128]]),
                                Pf[tc % 2][:, (b * 8 + s8) * 128:(b * 8 + s8 + 1) * 128],
                                p.R["identb"][:])
                        p.inc(mi, "n1_ftr", 1)
                for qr in range(4):
                    qi = hg * 4 + qr
                    for t4 in range(4):
                        tc = qr * 4 + t4
                        fi = f_i(K, g, tc)
                        if fi >= 2:
                            t.wait_ge(p.sems["n1_fex"], fi - 1)
                        t.wait_ge(p.sems["n1_rp"], hg + 1)
                        lt = rq1[hg % 2][:, tc * 128:(tc + 1) * 128]
                        wtot = (tc + 1) * 128
                        for q_ in range((wtot + 511) // 512):
                            wv = min(512, wtot - q_ * 512)
                            mi = t.matmul(ap_(pS, q_ * 512, [[2048, 128], [1, wv]]),
                                          lt,
                                          rk[:, K * 2048 + q_ * 512:K * 2048 + q_ * 512 + wv],
                                          start=True, stop=True)
                        p.inc(mi, "n1_fmm", 1)
                        if t4 >= 1:
                            emit_ftr(tc - 1)
                    emit_ftr(qr * 4 + 3)
                    t.wait_ge(p.sems["n1_ctv"], 4 * (hg + 1))
                    t.wait_ge(p.sems["n1_ftv"], FT_CUM[(K, g, qr * 4 + 3)])
                    if qi >= 2:
                        t.wait_ge(p.sems["n1_oev"], qi - 1)
                    t.matmul(pO[qi % 2][:],
                             cvT[:, K * 128:(K + 1) * 128],
                             PcT[:, g * 2048 + qr * 512:g * 2048 + (qr + 1) * 512],
                             start=True, stop=False, skip_group_check=True)
                    t.matmul(pO[qi % 2][:],
                             ap_(cvlast, (g % 2) * 64 * 1024 + K * 128,
                                 [[1024, 1], [1, 128]]),
                             pcrow_ap(g, qr * 512, 512),
                             start=False, stop=False, skip_group_check=True)
                    njc = qr * 4 + 4
                    for jc in range(njc):
                        a0 = max((jc - qr * 4) * 128, 0)
                        mi = t.matmul(ap_(pO[qi % 2], a0, [[512, 128], [1, 512 - a0]]),
                                      vT[:, (K * 16 + jc) * 128:(K * 16 + jc + 1) * 128],
                                      PT[:, jc * 512 + a0:(jc + 1) * 512],
                                      start=False, stop=(jc == njc - 1),
                                      skip_group_check=True)
                    p.inc(mi, "n1_omm", 1)

    @block.scalar
    def _(s):
        s = p.wrap(s, "s")
        for K in range(8):
            for g in range(4):
                hg = 4 * K + g
                s.wait_ge(p.sems["n1_csev"], cs_i(K, g, 15))
                if hg >= 1:
                    s.wait_ge(p.sems["n1_ctr"], 4 * hg)
                p.inc(s.activation(Sc[:], cst[:], AF.Exp), "n1_scx", 1)
            for tc in range(NCH):
                si = 16 * K + tc
                s.wait_ge(p.sems["n1_selv"], 3 * si + 1)
                s.activation(e2[:], imp[:, tc * 129 + 1:tc * 129 + 129],
                             AF.Exp, bias=wcol[:, 1:2])
                p.inc(s.activation(wcol[:, 3:4], wcol[:, 2:3], AF.Exp), "n1_sels", 1)
                s.wait_ge(p.sems["n1_selv"], 3 * si + 2)
                p.inc(s.activation(wcol[:, 5:6], wcol[:, 4:5], AF.Ln), "n1_sels", 1)
            for g in range(4):
                for tc in range(NCH):
                    fi = f_i(K, g, tc)
                    s.wait_ge(p.sems["n1_fmm"], fi)
                    if fi >= 3:
                        s.wait_ge(p.sems["n1_pv"], fi - 2)
                    p.inc(s.activation(S_f[tc % 2][:, 0:(tc + 1) * 128],
                                       pS[:, 0:(tc + 1) * 128], AF.Exp),
                          "n1_fex", 1)

    @block.vector
    def _(v):
        v = p.wrap(v, "v")
        for K in range(8):
            def emit_ctv(g):
                hg = 4 * K + g
                for b in range(4):
                    bi = hg * 4 + b + 1
                    v.wait_ge(p.sems["n1_ctr"], bi)
                    if K >= 1:
                        v.wait_ge(p.sems["n1_omm"], (4 * (K - 1) + g) * 4 + 4)
                    ptr = ptr2[bi % 2]
                    v.tensor_copy(PcT[:, g * 2048 + b * 512:g * 2048 + (b + 1) * 512],
                                  ptr[:, 0:512])
                    v.tensor_copy(pcrow_ap(g, b * 512, 512), ptr[0:1, 512:1024])
                    p.inc(v.engine_nop(), "n1_ctv", 1)
            for g in range(4):
                hg = 4 * K + g
                for tc in range(NCH):
                    ci = cs_i(K, g, tc)
                    v.wait_ge(p.sems["n1_csmm"], ci)
                    if tc == 0:
                        v.wait_ge(p.sems["n1_scx"], hg)
                        v.wait_ge(p.sems["n1_imp"], hg)
                    v.tensor_add(cst[:, tc * 129:(tc + 1) * 129],
                                 pCap(tc), cmaskS[:, tc * 129:(tc + 1) * 129])
                    p.inc(v.engine_nop(), "n1_csev", 1)
                if g == 0:
                    v.wait_ge(p.sems["n1_sels"], 32 * K)
                    v.tensor_copy(imp[:], cst[:])
                else:
                    v.tensor_add(imp[:], imp[:], cst[:])
                    if g == 3:
                        v.tensor_scalar_mul(imp[:], imp[:], 0.25)
                p.inc(v.engine_nop(), "n1_imp", 1)
                if g >= 1:
                    emit_ctv(g - 1)
                v.wait_ge(p.sems["n1_scx"], hg + 1)
                for tc in range(NCH):
                    v.reduce_sum(denc[:, tc:tc + 1],
                                 Sc[:, tc * 129:(tc + 1) * 129], axis=AX.X)
                v.reciprocal(rcg[:], denc[:])
                v.tensor_mul(rcg[:], rcg[:],
                             ap_(gtok, 3 * hg, [[2048, 128], [128, 16]]))
                for tc in range(NCH):
                    v.tensor_scalar(Sc[:, tc * 129:(tc + 1) * 129],
                                    Sc[:, tc * 129:(tc + 1) * 129],
                                    rcg[:, tc:tc + 1], None, ALU.mult)
                p.inc(v.engine_nop(), "n1_pcb", 1)
            emit_ctv(3)
            for tc in range(NCH):
                si = 16 * K + tc
                v.wait_ge(p.sems["n1_imp"], 4 * K + 4)
                v.max(v8[:], imp[:, tc * 129 + 1:tc * 129 + 129])
                v.tensor_scalar_max(wcol[:, 0:1], v8[:, 0:1], -1000.0)
                v.tensor_scalar_mul(wcol[:, 1:2], wcol[:, 0:1], -1.0)
                v.tensor_scalar_sub(wcol[:, 2:3], wcol[:, 1:2], 1000.0)
                p.inc(v.engine_nop(), "n1_selv", 1)
                v.wait_ge(p.sems["n1_sels"], 2 * si + 1)
                v.reduce_sum(wcol[:, 4:5], e2[:], axis=AX.X)
                v.tensor_add(wcol[:, 4:5], wcol[:, 4:5], wcol[:, 3:4])
                p.inc(v.engine_nop(), "n1_selv", 1)
                v.wait_ge(p.sems["n1_sels"], 2 * si + 2)
                v.tensor_add(wcol[:, 6:7], wcol[:, 0:1], wcol[:, 5:6])
                v.tensor_scalar_sub(wcol[:, 6:7], wcol[:, 6:7], 23.02585093)
                v.tensor_scalar(m1t[:], imp[:, tc * 129 + 1:tc * 129 + 129],
                                v8[:, 3:4], None, ALU.is_ge)
                v.tensor_scalar(m2t[:], imp[:, tc * 129 + 1:tc * 129 + 129],
                                wcol[:, 6:7], None, ALU.is_gt)
                v.tensor_mul(selm[:, tc * 128:(tc + 1) * 128], m1t[:], m2t[:])
                p.inc(v.engine_nop(), "n1_selv", 1)
            for g in range(4):
                hg = 4 * K + g
                def emit_ptc(qr, tc):
                    qi = (4 * K + g) * 4 + qr
                    nb = FT_NB[(K, g, tc)]
                    for b in range(nb):
                        bti = FT_CUM[(K, g, tc)] - nb + b + 1
                        v.wait_ge(p.sems["n1_ftr"], bti)
                        if qi >= 1:
                            v.wait_ge(p.sems["n1_omm"], qi)
                        ns_ = min(8, tc + 1 - b * 8)
                        v.tensor_copy(
                            ap_(PT, (b * 8) * 512 + (tc % 4) * 128,
                                [[8192, 128], [512, ns_], [1, 128]]),
                            ap_(ptr2[bti % 2], 0, [[1024, 128], [128, ns_], [1, 128]]))
                        p.inc(v.engine_nop(), "n1_ftv", 1)
                for qr in range(4):
                    qi = hg * 4 + qr
                    for t4 in range(4):
                        tc = qr * 4 + t4
                        fi = f_i(K, g, tc)
                        v.wait_ge(p.sems["n1_fex"], fi)
                        v.wait_ge(p.sems["n1_selv"], 3 * (16 * K + tc) + 3)
                        if fi >= 3:
                            v.wait_ge(p.sems["n1_ftr"],
                                      FT_CUM[prev_tile(*prev_tile(K, g, tc))])
                        Pfs = Pf[tc % 2]
                        Sfs = S_f[tc % 2]
                        v.tensor_mul(Pfs[:, 0:(tc + 1) * 128],
                                     Sfs[:, 0:(tc + 1) * 128],
                                     ap_(selm, tc * 128, [[2048, 128], [1, (tc + 1) * 8], [0, 16]]))
                        v.tensor_mul(m2t[:], Sfs[:, tc * 128:(tc + 1) * 128], fownS[:])
                        v.tensor_add(Pfs[:, tc * 128:(tc + 1) * 128],
                                     Pfs[:, tc * 128:(tc + 1) * 128], m2t[:])
                        v.reduce_sum(fcol[:, 0:1], Pfs[:, 0:(tc + 1) * 128], axis=AX.X)
                        v.reciprocal(fcol[:, 1:2], fcol[:, 0:1])
                        v.tensor_mul(fcol[:, 1:2], fcol[:, 1:2],
                                     gtok[:, tc * 128 + 3 * hg + 1:tc * 128 + 3 * hg + 2])
                        v.tensor_scalar(Pfs[:, 0:(tc + 1) * 128],
                                        Pfs[:, 0:(tc + 1) * 128],
                                        fcol[:, 1:2], None, ALU.mult)
                        if tc == 0:
                            v.tensor_mul(ssb[:, 0:128], Sfs[:, 0:128], sbF0[:])
                            r0, r1 = 0, 128
                            pf0 = 0
                        else:
                            v.tensor_mul(ssb[:, 0:256],
                                         Sfs[:, (tc - 1) * 128:(tc + 1) * 128], sbF[:])
                            r0, r1 = 0, 256
                            pf0 = (tc - 1) * 128
                        v.reduce_sum(gcol[:, 0:1], ssb[:, r0:r1], axis=AX.X)
                        v.reciprocal(gcol[:, 1:2], gcol[:, 0:1])
                        v.tensor_mul(gcol[:, 1:2], gcol[:, 1:2],
                                     gtok[:, tc * 128 + 3 * hg + 2:tc * 128 + 3 * hg + 3])
                        v.tensor_scalar(ssb[:, r0:r1], ssb[:, r0:r1],
                                        gcol[:, 1:2], None, ALU.mult)
                        v.tensor_add(Pfs[:, pf0:pf0 + (r1 - r0)],
                                     Pfs[:, pf0:pf0 + (r1 - r0)], ssb[:, r0:r1])
                        p.inc(v.engine_nop(), "n1_pv", 1)
                        if t4 >= 1:
                            emit_ptc(qr, tc - 1)
                    emit_ptc(qr, qr * 4 + 3)
                    v.wait_ge(p.sems["n1_omm"], qi + 1)
                    if qi >= 2:
                        v.wait_ge(p.sems["n1_gst"], 16 * (qi - 1))
                    v.tensor_copy(ogat[qi % 2][:], pO[qi % 2][:])
                    p.inc(v.engine_nop(), "n1_oev", 1)

    @block.gpsimd
    def _(g_):
        g_ = p.wrap(g_, "g")
        for K in range(8):
            for g in range(4):
                hg = 4 * K + g
                g_.wait_ge(p.sems["n1_qld"], 16 * (hg + 1))
                if hg >= 2:
                    g_.wait_ge(p.sems["n1_fmm"], (hg - 1) * 16)
                qhi = ap_(q1[hg % 2], 0, [[T, 64], [1, T]])
                qlo = ap_(q1[hg % 2], 64 * T, [[T, 64], [1, T]])
                rhi = ap_(rq1[hg % 2], 0, [[T, 64], [1, T]])
                rlo = ap_(rq1[hg % 2], 64 * T, [[T, 64], [1, T]])
                cos0 = ap_(rcosf, 0, [[T, 64], [1, T]])
                cos1 = ap_(rcosf, 64 * T, [[T, 64], [1, T]])
                sin0 = ap_(rsinf, 0, [[T, 64], [1, T]])
                sin1 = ap_(rsinf, 64 * T, [[T, 64], [1, T]])
                tmp0 = ap_(rtmp, 0, [[T, 64], [1, T]])
                tmp1 = ap_(rtmp, 64 * T, [[T, 64], [1, T]])
                g_.tensor_mul(rhi, qhi, cos0)
                g_.tensor_mul(tmp0, qlo, sin1)
                g_.tensor_sub(rhi, rhi, tmp0)
                g_.tensor_mul(rlo, qlo, cos1)
                g_.tensor_mul(tmp1, qhi, sin0)
                g_.tensor_add(rlo, rlo, tmp1)
                p.inc(g_.engine_nop(), "n1_rp", 1)
    na.close()



def emit_nsa_out(p, d, res_old_idx, res_new_idx):
    block = p.block
    ms = p.scope()
    gatb = p.sbuf("no_gat", [128, 32 * 1024], BF16, scope=ms)
    wob = [p.sbuf(f"no_w{i}", [128, 32 * 128], BF16, scope=ms) for i in range(2)]
    rold = [p.sbuf(f"no_r{i}", [128, 1024], BF16, scope=ms) for i in range(2)]
    rnew = [p.sbuf(f"no_n{i}", [128, 1024], BF16, scope=ms) for i in range(2)]
    psE = [p.psum(f"no_ps{i}", [128, 1024], scope=ms) for i in range(2)]
    for n in ("no_gld", "no_wld", "no_rold", "no_mm", "no_ev", "no_st"):
        p.sem(n)

    for th in range(2):
        B = dict(p.cnt)

        @block.sync
        def _(s):
            if th == 1:
                s.wait_ge(p.sems["no_mm"], B["no_mm"])
            for h in range(32):
                p.dma(s, "no_gld", out=gatb[:, h * 1024:(h + 1) * 1024],
                      in_=ap_(d["gat_sp"], h * 128 * T + th * 1024, [[T, 128], [1, 1024]]))

        def _rold(g, o):
            if o >= 2:
                g.wait_ge(p.sems["no_ev"], B["no_ev"] + o - 1)
            elif B["no_ev"]:
                g.wait_ge(p.sems["no_ev"], B["no_ev"])
            src_ = ap_(d["res_sp"],
                       res_old_idx * DC * 128 * T + o * 128 * T + th * 1024,
                       [[T, 128], [1, 1024]])
            p.dma(g, "no_rold", out=rold[o % 2][:], in_=src_)

        @block.gpsimd
        def _(g):
            g = p.wrap(g, "g")
            if B["no_mm"]:
                g.wait_ge(p.sems["no_mm"], B["no_mm"])
            for o in range(32):
                if o >= 2:
                    g.wait_ge(p.sems["no_mm"], B["no_mm"] + o - 1)
                p.dma(g, "no_wld",
                      out=ap_(wob[o % 2], 0, [[32 * 128, 128], [128, 32], [1, 128]]),
                      in_=d["w_o"][o].rearrange("a p q -> p a q"))
                if o >= 2:
                    _rold(g, o - 2)
            _rold(g, 30)
            _rold(g, 31)

        @block.tensor
        def _(t):
            t.wait_ge(p.sems["no_gld"], B["no_gld"] + 16 * 32)
            if B["no_ev"]:
                t.wait_ge(p.sems["no_ev"], B["no_ev"])
            for o in range(32):
                t.wait_ge(p.sems["no_wld"], B["no_wld"] + 16 * (o + 1))
                if o >= 2:
                    t.wait_ge(p.sems["no_ev"], B["no_ev"] + o - 1)
                for k in range(32):
                    lt = wob[o % 2][:, k * 128:(k + 1) * 128]
                    for j in range(2):
                        mi = t.matmul(ap_(psE[o % 2], j * 512, [[1024, 128], [1, 512]]),
                                      lt, gatb[:, k * 1024 + j * 512:k * 1024 + (j + 1) * 512],
                                      start=(k == 0), stop=(k == 31))
                p.inc(mi, "no_mm", 1)

        @block.vector
        def _(v):
            v = p.wrap(v, "v")
            if B["no_st"]:
                v.wait_ge(p.sems["no_st"], B["no_st"])
            for o in range(32):
                v.wait_ge(p.sems["no_mm"], B["no_mm"] + o + 1)
                v.wait_ge(p.sems["no_rold"], B["no_rold"] + 16 * (o + 1))
                if o >= 2:
                    v.wait_ge(p.sems["no_st"], B["no_st"] + 16 * (o - 1))
                v.tensor_add(rnew[o % 2][:], psE[o % 2][:], rold[o % 2][:])
                p.inc(v.engine_nop(), "no_ev", 1)

        @block.sync
        def _(s):
            for o in range(32):
                s.wait_ge(p.sems["no_ev"], B["no_ev"] + o + 1)
                p.dma(s, "no_st",
                      out=ap_(d["res_sp"],
                              res_new_idx * DC * 128 * T + o * 128 * T + th * 1024,
                              [[T, 128], [1, 1024]]),
                      in_=rnew[o % 2][:])
            s.wait_ge(p.sems["no_st"], B["no_st"] + 16 * 32)
    ms.close()
